# version 07
# baseline (speedup 1.0000x reference)
import math
import os
from contextlib import ExitStack

import numpy as np
import concourse.bass as bass
import concourse.mybir as mybir
from concourse.bass_utils import run_bass_kernel_spmd

F32 = mybir.dt.float32
F32R = mybir.dt.float32r
AF = mybir.ActivationFunctionType
ALU = mybir.AluOpType
AXX = mybir.AxisListType.X

B, L, D = 16, 2048, 1024
INNER = 512
HDIM = 1408
NEXP = 8
NCORES = 8
BPC = B // NCORES
TPC = BPC * L
CA = 256
NCHS = L // CA
TB = 512
NTB = TPC // TB
EPS = 1e-5
NIT_R = 36
NIT_F = 4
PA_LIMIT = 10**9
WSCR_PARTS = 3
HT = HDIM // 128

LN_INV102 = float(-math.log(1.02))

CONST_VALS = [1.0, 2.0, -1.0, 0.5, -0.5, 2.0 * math.pi, math.pi,
              -2.0 * math.pi, EPS, 1e-30, 1.0 / D, -1e30, LN_INV102, 0.0]
CIDX = {v: i for i, v in enumerate(CONST_VALS)}


class KB:

    def __init__(self, nc, es):
        self.nc = nc
        self.plan = []
        self.sem = {}
        for nm, step in [("t", 1), ("v", 1), ("s", 1), ("g", 1), ("y", 1),
                         ("sy", 1)]:
            h = es.enter_context(nc.semaphore("sem_" + nm))
            self.sem[nm] = [h, step, 0]

    def emit(self, eng, fn):
        self.plan.append((eng, fn))

    def cnt(self, nm):
        return self.sem[nm][2]

    def wait(self, eng, nm, val=None):
        s = self.sem[nm]
        v = s[2] if val is None else val
        if v > 0:
            self.emit(eng, lambda e, h=s[0], v=v: e.wait_ge(h, v))

    def run(self, eng, nm, fn):
        s = self.sem[nm]
        s[2] += s[1]
        def wrapped(e, fn=fn, h=s[0], step=s[1]):
            ins = fn(e)
            assert ins is not None, "run() closure must return an instruction"
            ins.then_inc(h, step)
        self.emit(eng, wrapped)


class DQ:

    NSEM = 4

    def __init__(self, kb, es, name, eng):
        self.kb = kb
        self.eng = eng
        self.name = name
        self.sems = [es.enter_context(kb.nc.semaphore(f"dq_{name}{i}"))
                     for i in range(self.NSEM)]
        self.count = [0] * self.NSEM
        self.pend = [None] * self.NSEM
        self.cur = 0
        self.dirty = False

    def dma(self, out_ap, in_ap):
        i = self.cur
        if not self.dirty and self.pend[i] is not None:
            tgt = self.pend[i]
            self.kb.emit(self.eng,
                         lambda e, h=self.sems[i], v=tgt: e.wait_ge(h, v))
            self.pend[i] = None
        self.dirty = True
        self.count[i] += 16
        h = self.sems[i]
        self.kb.emit(self.eng, lambda e, o=out_ap, a=in_ap, h=h:
                     e.dma_start(out=o, in_=a).then_inc(h, 16))

    def fence(self):
        assert self.dirty, "fence with no dmas"
        i = self.cur
        tok = (self.sems[i], self.count[i])
        self.pend[i] = self.count[i]
        self.cur = (i + 1) % self.NSEM
        self.dirty = False
        return tok

    def wait(self, eng, tok):
        h, v = tok
        self.kb.emit(eng, lambda e, h=h, v=v: e.wait_ge(h, v))

    def final_tokens(self):
        out = []
        for i in range(self.NSEM):
            if self.count[i]:
                out.append((self.sems[i], self.count[i]))
        return out


class Bank:

    def __init__(self, kb, ap, idx):
        self.kb = kb
        self.ap = ap
        self.idx = idx
        self.prod = 0
        self.cons = ("v", 0)

    def produce(self, fn, waitv=None):
        kb = self.kb
        nm, c = self.cons
        kb.wait("tensor", nm, c)
        if waitv is not None:
            kb.wait("tensor", "v", waitv)
        kb.run("tensor", "t", fn)
        self.prod = kb.cnt("t")

    def consume(self, eng, fn, guard=None):
        kb = self.kb
        kb.wait(eng, "t", self.prod)
        nm = {"vector": "v", "scalar": "s", "gpsimd": "g"}[eng]
        if guard is not None:
            ap = self.ap
            if eng == "vector":
                kb.run(eng, nm,
                       lambda ve, ap=ap, g=guard: ve.tensor_copy(g, ap[:, 0:1]))
            else:
                kb.run(eng, nm,
                       lambda se, ap=ap, g=guard: se.square(g, ap[:, 0:1]))
            kb.wait(eng, nm, kb.cnt(nm))
        kb.run(eng, nm, fn)
        self.cons = (nm, kb.cnt(nm))


BARRIER_ENGS = ("tensor", "vector", "scalar", "gpsimd", "sync")


def barrier(kb):
    targets = {}
    for eng, nm in (("vector", "v"), ("scalar", "s"), ("gpsimd", "g"),
                    ("sync", "sy")):
        def bump(e, eng=eng):
            if eng in ("vector", "gpsimd"):
                return e.engine_nop()
            return e.nop()
        kb.run(eng, nm, bump)
        targets[eng] = (nm, kb.cnt(nm))
    targets["tensor"] = ("t", kb.cnt("t"))
    for eng in ("tensor", "vector", "scalar", "gpsimd", "sync"):
        for other, (nm, v) in targets.items():
            if other != eng:
                kb.wait(eng, nm, v)


def build_program(stage="full", debug=False):
    nc = bass.Bass("TRN2", target_bir_lowering=False,
                   detect_race_conditions=bool(
                       os.environ.get("RACEDET")))
    es = ExitStack()
    kb = KB(nc, es)

    def din(name, shape):
        return nc.dram_tensor(name, shape, F32, kind="ExternalInput").ap()

    x_in = din("x", [BPC, L, D])
    w_a_h = din("w_a", [2 * INNER, D])
    w_x_h = din("w_x", [2 * INNER, D])
    w_out_h = din("w_out", [D, INNER])
    v_re_h = din("v_re", [INNER, INNER])
    v_im_h = din("v_im", [INNER, INNER])
    h0_re_h = din("h0_re", [INNER])
    h0_im_h = din("h0_im", [INNER])
    gate_w_h = din("gate_w", [NEXP, D])
    gate_b_h = din("gate_b", [NEXP])
    w1_h = din("w1", [NEXP, D, HDIM])
    b1_h = din("b1", [NEXP, HDIM])
    w3_h = din("w3", [NEXP, D, HDIM])
    b3_h = din("b3", [NEXP, HDIM])
    w2_h = din("w2", [NEXP, HDIM, D])
    b2_h = din("b2", [NEXP, D])
    sw1_h = din("sw1", [D, HDIM])
    sb1_h = din("sb1", [HDIM])
    sw3_h = din("sw3", [D, HDIM])
    sb3_h = din("sb3", [HDIM])
    sw2_h = din("sw2", [HDIM, D])
    sb2_h = din("sb2", [D])

    out_h = nc.dram_tensor("out", [BPC, L, D], F32, kind="ExternalOutput").ap()
    out_flat = out_h.rearrange("b l d -> (b l) d")
    x_flat = x_in.rearrange("b l d -> (b l) d")

    if os.environ.get("XPOUT"):
        xp_scr = nc.dram_tensor("xp_scr", [8, 128, TPC], F32,
                                kind="ExternalOutput").ap()
    else:
        xp_scr = nc.dram_tensor("xp_scr", [8, 128, TPC], F32).ap()
    waT_scr = nc.dram_tensor("waT_scr", [8, 128, 2 * INNER], F32).ap()
    wxT_scr = nc.dram_tensor("wxT_scr", [8, 128, 2 * INNER], F32).ap()
    woT_scr = nc.dram_tensor("woT_scr", [4, 128, D], F32).ap()
    sac_scr = nc.dram_tensor("sac_scr", [128, 256], F32).ap()

    dbg = {}
    if debug:
        for nm_, shp in [("dbg_xinv_re", [4, 128, INNER]),
                         ("dbg_xinv_im", [4, 128, INNER]),
                         ("dbg_waT", [9, 128, 2 * INNER]),
                         ("dbg_pl", [20, 128, 2048]),
                         ("dbg_xp", [8, 128, TPC]),
                         ("dbg_it", [NIT_R + NIT_F + 6, 128, 2048])]:
            dbg[nm_] = nc.dram_tensor(nm_, shp, F32, kind="ExternalOutput").ap()

    def sb(name, p, f, dtype=F32):
        return es.enter_context(nc.sbuf_tensor(name, [p, f], dtype))

    ident = sb("ident", 128, 128)
    gsc = sb("gsc", 128, 2)
    xp_keep = sb("xp_keep", 128, 8 * CA)
    ones_row = sb("ones_row", 1, 1024)
    consts = sb("consts", 128, 16)
    crow = sb("crow", 1, 16)
    h0_sb = sb("h0_sb", 128, 8)
    gwT = sb("gwT", 128, 8 * NEXP)
    gbb = sb("gbb", 128, NEXP)
    gb8 = sb("gb8", 1, 8)

    g_stg = sb("g_stg", NEXP, D)
    es_vx = ExitStack()
    def sbx(name, p, f):
        return es_vx.enter_context(nc.sbuf_tensor(name, [p, f], F32))
    x_re = sbx("x_re", 128, 4 * INNER)
    x_im = sbx("x_im", 128, 4 * INNER)
    x_imn = sbx("x_imn", 128, 4 * INNER)
    vT_re = sbx("vT_re", 128, 4 * INNER)
    vT_imn = sbx("vT_imn", 128, 4 * INNER)
    es_va = ExitStack()
    ii_a = es_va.enter_context(nc.sbuf_tensor("ii_a", [128, 128], F32))
    ii_b = es_va.enter_context(nc.sbuf_tensor("ii_b", [128, 128], F32))
    vA_re = es_va.enter_context(nc.sbuf_tensor("vA_re", [128, 4 * INNER], F32))
    vA_im = es_va.enter_context(nc.sbuf_tensor("vA_im", [128, 4 * INNER], F32))
    vA_imn = es_va.enter_context(nc.sbuf_tensor("vA_imn", [128, 4 * INNER], F32))

    psb = [es.enter_context(nc.psum_tensor(f"ps{i}", [128, 512], F32))
           for i in range(8)]
    banks = [Bank(kb, psb[i], i) for i in range(8)]
    dq_ld = DQ(kb, es, "ld", "sync")
    dq_gl = DQ(kb, es, "gl", "gpsimd")
    _steng = "sync" if os.environ.get("STSYNC") else "gpsimd"
    dq_st = DQ(kb, es, "st", _steng)

    def ccol(v):
        return consts[:, CIDX[v]:CIDX[v] + 1]

    def crow_ap(v):
        return crow[0:1, CIDX[v]:CIDX[v] + 1]

    def tile(plane, W, r, c0=None, c1=None):
        if c0 is None:
            return plane[:, r * W:(r + 1) * W]
        return plane[:, r * W + c0: r * W + c1]

    def ms(ap, val):
        kb.run("vector", "v", lambda ve, a=ap, x=float(val): ve.memset(a, x))

    ms(ones_row[:], 1.0)
    for j, val in enumerate(CONST_VALS):
        ms(consts[:, j:j + 1], val)
        ms(crow[0:1, j:j + 1], val)
    kb.run("gpsimd", "g", lambda e: e.iota(ii_a[:], [[1, 128]], base=0,
                                           channel_multiplier=0,
                                           allow_small_or_imprecise_dtypes=True))
    kb.run("gpsimd", "g", lambda e: e.iota(ii_b[:], [[0, 128]], base=0,
                                           channel_multiplier=1,
                                           allow_small_or_imprecise_dtypes=True))
    kb.wait("vector", "g")
    kb.run("vector", "v",
           lambda ve: ve.tensor_tensor(ident[:], ii_a[:], ii_b[:], ALU.is_equal))
    V_CONSTS = kb.cnt("v")
    for eng in ("tensor", "scalar", "gpsimd", "sync"):
        kb.wait(eng, "v", V_CONSTS)
    kb.wait("vector", "v", V_CONSTS)

    dq_gl.dma(gb8[0:1, 0:NEXP], gate_b_h.unsqueeze(0))
    for g in range(4):
        dq_gl.dma(h0_sb[:, g:g + 1], h0_re_h[g * 128:(g + 1) * 128].unsqueeze(1))
        dq_gl.dma(h0_sb[:, 4 + g:5 + g],
                  h0_im_h[g * 128:(g + 1) * 128].unsqueeze(1))
    tok_small = dq_gl.fence()

    for g in range(4):
        dq_ld.dma(tile(vA_re, INNER, g), v_re_h[g * 128:(g + 1) * 128, :])
        dq_ld.dma(tile(vA_im, INNER, g), v_im_h[g * 128:(g + 1) * 128, :])
    tok_v = dq_ld.fence()
    dq_ld.wait("vector", tok_v)
    kb.run("vector", "v", lambda ve: ve.tensor_scalar_mul(vA_imn[:], vA_im[:], -1.0))
    V_VA = kb.cnt("v")

    dq_ld.wait("tensor", tok_v)

    def emit_transpose(src_fn, dst_fn, nrt, nct, negate=False, waitv=None):
        for c in range(nct):
            for r0 in range(0, nrt, 4):
                grp = list(range(r0, min(r0 + 4, nrt)))
                bk = banks[6 + ((c * nrt + r0) // 4) % 2]
                def tf(te, c=c, grp=grp, bk=bk):
                    last = None
                    for k, r in enumerate(grp):
                        last = te.transpose(bk.ap[:, k * 128:(k + 1) * 128],
                                            src_fn(r, c), ident[:])
                    return last
                bk.produce(tf, waitv=waitv)
                if negate:
                    def ev(ve, c=c, grp=grp, bk=bk):
                        return ve.tensor_scalar_mul(
                            dst_fn(c, grp[0], len(grp)),
                            bk.ap[:, 0:len(grp) * 128], -1.0)
                else:
                    def ev(ve, c=c, grp=grp, bk=bk):
                        return ve.tensor_copy(dst_fn(c, grp[0], len(grp)),
                                              bk.ap[:, 0:len(grp) * 128])
                bk.consume("vector", ev, guard=gsc[:, 0:1])

    def src_plane(plane, W):
        return lambda r, c: plane[:, r * W + c * 128: r * W + c * 128 + 128]

    def dst_plane(plane, W):
        return lambda c, r0, n: plane[:, c * W + r0 * 128: c * W + (r0 + n) * 128]

    emit_transpose(src_plane(vA_re, INNER), dst_plane(vT_re, INNER), 4, 4,
                   waitv=V_CONSTS)
    emit_transpose(src_plane(vA_im, INNER), dst_plane(vT_imn, INNER), 4, 4,
                   negate=True)

    dq_ld.dma(g_stg[:, :], gate_w_h[:, :])
    tok_gw = dq_ld.fence()
    dq_ld.wait("tensor", tok_gw)
    for c in range(8):
        bk = banks[6]
        def tg(te, c=c, bk=bk):
            return te.transpose(bk.ap[:, 0:NEXP],
                                g_stg[:, c * 128:(c + 1) * 128],
                                ident[0:NEXP, 0:NEXP])
        bk.produce(tg)
        def eg(ve, c=c, bk=bk):
            return ve.tensor_copy(gwT[:, c * NEXP:(c + 1) * NEXP],
                                  bk.ap[:, 0:NEXP])
        bk.consume("vector", eg)

    dq_gl.wait("tensor", tok_small)
    bk = banks[7]
    def bgb(te, bk=bk):
        return te.matmul(bk.ap[:, 0:NEXP], ones_row[0:1, 0:128],
                         gb8[0:1, 0:NEXP], start=True, stop=True)
    bk.produce(bgb)
    bk.consume("vector",
               lambda ve, bk=bk: ve.tensor_copy(gbb[:], bk.ap[:, 0:NEXP]))

    es_n = ExitStack()
    def sbn(name, p, f, dtype=F32):
        return es_n.enter_context(nc.sbuf_tensor(name, [p, f], dtype))
    vr_re = sbn("vr_re", 128, 4 * INNER, F32R)
    vr_im = sbn("vr_im", 128, 4 * INNER, F32R)
    vr_imn = sbn("vr_imn", 128, 4 * INNER, F32R)
    xr_re = sbn("xr_re", 128, 4 * INNER, F32R)
    xr_im = sbn("xr_im", 128, 4 * INNER, F32R)
    xtr_re = sbn("xtr_re", 128, 4 * INNER, F32R)
    xtr_im = sbn("xtr_im", 128, 4 * INNER, F32R)
    xtr_imn = sbn("xtr_imn", 128, 4 * INNER, F32R)
    pr_re = sbn("pr_re", 128, 4 * INNER, F32R)
    pr_im = sbn("pr_im", 128, 4 * INNER, F32R)
    xt32_re = sbn("xt32_re", 128, 4 * INNER)
    xt32_im = sbn("xt32_im", 128, 4 * INNER)
    xt32_imn = sbn("xt32_imn", 128, 4 * INNER)
    p32_re = sbn("p32_re", 128, 4 * INNER)
    p32_im = sbn("p32_im", 128, 4 * INNER)
    nsc = sbn("nsc", 128, 4)
    ones128 = sbn("ones128", 128, 128)

    kb.wait("vector", "v", V_VA)
    def cast_vr(ve):
        ve.tensor_copy(vr_re[:], vA_re[:])
        ve.tensor_copy(vr_im[:], vA_im[:])
        return ve.tensor_copy(vr_imn[:], vA_imn[:])
    kb.run("vector", "v", cast_vr)
    V_VR = kb.cnt("v")

    kb.run("vector", "v", lambda ve: ve.memset(nsc[:, 0:3], 0.0))
    V_NSC0 = kb.cnt("v")
    kb.wait("vector", "v", V_NSC0)
    V_red = None
    for mi in range(4):
        for comp in (0, 1):
            bk = banks[(2 * mi + comp) % 4]
            if comp == 0:
                pairs = [(vr_re, vr_re), (vr_im, vr_im)]
            else:
                pairs = [(vr_im, vr_re), (vr_re, vr_imn)]
            def fg(te, mi=mi, pairs=pairs, bk=bk):
                last = None
                first = True
                for lhs, rhs in pairs:
                    for k in range(4):
                        last = te.matmul(
                            bk.ap[:, 0:INNER],
                            lhs[:, k * INNER + mi * 128: k * INNER + mi * 128 + 128],
                            rhs[:, k * INNER:(k + 1) * INNER],
                            start=first, stop=(lhs is pairs[1][0] and k == 3))
                        first = False
                return last
            bk.produce(fg, waitv=V_VR if mi == 0 and comp == 0 else None)
            sl = ((2 * mi + comp) % 4) * INNER
            if V_red is not None:
                kb.wait("scalar", "v", V_red)
            def evsq(se, bk=bk, sl=sl):
                return se.square(p32_re[:, sl:sl + INNER], bk.ap[:, 0:INNER])
            bk.consume("scalar", evsq, guard=gsc[:, 1:2])
            S_GSQ = kb.cnt("s")
            kb.wait("vector", "s", S_GSQ)
            kb.run("vector", "v",
                   lambda ve, sl=sl: ve.reduce_sum(nsc[:, 2:3],
                                                   p32_re[:, sl:sl + INNER],
                                                   axis=AXX))
            kb.wait("vector", "v")
            kb.run("vector", "v",
                   lambda ve: ve.tensor_add(nsc[:, 0:1], nsc[:, 0:1],
                                            nsc[:, 2:3]))
            V_red = kb.cnt("v")
    V_GRAM = kb.cnt("v")

    kb.run("vector", "v", lambda ve: ve.memset(ones128[:], 1.0))
    V_ONES128 = kb.cnt("v")
    bk = banks[4]
    bk.produce(lambda te, bk=bk: te.matmul(bk.ap[:, 0:1], ones128[:],
                                           nsc[:, 0:1], start=True, stop=True),
               waitv=max(V_GRAM, V_ONES128))
    def alpha_fn(se, bk=bk):
        return se.activation(nsc[:, 2:3], bk.ap[:, 0:1], AF.Ln)
    bk.consume("scalar", alpha_fn, guard=gsc[:, 1:2])
    kb.wait("scalar", "s")
    kb.run("scalar", "s",
           lambda se: se.activation(nsc[:, 3:4], nsc[:, 2:3], AF.Exp,
                                    scale=ccol(-0.5),
                                    bias=ccol(LN_INV102)))
    kb.wait("scalar", "s", kb.cnt("s"))
    kb.run("scalar", "s", lambda se: se.square(gsc[:, 1:2], nsc[:, 3:4]))
    S_ALPHA = kb.cnt("s")
    kb.wait("scalar", "s", S_ALPHA)
    kb.run("scalar", "s", lambda se: se.square(gsc[:, 1:2], nsc[:, 3:4]))

    import os as _os
    if debug and not _os.environ.get("NODUMP"):
        kb.wait(_steng, "s", S_ALPHA)
        dq_st.dma(dbg["dbg_waT"][8, :, 0:4], nsc[:, 0:4])
        dq_st.fence()

    kb.wait("vector", "s", S_ALPHA)
    def x0(ve):
        al = bass.AP(nsc, 3, [[4, 128], [0, 4 * INNER]])
        ve.tensor_tensor(x_re[:], vA_re[:], al, ALU.mult)
        return ve.tensor_tensor(x_im[:], vA_imn[:], al, ALU.mult)
    kb.run("vector", "v", x0)
    kb.wait("vector", "v", kb.cnt("v"))

    def ndump(idx, plane):
        import os
        if os.environ.get("NODUMP"):
            return
        if not (debug and stage == "newton"):
            return
        kb.wait(_steng, "v")
        kb.wait(_steng, "s")
        w = plane.shape[-1]
        if w > 1024:
            dq_st.dma(dbg["dbg_waT"][idx, :, :], plane[:, 0:1024])
            dq_st.dma(dbg["dbg_waT"][idx + 1, :, :], plane[:, 1024:2048])
        else:
            dq_st.dma(dbg["dbg_waT"][idx, :, 0:w], plane)
        tok = dq_st.fence()
        dq_st.wait("vector", tok)
        dq_st.wait("scalar", tok)

    ndump(0, x_re[:])
    ndump(2, nsc[:, 0:4])

    it_counter = [0]

    def newton_iter(f32r_phase):
        if f32r_phase:
            a_re, a_im, a_imn = vr_re, vr_im, vr_imn
            tt_re, tt_im, tt_imn = xtr_re, xtr_im, xtr_imn
            pp_re, pp_im = pr_re, pr_im
        else:
            a_re, a_im, a_imn = vA_re, vA_im, vA_imn
            tt_re, tt_im, tt_imn = xt32_re, xt32_im, xt32_imn
            pp_re, pp_im = p32_re, p32_im

        if f32r_phase:
            kb.wait("vector", "v", kb.cnt("v"))
            def cx(ve):
                ve.tensor_copy(xr_re[:], x_re[:])
                return ve.tensor_copy(xr_im[:], x_im[:])
            kb.run("vector", "v", cx)
            xx_re, xx_im = xr_re, xr_im
        else:
            xx_re, xx_im = x_re, x_im
        V_X = kb.cnt("v")
        import os as _os2
        first_dbg = (debug and it_counter[0] == 0 and f32r_phase
                     and not _os2.environ.get("NODUMP"))
        if first_dbg:
            kb.wait(_steng, "v", V_X)
            dq_st.dma(dbg["dbg_it"][40, :, :], x_re[:])
            dq_st.dma(dbg["dbg_it"][44, :, :], xr_re[:])
            dq_st.fence()

        for g in range(4):
            bk = banks[6 + g % 2]
            def tf(te, g=g, bk=bk):
                last = None
                for r in range(4):
                    last = te.transpose(
                        bk.ap[:, r * 128:(r + 1) * 128],
                        x_re[:, r * INNER + g * 128: r * INNER + g * 128 + 128],
                        ident[:])
                return last
            bk.produce(tf, waitv=V_X)
            def ev(ve, g=g, bk=bk, tt_re=tt_re):
                return ve.tensor_copy(tile(tt_re, INNER, g), bk.ap[:, 0:INNER])
            bk.consume("vector", ev, guard=gsc[:, 0:1])
        for g in range(4):
            bk = banks[6 + g % 2]
            def tf(te, g=g, bk=bk):
                last = None
                for r in range(4):
                    last = te.transpose(
                        bk.ap[:, r * 128:(r + 1) * 128],
                        x_im[:, r * INNER + g * 128: r * INNER + g * 128 + 128],
                        ident[:])
                return last
            bk.produce(tf)
            def ev(ve, g=g, bk=bk, tt_im=tt_im, tt_imn=tt_imn):
                ve.tensor_copy(tile(tt_im, INNER, g), bk.ap[:, 0:INNER])
                return ve.tensor_scalar_mul(tile(tt_imn, INNER, g),
                                            bk.ap[:, 0:INNER], -1.0)
            bk.consume("vector", ev, guard=gsc[:, 0:1])
        V_TT = kb.cnt("v")
        if first_dbg:
            kb.wait(_steng, "v", V_TT)
            dq_st.dma(dbg["dbg_it"][41, :, :], x_re[:])
            dq_st.dma(dbg["dbg_it"][43, :, :], tt_re[:])
            dq_st.fence()

        for mi in range(4):
            for comp in (0, 1):
                bk = banks[(2 * mi + comp) % 4]
                if comp == 0:
                    pairs = [(a_re, xx_re), (a_imn, xx_im)]
                    dst = pp_re
                else:
                    pairs = [(a_im, xx_re), (a_re, xx_im)]
                    dst = pp_im
                def fp(te, mi=mi, pairs=pairs, bk=bk):
                    last = None
                    first = True
                    for pi, (lhs, rhs) in enumerate(pairs):
                        for k in range(4):
                            last = te.matmul(
                                bk.ap[:, 0:INNER],
                                lhs[:, k * INNER + mi * 128:
                                    k * INNER + mi * 128 + 128],
                                rhs[:, k * INNER:(k + 1) * INNER],
                                start=first, stop=(pi == 1 and k == 3))
                            first = False
                    return last
                bk.produce(fp, waitv=V_X)
                def ev(ve, mi=mi, bk=bk, dst=dst):
                    return ve.tensor_copy(tile(dst, INNER, mi),
                                          bk.ap[:, 0:INNER])
                bk.consume("vector", ev, guard=gsc[:, 0:1])
        V_P = kb.cnt("v")
        if first_dbg:
            kb.wait(_steng, "v", V_P)
            dq_st.dma(dbg["dbg_it"][42, :, :], pp_re[:])
            dq_st.dma(dbg["dbg_it"][45, :, :], x_re[:])
            dq_st.fence()
        it_counter[0] += 1

        for mi in range(4):
            for comp in (0, 1):
                bk = banks[(2 * mi + comp) % 4]
                if comp == 0:
                    pairs = [(tt_re, pp_re), (tt_imn, pp_im)]
                    dst = x_re
                else:
                    pairs = [(tt_im, pp_re), (tt_re, pp_im)]
                    dst = x_im
                def fq(te, mi=mi, pairs=pairs, bk=bk):
                    last = None
                    first = True
                    for pi, (lhs, rhs) in enumerate(pairs):
                        for k in range(4):
                            last = te.matmul(
                                bk.ap[:, 0:INNER],
                                lhs[:, k * INNER + mi * 128:
                                    k * INNER + mi * 128 + 128],
                                rhs[:, k * INNER:(k + 1) * INNER],
                                start=first, stop=(pi == 1 and k == 3))
                            first = False
                    return last
                bk.produce(fq, waitv=max(V_P, V_TT))
                def ev(ve, mi=mi, bk=bk, dst=dst):
                    return ve.scalar_tensor_tensor(
                        tile(dst, INNER, mi), tile(dst, INNER, mi), 2.0,
                        bk.ap[:, 0:INNER], ALU.mult, ALU.subtract)
                bk.consume("vector", ev, guard=gsc[:, 0:1])

    def itdump(i):
        import os
        if (not debug) or os.environ.get("NODUMP"):
            return
        dq_st.dma(dbg["dbg_it"][i, :, :], x_re[:])
        dq_st.fence()

    for _it in range(NIT_R):
        newton_iter(True)
        itdump(_it)
        if _it == 0:
            ndump(4, x_re[:])
            ndump(6, p32_re[:])
    for _it in range(NIT_F):
        newton_iter(False)
        itdump(NIT_R + _it)

    kb.wait("vector", "v", kb.cnt("v"))
    kb.run("vector", "v",
           lambda ve: ve.tensor_scalar_mul(x_imn[:], x_im[:], -1.0))
    V_NEWTON = kb.cnt("v")

    if debug:
        kb.wait(_steng, "v", V_NEWTON)
        kb.wait(_steng, "s", S_ALPHA)
        for g in range(4):
            dq_st.dma(dbg["dbg_xinv_re"][g, :, :], tile(x_re, INNER, g))
            dq_st.dma(dbg["dbg_xinv_im"][g, :, :], tile(x_im, INNER, g))
        dq_st.fence()

    if stage == "newton":
        for tok in dq_st.final_tokens():
            dq_st.wait("gpsimd", tok)
        _finish(nc, kb)
        es_n.close()
        es_va.close()
        es_vx.close()
        es.close()
        return nc, dbg

    if WSCR_PARTS >= -1:
        barrier(kb)
    es_n.close()
    es_va.close()

    es_w = ExitStack()
    stg = es_w.enter_context(nc.sbuf_tensor("stg", [128, 8 * 1024], F32))
    wstg = es_w.enter_context(nc.sbuf_tensor("wstg", [128, 2 * 1024], F32))

    wstg_toks = {}

    def transpose_to_scratch(load_fn, n_src_tiles, n_dst_tiles, scr, dstw):
        if WSCR_PARTS < 0:
            return {}
        kb.wait("sync", "t")
        for r in range(n_src_tiles):
            dq_ld.dma(stg[:, r * 1024: r * 1024 + n_dst_tiles * 128], load_fn(r))
        tok = dq_ld.fence()
        if WSCR_PARTS < 1:
            return {}
        dq_ld.wait("tensor", tok)
        store_toks = wstg_toks
        for c in range(n_dst_tiles):
            par = c % 2
            if par in store_toks:
                dq_st.wait("vector", store_toks[par])
            for r0 in range(0, n_src_tiles, 4):
                grp = list(range(r0, min(r0 + 4, n_src_tiles)))
                bk = banks[6 + (r0 // 4) % 2]
                def tf(te, c=c, grp=grp, bk=bk):
                    last = None
                    for k, r in enumerate(grp):
                        last = te.transpose(
                            bk.ap[:, k * 128:(k + 1) * 128],
                            stg[:, r * 1024 + c * 128: r * 1024 + c * 128 + 128],
                            ident[:])
                    return last
                bk.produce(tf)
                def ev(ve, grp=grp, bk=bk, par=par):
                    return ve.tensor_copy(
                        wstg[:, par * 1024 + grp[0] * 128:
                             par * 1024 + (grp[0] + len(grp)) * 128],
                        bk.ap[:, 0:len(grp) * 128])
                bk.consume("vector", ev, guard=gsc[:, 0:1])
            if WSCR_PARTS < 2:
                continue
            kb.wait(_steng, "v")
            dq_st.dma(scr[c, :, 0:n_src_tiles * 128],
                      wstg[:, par * 1024: par * 1024 + n_src_tiles * 128])
            store_toks[par] = dq_st.fence()
        return store_toks


    wa2 = w_a_h.rearrange("(c two) d -> c two d", two=2)
    wx2 = w_x_h.rearrange("(c two) d -> c two d", two=2)

    def wa_src(r):
        half, i = divmod(r, 4)
        return wa2[i * 128:(i + 1) * 128, half, :]

    def wa_src_fixed(r):
        half = r // 4
        i = r % 4
        return wa2[i * 128:(i + 1) * 128, half, :]

    def wx_src(r):
        half = r // 4
        i = r % 4
        return wx2[i * 128:(i + 1) * 128, half, :]

    toks_wa = transpose_to_scratch(wa_src_fixed, 8, 8, waT_scr, None)
    if WSCR_PARTS >= 3:
        toks_wx = transpose_to_scratch(wx_src, 8, 8, wxT_scr, None)
        toks_wo = transpose_to_scratch(
            lambda r: w_out_h[r * 128:(r + 1) * 128, :], 8, 4, woT_scr, None)
    else:
        toks_wx, toks_wo = {}, {}
    for tk in list(toks_wa.values()) + list(toks_wx.values()) + list(toks_wo.values()):
        dq_st.wait("sync", tk)
    if debug:
        for c in range(8):
            dq_gl.dma(dbg["dbg_waT"][c, :, :], waT_scr[c, :, :])
        dq_gl.fence()

    if debug and stage == "wscr" and WSCR_PARTS >= 2:
        for c in range(8):
            dq_ld.dma(stg[:, 0:1024], waT_scr[c, :, :])
            tk1 = dq_ld.fence()
            dq_ld.wait("gpsimd", tk1)
            dq_gl.dma(dbg["dbg_waT"][c, :, :], stg[:, 0:1024])
            tk2 = dq_gl.fence()
            dq_gl.wait("sync", tk2)
    if stage == "wscr":
        for dq in (dq_st, dq_ld, dq_gl):
            for tok in dq.final_tokens():
                dq.wait("gpsimd", tok)
        _finish(nc, kb)
        es_w.close()
        es_vx.close()
        es.close()
        return nc, dbg

    barrier(kb)
    es_w.close()

    es_pa = ExitStack()
    def sba(name, p, f):
        return es_pa.enter_context(nc.sbuf_tensor(name, [p, f], F32))
    x_tm = sba("x_tm", 128, 2 * 1024)
    x_fm = sba("x_fm", 128, 8 * CA)
    xn = sba("xn", 128, 8 * CA)
    tmp8 = sba("tmp8", 128, 8 * CA)
    srow = sba("srow", 1, 1024)
    rstd_b = sba("rstd_b", 128, CA)
    mrs_b = sba("mrs_b", 128, CA)
    ar_p = sba("ar_p", 128, 4 * CA)
    ai_p = sba("ai_p", 128, 4 * CA)
    m_a = sba("m_a", 128, 4 * CA)
    rho = sba("rho", 128, 4 * CA)
    t4a = sba("t4a", 128, 4 * CA)
    t4b = sba("t4b", 128, 4 * CA)
    X_s = sba("X_s", 128, 4 * (CA + 1))
    kint = es_pa.enter_context(
        nc.sbuf_tensor("kint", [128, 4 * (CA + 1)], mybir.dt.int32))
    psi = sba("psi", 128, 4 * (CA + 1))
    schalf = sba("schalf", 128, 4 * (CA + 1))
    cosS = sba("cosS", 128, 4 * (CA + 1))
    sinS = sba("sinS", 128, 4 * (CA + 1))
    xcr_s = sba("xcr_s", 128, 4 * (CA + 1))
    xci_s = sba("xci_s", 128, 4 * (CA + 1))
    d_re = sba("d_re", 128, 4 * CA)
    d_im = sba("d_im", 128, 4 * CA)
    w_re = sba("w_re", 128, 4 * (CA + 1))
    w_im = sba("w_im", 128, 4 * (CA + 1))
    hre = sba("hre", 128, 4 * CA)
    onesp = sba("onesp", 128, CA)
    wa_buf = sba("wa_buf", 128, 2 * 1024)
    wx_buf = sba("wx_buf", 128, 2 * 1024)
    _wofull = os.environ.get("WOFULL", "1") != "0"
    wo_buf = sba("wo_buf", 128, 4 * 1024 if _wofull else 2 * 1024)
    if _wofull:
        for k in range(4):
            dq_ld.dma(wo_buf[:, k * 1024:(k + 1) * 1024], woT_scr[k, :, :])
        tok_wof = dq_ld.fence()
        dq_ld.wait("tensor", tok_wof)

    W1 = CA + 1

    def ap3(t, col0, tstride, nt, w, pw):
        return bass.AP(t, col0, [[pw, 128], [tstride, nt], [1, w]])

    def pl(t, nt=4, w=CA, col0=0, pw=None):
        return ap3(t, col0, w if pw is None else None, nt, w, nt * w)

    def full4(t, w=CA):
        return bass.AP(t, 0, [[4 * w, 128], [1, 4 * w]])

    def strided4(t, col0, w, tstride):
        return bass.AP(t, col0, [[4 * tstride, 128], [tstride, 4], [1, w]])

    def cbc(v, n):
        return bass.AP(consts, CIDX[v], [[16, 128], [0, n]])

    kb.run("vector", "v", lambda ve: ve.memset(onesp[:], 1.0))
    V_ONESP = kb.cnt("v")
    kb.wait("vector", "v", V_ONESP)

    tok_x = [None] * 3
    wa_use = [0, 0]
    wx_use = [0, 0]
    wo_use = [0, 0]
    wa_tok = [None, None]
    wx_tok = [None, None]
    wo_tok = [None, None]

    def stream_tile(dq, scr, k, buf, use, toks, eng="sync"):
        par = k % 2
        if use[par] > 0:
            kb.wait(eng, "t", use[par])
        dq.dma(buf[:, par * 1024:(par + 1) * 1024], scr[k, :, :])
        toks[par] = dq.fence()

    for ch in range(min(BPC * NCHS, PA_LIMIT)):
        first = (ch % NCHS) == 0
        g0 = ch * CA

        def dbgdump(idx, ap, width, ch=ch):
            _c = int(os.environ.get("CHDUMP", "0") or 0)
            if not (debug and ch == _c):
                return
            kb.wait(_steng, "v")
            kb.wait(_steng, "s")
            dq_st.dma(dbg["dbg_pl"][idx, :, 0:width], ap)
            tok = dq_st.fence()
            dq_st.wait("vector", tok)
            dq_st.wait("scalar", tok)

        for tt in range(2):
            dq_ld.dma(x_tm[:, tt * 1024:(tt + 1) * 1024],
                      x_flat[g0 + tt * 128: g0 + (tt + 1) * 128, :])
        tokx = dq_ld.fence()
        dq_ld.wait("tensor", tokx)
        _chd = os.environ.get("CHDUMP")
        if debug and _chd is not None and ch == int(_chd):
            dq_ld.wait(_steng, tokx)
            dq_st.dma(dbg["dbg_pl"][17, :, 0:2048], x_tm[:, 0:2048])
            tok_chd = dq_st.fence()
            dq_st.wait("vector", tok_chd)
        for g in range(8):
            bk = banks[6 + g % 2]
            def tf(te, g=g, bk=bk):
                last = None
                for tt in range(2):
                    last = te.transpose(
                        bk.ap[:, tt * 128:(tt + 1) * 128],
                        x_tm[:, tt * 1024 + g * 128: tt * 1024 + g * 128 + 128],
                        ident[:])
                return last
            bk.produce(tf)
            def ev(ve, g=g, bk=bk):
                return ve.tensor_copy(x_fm[:, g * CA:(g + 1) * CA],
                                      bk.ap[:, 0:CA])
            bk.consume("vector", ev, guard=gsc[:, 0:1])
        V_XFM = kb.cnt("v")
        if debug and _chd is not None and ch == int(_chd):
            kb.wait(_steng, "v", V_XFM)
            dq_st.dma(dbg["dbg_pl"][18, :, 0:2048], x_fm[:, 0:2048])
            tok_chd2 = dq_st.fence()
            dq_st.wait("vector", tok_chd2)

        kb.wait("scalar", "v", V_XFM)
        kb.run("scalar", "s", lambda se: se.square(tmp8[:], x_fm[:]))
        S_XSQ = kb.cnt("s")
        bk6, bk7 = banks[6], banks[7]
        def sums(te, bk=bk6):
            last = None
            for k in range(8):
                last = te.matmul(bk.ap[0:1, 0:CA], ccol(1.0),
                                 x_fm[:, k * CA:(k + 1) * CA],
                                 start=(k == 0), stop=(k == 7))
            return last
        bk6.produce(sums, waitv=V_XFM)
        kb.wait("tensor", "s", S_XSQ)
        def sumsq(te, bk=bk7):
            last = None
            for k in range(8):
                last = te.matmul(bk.ap[0:1, 0:CA], ccol(1.0),
                                 tmp8[:, k * CA:(k + 1) * CA],
                                 start=(k == 0), stop=(k == 7))
            return last
        bk7.produce(sumsq)

        def stats_a(ve, bk=bk6, bq=bk7):
            ve.tensor_single_scalar(srow[0:1, 0:CA], bk.ap[0:1, 0:CA],
                                    1.0 / D, ALU.mult)
            return ve.tensor_single_scalar(srow[0:1, CA:2 * CA],
                                           bq.ap[0:1, 0:CA], 1.0 / D, ALU.mult)
        kb.wait("vector", "t", bk7.prod)
        bk6.consume("vector", stats_a)
        bk7.cons = ("v", kb.cnt("v"))
        kb.wait("vector", "v")
        kb.run("vector", "v",
               lambda ve: ve.tensor_tensor(srow[0:1, 2 * CA:3 * CA],
                                           srow[0:1, 0:CA], srow[0:1, 0:CA],
                                           ALU.mult))
        kb.wait("vector", "v")
        kb.run("vector", "v",
               lambda ve: ve.tensor_tensor(srow[0:1, CA:2 * CA],
                                           srow[0:1, CA:2 * CA],
                                           srow[0:1, 2 * CA:3 * CA],
                                           ALU.subtract))
        kb.wait("scalar", "v")
        kb.run("scalar", "s",
               lambda se: se.activation(srow[0:1, 2 * CA:3 * CA],
                                        srow[0:1, CA:2 * CA], AF.Ln,
                                        bias=crow_ap(EPS)))
        kb.wait("scalar", "s")
        kb.run("scalar", "s",
               lambda se: se.activation(srow[0:1, 2 * CA:3 * CA],
                                        srow[0:1, 2 * CA:3 * CA], AF.Exp,
                                        scale=crow_ap(-0.5)))
        S_RSTD = kb.cnt("s")
        kb.wait("vector", "s", S_RSTD)
        kb.run("vector", "v",
               lambda ve: ve.tensor_tensor(srow[0:1, 3 * CA:4 * CA],
                                           srow[0:1, 0:CA],
                                           srow[0:1, 2 * CA:3 * CA], ALU.mult))
        V_STATS = kb.cnt("v")
        def bc_rstd(te, bk=bk6):
            return te.matmul(bk.ap[:, 0:CA], ones_row[0:1, 0:128],
                             srow[0:1, 2 * CA:3 * CA], start=True, stop=True)
        bk6.produce(bc_rstd, waitv=V_STATS)
        def bc_mrs(te, bk=bk7):
            return te.matmul(bk.ap[:, 0:CA], ones_row[0:1, 0:128],
                             srow[0:1, 3 * CA:4 * CA], start=True, stop=True)
        bk7.produce(bc_mrs)
        def evb(ve, bk=bk6, bq=bk7):
            ve.tensor_copy(rstd_b[:], bk.ap[:, 0:CA])
            return ve.tensor_copy(mrs_b[:], bq.ap[:, 0:CA])
        kb.wait("vector", "t", bk7.prod)
        bk6.consume("vector", evb)
        bk7.cons = ("v", kb.cnt("v"))
        kb.wait("vector", "v", kb.cnt("v"))
        def apply_ln(ve):
            rb = bass.AP(rstd_b, 0, [[CA, 128], [0, 8], [1, CA]])
            mb = bass.AP(mrs_b, 0, [[CA, 128], [0, 8], [1, CA]])
            x8 = bass.AP(x_fm, 0, [[8 * CA, 128], [CA, 8], [1, CA]])
            xn8 = bass.AP(xn, 0, [[8 * CA, 128], [CA, 8], [1, CA]])
            t8 = bass.AP(tmp8, 0, [[8 * CA, 128], [CA, 8], [1, CA]])
            ve.tensor_tensor(t8, x8, rb, ALU.mult)
            return ve.tensor_tensor(xn8, t8, mb, ALU.subtract)
        kb.run("vector", "v", apply_ln)
        V_XN = kb.cnt("v")
        dbgdump(0, xn[:], 8 * CA)

        for k in range(8):
            stream_tile(dq_ld, waT_scr, k, wa_buf, wa_use, wa_tok)
            dq_ld.wait("tensor", wa_tok[k % 2])
            if k == 0:
                for gb in range(8):
                    nm, c = banks[gb].cons
                    kb.wait("tensor", nm, c)
                kb.wait("tensor", "v", V_XN)
            def mma(te, k=k):
                par = k % 2
                last = None
                for mi in range(8):
                    last = te.matmul(
                        banks[mi].ap[:, 0:CA],
                        wa_buf[:, par * 1024 + mi * 128: par * 1024 + (mi + 1) * 128],
                        xn[:, k * CA:(k + 1) * CA],
                        start=(k == 0), stop=(k == 7))
                return last
            kb.run("tensor", "t", mma)
            wa_use[k % 2] = kb.cnt("t")
        for gb in range(8):
            banks[gb].prod = kb.cnt("t")
        def eva(ve):
            for gb in range(4):
                ve.tensor_copy(ar_p[:, gb * CA:(gb + 1) * CA],
                               banks[gb].ap[:, 0:CA])
            last = None
            for gb in range(4):
                last = ve.tensor_copy(ai_p[:, gb * CA:(gb + 1) * CA],
                                      banks[gb + 4].ap[:, 0:CA])
            return last
        kb.wait("vector", "t")
        kb.run("vector", "v", eva)
        for gb in range(8):
            banks[gb].cons = ("v", kb.cnt("v"))

        def postp_a(ve):
            ve.tensor_tensor(t4a[:], ar_p[:], ar_p[:], ALU.mult)
            ve.tensor_tensor(t4b[:], ai_p[:], ai_p[:], ALU.mult)
            return ve.tensor_tensor(m_a[:], t4a[:], t4b[:], ALU.add)
        kb.run("vector", "v", postp_a)
        V_MA = kb.cnt("v")
        dbgdump(1, ar_p[:], 4 * CA)
        dbgdump(2, ai_p[:], 4 * CA)
        kb.wait("scalar", "v", V_MA)
        def lut_a(se):
            se.activation(t4a[:], m_a[:], AF.Ln, bias=ccol(1.0))
            se.activation(t4a[:], t4a[:], AF.Exp, scale=ccol(-1.0))
            se.activation(t4b[:], m_a[:], AF.Ln)
            return se.activation(t4b[:], t4b[:], AF.Exp, scale=ccol(0.5))
        kb.run("scalar", "s", lut_a)
        S_LUTA = kb.cnt("s")
        kb.wait("vector", "s", S_LUTA)
        def rho_den(ve):
            return ve.tensor_tensor(rho[:], m_a[:], t4a[:], ALU.mult)
        kb.run("vector", "v", rho_den)
        V_DEN = kb.cnt("v")
        dbgdump(3, rho[:], 4 * CA)
        def formsel(ve):
            ve.tensor_single_scalar(kint[:, 0:4 * CA], ar_p[:], 0.0,
                                     ALU.is_ge)
            ve.tensor_tensor(d_im[:], t4b[:], ar_p[:], ALU.subtract)
            ve.tensor_tensor(t4a[:], t4b[:], ar_p[:], ALU.add)
            ve.select(d_re[:], kint[:, 0:4 * CA], ai_p[:], d_im[:])
            return ve.select(t4b[:], kint[:, 0:4 * CA], t4a[:], ai_p[:])
        kb.run("vector", "v", formsel)
        def signs(ve):
            ve.tensor_single_scalar(m_a[:], d_re[:], 0.0, ALU.is_ge)
            ve.tensor_scalar(m_a[:], m_a[:], 2.0, -1.0, ALU.mult, ALU.add)
            ve.tensor_single_scalar(d_im[:], t4b[:], 0.0, ALU.is_ge)
            ve.tensor_scalar(d_im[:], d_im[:], 2.0, -1.0, ALU.mult, ALU.add)
            return ve.tensor_tensor(m_a[:], m_a[:], d_im[:], ALU.mult)
        kb.run("vector", "v", signs)
        V_SGN = kb.cnt("v")
        dbgdump(17, m_a[:], 4 * CA)
        dbgdump(18, d_re[:], 4 * CA)
        dbgdump(19, t4b[:], 4 * CA)
        kb.wait("scalar", "v", V_SGN)
        def lut_absnd(se):
            se.activation(d_re[:], d_re[:], AF.Abs)
            return se.activation(t4b[:], t4b[:], AF.Abs)
        kb.run("scalar", "s", lut_absnd)
        S_ABSND = kb.cnt("s")
        kb.wait("vector", "s", S_ABSND)
        kb.run("vector", "v",
               lambda ve: ve.tensor_single_scalar(t4b[:], t4b[:], 1e-12,
                                                  ALU.max))
        V_CL = kb.cnt("v")
        kb.wait("scalar", "v", V_CL)
        def lut_den(se):
            se.activation(t4b[:], t4b[:], AF.Ln)
            return se.activation(t4b[:], t4b[:], AF.Exp, scale=ccol(-1.0))
        kb.run("scalar", "s", lut_den)
        S_DEN = kb.cnt("s")
        kb.wait("vector", "s", S_DEN)
        kb.run("vector", "v",
               lambda ve: ve.tensor_tensor(t4a[:], d_re[:], t4b[:], ALU.mult))
        V_RATIO = kb.cnt("v")
        kb.wait("scalar", "v", V_RATIO)
        def lut_abs(se):
            se.activation(tmp8[:, 0:4 * CA], t4a[:], AF.Ln)
            return se.activation(tmp8[:, 0:4 * CA], tmp8[:, 0:4 * CA], AF.Exp,
                                 scale=ccol(-1.0))
        kb.run("scalar", "s", lut_abs)
        S_ABS = kb.cnt("s")
        kb.wait("vector", "s", S_ABS)
        def minmask(ve):
            ve.tensor_tensor(tmp8[:, 0:4 * CA], t4a[:], tmp8[:, 0:4 * CA],
                             ALU.min)
            return ve.tensor_single_scalar(t4a[:], t4a[:], 1.0, ALU.is_gt)
        kb.run("vector", "v", minmask)
        V_MIN = kb.cnt("v")
        kb.wait("scalar", "v", V_MIN)
        kb.run("scalar", "s",
               lambda se: se.activation(tmp8[:, 0:4 * CA], tmp8[:, 0:4 * CA],
                                        AF.Arctan))
        S_AT = kb.cnt("s")
        kb.wait("vector", "s", S_AT)
        def atanfold(ve):
            ve.tensor_scalar(t4b[:], tmp8[:, 0:4 * CA], -2.0, math.pi / 2,
                             ALU.mult, ALU.add)
            ve.tensor_tensor(t4b[:], t4a[:], t4b[:], ALU.mult)
            ve.tensor_tensor(t4a[:], tmp8[:, 0:4 * CA], t4b[:], ALU.add)
            return ve.tensor_tensor(t4a[:], t4a[:], m_a[:], ALU.mult)
        kb.run("vector", "v", atanfold)
        S_ATAN = kb.cnt("s")
        dbgdump(4, t4a[:], 4 * CA)

        def xcarry(ve, first=first):
            if first:
                return ve.tensor_copy(strided4(X_s, 0, 1, W1), cbc(0.0, 4).unsqueeze(2))
            return ve.tensor_copy(strided4(X_s, 0, 1, W1),
                                  strided4(X_s, CA, 1, W1))
        kb.wait("vector", "s", S_ATAN)
        kb.run("vector", "v", xcarry)
        kb.wait("vector", "v")
        def scans_x(ve):
            last = None
            for g in range(4):
                last = ve.tensor_tensor_scan(
                    X_s[:, g * W1 + 1: g * W1 + 1 + CA],
                    onesp[:], t4a[:, g * CA:(g + 1) * CA],
                    X_s[:, g * W1: g * W1 + 1], ALU.mult, ALU.add)
            return last
        kb.run("vector", "v", scans_x)
        def fold_psi(ve):
            a_ = bass.AP(X_s, 0, [[4 * W1, 128], [1, 4 * W1]])
            p_ = bass.AP(psi, 0, [[4 * W1, 128], [1, 4 * W1]])
            t_ = bass.AP(tmp8, 0, [[8 * CA, 128], [1, 4 * W1]])
            k_ = bass.AP(kint, 0, [[4 * W1, 128], [1, 4 * W1]])
            ve.tensor_single_scalar(t_, a_, 1.0 / math.pi, ALU.mult)
            ve.tensor_copy(k_, t_)
            ve.tensor_copy(t_, k_)
            ve.scalar_tensor_tensor(p_, t_, -math.pi, a_, ALU.mult, ALU.add)
            ve.tensor_single_scalar(p_, p_, 2.0, ALU.mult)
            ve.tensor_single_scalar(t_, p_, math.pi, ALU.is_gt)
            ve.scalar_tensor_tensor(p_, t_, -2.0 * math.pi, p_, ALU.mult, ALU.add)
            ve.tensor_single_scalar(t_, p_, -math.pi, ALU.is_lt)
            return ve.scalar_tensor_tensor(p_, t_, 2.0 * math.pi, p_,
                                           ALU.mult, ALU.add)
        kb.run("vector", "v", fold_psi)
        V_PSI = kb.cnt("v")
        kb.wait("scalar", "v", V_PSI)
        def lut_trig(se):
            se.activation(sinS[:], psi[:], AF.Sin)
            se.activation(schalf[:], psi[:], AF.Sin, scale=ccol(0.5))
            return se.square(schalf[:], schalf[:])
        kb.run("scalar", "s", lut_trig)
        S_TRIG = kb.cnt("s")
        kb.wait("vector", "s", S_TRIG)
        kb.run("vector", "v",
               lambda ve: ve.tensor_scalar(cosS[:], schalf[:], -2.0, 1.0,
                                           ALU.mult, ALU.add))
        V_TRIG = kb.cnt("v")
        dbgdump(5, X_s[:], 4 * W1)
        dbgdump(6, sinS[:], 4 * W1)
        dbgdump(7, cosS[:], 4 * W1)

        for k in range(8):
            stream_tile(dq_ld, wxT_scr, k, wx_buf, wx_use, wx_tok)
            dq_ld.wait("tensor", wx_tok[k % 2])
            if k == 0:
                for gb in range(8):
                    nm, c = banks[gb].cons
                    kb.wait("tensor", nm, c)
            def mmx(te, k=k):
                par = k % 2
                last = None
                for mi in range(8):
                    last = te.matmul(
                        banks[mi].ap[:, 0:CA],
                        wx_buf[:, par * 1024 + mi * 128: par * 1024 + (mi + 1) * 128],
                        xn[:, k * CA:(k + 1) * CA],
                        start=(k == 0), stop=(k == 7))
                return last
            kb.run("tensor", "t", mmx)
            wx_use[k % 2] = kb.cnt("t")
        for gb in range(8):
            banks[gb].prod = kb.cnt("t")
        def xccarry(ve, first=first):
            if first:
                ve.tensor_copy(strided4(xcr_s, 0, 1, W1),
                               bass.AP(h0_sb, 0, [[8, 128], [1, 4]]).unsqueeze(2))
                return ve.tensor_copy(strided4(xci_s, 0, 1, W1),
                                      bass.AP(h0_sb, 4, [[8, 128], [1, 4]]).unsqueeze(2))
            ve.tensor_copy(strided4(xcr_s, 0, 1, W1), strided4(xcr_s, CA, 1, W1))
            return ve.tensor_copy(strided4(xci_s, 0, 1, W1),
                                  strided4(xci_s, CA, 1, W1))
        kb.run("vector", "v", xccarry)
        def evx(ve):
            for gb in range(4):
                ve.tensor_copy(t4a[:, gb * CA:(gb + 1) * CA],
                               banks[gb].ap[:, 0:CA])
            last = None
            for gb in range(4):
                last = ve.tensor_copy(t4b[:, gb * CA:(gb + 1) * CA],
                                      banks[gb + 4].ap[:, 0:CA])
            return last
        kb.wait("vector", "t")
        kb.run("vector", "v", evx)
        for gb in range(8):
            banks[gb].cons = ("v", kb.cnt("v"))
        def mx_fn(ve):
            ve.tensor_tensor(m_a[:], t4a[:], t4a[:], ALU.mult)
            ve.tensor_tensor(ar_p[:], t4b[:], t4b[:], ALU.mult)
            return ve.tensor_tensor(m_a[:], m_a[:], ar_p[:], ALU.add)
        kb.run("vector", "v", mx_fn)
        V_MX = kb.cnt("v")
        kb.wait("scalar", "v", V_MX)
        def lut_sqx(se):
            se.activation(ar_p[:], m_a[:], AF.Ln)
            return se.activation(ar_p[:], ar_p[:], AF.Exp, scale=ccol(0.5))
        kb.run("scalar", "s", lut_sqx)
        S_SQX = kb.cnt("s")
        kb.wait("vector", "s", S_SQX)
        def xc_fn(ve):
            ve.tensor_tensor(
                strided4(xcr_s, 1, CA, W1), pack4(t4a), pack4(ar_p), ALU.mult)
            return ve.tensor_tensor(
                strided4(xci_s, 1, CA, W1), pack4(t4b), pack4(ar_p), ALU.mult)
        def pack4(t):
            return bass.AP(t, 0, [[4 * CA, 128], [CA, 4], [1, CA]])
        kb.run("vector", "v", xc_fn)
        V_XC = kb.cnt("v")
        dbgdump(8, xcr_s[:], 4 * W1)
        dbgdump(9, xci_s[:], 4 * W1)

        for gb in range(8):
            nm, c = banks[gb].cons
            kb.wait("tensor", nm, c)
        kb.wait("tensor", "v", V_XC)
        def mmu(te):
            last = None
            for comp in range(2):
                for g in range(4):
                    bk_ap = banks[4 * comp + g].ap
                    if comp == 0:
                        pairs = [(x_re, xcr_s), (x_imn, xci_s)]
                    else:
                        pairs = [(x_im, xcr_s), (x_re, xci_s)]
                    firstmm = True
                    for pi, (lhs, rhs) in enumerate(pairs):
                        for k in range(4):
                            last = te.matmul(
                                bk_ap[:, 0:CA],
                                lhs[:, k * INNER + g * 128: k * INNER + g * 128 + 128],
                                rhs[:, k * W1: k * W1 + CA],
                                start=firstmm, stop=(pi == 1 and k == 3))
                            firstmm = False
            return last
        kb.run("tensor", "t", mmu)
        for gb in range(8):
            banks[gb].prod = kb.cnt("t")
        T_U = kb.cnt("t")

        kb.wait("vector", "t", T_U)
        def dbuild(ve):
            last = None
            for comp in range(2):
                dst = d_re if comp == 0 else d_im
                for g in range(4):
                    u_re_ap = banks[g].ap[:, 0:CA]
                    u_im_ap = banks[4 + g].ap[:, 0:CA]
                    cS = cosS[:, g * W1: g * W1 + CA]
                    sS = sinS[:, g * W1: g * W1 + CA]
                    ta = t4a[:, g * CA:(g + 1) * CA]
                    tb = t4b[:, g * CA:(g + 1) * CA]
                    if comp == 0:
                        ve.tensor_tensor(ta, cS, u_re_ap, ALU.mult)
                        ve.tensor_tensor(tb, sS, u_im_ap, ALU.mult)
                        ve.tensor_tensor(ta, ta, tb, ALU.add)
                    else:
                        ve.tensor_tensor(ta, cS, u_im_ap, ALU.mult)
                        ve.tensor_tensor(tb, sS, u_re_ap, ALU.mult)
                        ve.tensor_tensor(ta, ta, tb, ALU.subtract)
                    last = ve.tensor_tensor(dst[:, g * CA:(g + 1) * CA], ta,
                                            rho[:, g * CA:(g + 1) * CA], ALU.mult)
            return last
        kb.run("vector", "v", dbuild)
        for gb in range(8):
            banks[gb].cons = ("v", kb.cnt("v"))
        dbgdump(10, d_re[:], 4 * CA)
        dbgdump(11, d_im[:], 4 * CA)

        def wcarry(ve, first=first):
            if first:
                ve.tensor_copy(strided4(w_re, 0, 1, W1), cbc(0.0, 4).unsqueeze(2))
                return ve.tensor_copy(strided4(w_im, 0, 1, W1),
                                      cbc(0.0, 4).unsqueeze(2))
            ve.tensor_copy(strided4(w_re, 0, 1, W1), strided4(w_re, CA, 1, W1))
            return ve.tensor_copy(strided4(w_im, 0, 1, W1),
                                  strided4(w_im, CA, 1, W1))
        kb.run("vector", "v", wcarry)
        kb.wait("vector", "v")
        def scans_w(ve):
            last = None
            for g in range(4):
                last = ve.tensor_tensor_scan(
                    w_re[:, g * W1 + 1: g * W1 + 1 + CA],
                    rho[:, g * CA:(g + 1) * CA], d_re[:, g * CA:(g + 1) * CA],
                    w_re[:, g * W1: g * W1 + 1], ALU.mult, ALU.add)
            for g in range(4):
                last = ve.tensor_tensor_scan(
                    w_im[:, g * W1 + 1: g * W1 + 1 + CA],
                    rho[:, g * CA:(g + 1) * CA], d_im[:, g * CA:(g + 1) * CA],
                    w_im[:, g * W1: g * W1 + 1], ALU.mult, ALU.add)
            return last
        kb.run("vector", "v", scans_w)
        dbgdump(12, w_re[:], 4 * W1)
        dbgdump(13, w_im[:], 4 * W1)

        def unrot(ve):
            cS = strided4(cosS, 1, CA, W1)
            sS = strided4(sinS, 1, CA, W1)
            wr = strided4(w_re, 1, CA, W1)
            wi = strided4(w_im, 1, CA, W1)
            ve.tensor_tensor(pack4(t4a), cS, wr, ALU.mult)
            ve.tensor_tensor(pack4(t4b), sS, wi, ALU.mult)
            ve.tensor_tensor(pack4(d_re), pack4(t4a), pack4(t4b), ALU.subtract)
            ve.tensor_tensor(pack4(t4a), cS, wi, ALU.mult)
            ve.tensor_tensor(pack4(t4b), sS, wr, ALU.mult)
            return ve.tensor_tensor(pack4(d_im), pack4(t4a), pack4(t4b), ALU.add)
        kb.run("vector", "v", unrot)
        V_S = kb.cnt("v")
        dbgdump(14, d_re[:], 4 * CA)
        dbgdump(15, d_im[:], 4 * CA)

        for gb in range(4):
            nm, c = banks[gb].cons
            kb.wait("tensor", nm, c)
        kb.wait("tensor", "v", V_S)
        def mmvs(te):
            last = None
            for g in range(4):
                bk_ap = banks[g].ap
                firstmm = True
                for pi, (lhs, rhs) in enumerate([(vT_re, d_re), (vT_imn, d_im)]):
                    for k in range(4):
                        last = te.matmul(
                            bk_ap[:, 0:CA],
                            lhs[:, k * INNER + g * 128: k * INNER + g * 128 + 128],
                            rhs[:, k * CA:(k + 1) * CA],
                            start=firstmm, stop=(pi == 1 and k == 3))
                        firstmm = False
            return last
        kb.run("tensor", "t", mmvs)
        for gb in range(4):
            banks[gb].prod = kb.cnt("t")
        kb.wait("vector", "t")
        def evvs(ve):
            last = None
            for g in range(4):
                last = ve.tensor_tensor(
                    hre[:, g * CA:(g + 1) * CA], banks[g].ap[:, 0:CA],
                    xcr_s[:, g * W1 + 1: g * W1 + 1 + CA], ALU.add)
            return last
        kb.run("vector", "v", evvs)
        for gb in range(4):
            banks[gb].cons = ("v", kb.cnt("v"))
        V_HRE = kb.cnt("v")
        dbgdump(16, hre[:], 4 * CA)

        for k in range(4):
            if not _wofull:
                stream_tile(dq_ld, woT_scr, k, wo_buf, wo_use, wo_tok)
                dq_ld.wait("tensor", wo_tok[k % 2])
            if k == 0:
                for gb in range(8):
                    nm, c = banks[gb].cons
                    kb.wait("tensor", nm, c)
                kb.wait("tensor", "v", V_HRE)
            def mmo(te, k=k):
                par = k if _wofull else k % 2
                last = None
                for mi in range(8):
                    last = te.matmul(
                        banks[mi].ap[:, 0:CA],
                        wo_buf[:, par * 1024 + mi * 128: par * 1024 + (mi + 1) * 128],
                        hre[:, k * CA:(k + 1) * CA],
                        start=(k == 0), stop=(k == 3))
                return last
            kb.run("tensor", "t", mmo)
            wo_use[k % 2] = kb.cnt("t")
        for gb in range(8):
            banks[gb].prod = kb.cnt("t")
        if debug and _chd is not None and ch == int(_chd):
            kb.wait(_steng, "s")
            kb.wait(_steng, "v")
            kb.wait(_steng, "t")
            dq_st.dma(dbg["dbg_pl"][19, :, 0:1024], hre[:, 0:1024])
            dq_st.dma(dbg["dbg_pl"][16, :, 0:2048], x_fm[:, 0:2048])
            dq_st.dma(dbg["dbg_xp"][0, :, 2048:4096], wo_buf[:, 0:2048])
            kb.wait("vector", "t")
            def _psevac(ve):
                last = None
                for _gb in range(8):
                    last = ve.tensor_copy(tmp8[:, _gb * CA:(_gb + 1) * CA],
                                          banks[_gb].ap[:, 0:CA])
                return last
            kb.run("vector", "v", _psevac)
            kb.wait(_steng, "v", kb.cnt("v"))
            dq_st.dma(dbg["dbg_xp"][1, :, 2048:4096], tmp8[:, 0:2048])
            tok_chd3 = dq_st.fence()
            dq_st.wait("vector", tok_chd3)
        kb.wait("vector", "t")
        def evo(ve):
            last = None
            for mi in range(8):
                last = ve.tensor_tensor(
                    x_fm[:, mi * CA:(mi + 1) * CA], x_fm[:, mi * CA:(mi + 1) * CA],
                    banks[mi].ap[:, 0:CA], ALU.add)
            return last
        kb.run("vector", "v", evo)
        for gb in range(8):
            banks[gb].cons = ("v", kb.cnt("v"))
        V_XP = kb.cnt("v")
        if ch == min(BPC * NCHS, PA_LIMIT) - 1:
            kb.wait("vector", "v", V_XP)
            kb.run("vector", "v",
                   lambda ve: ve.tensor_copy(xp_keep[:], x_fm[:]))
            V_XPKEEP = kb.cnt("v")
        if debug and _chd is not None and ch == int(_chd):
            kb.wait(_steng, "v", V_XP)
            dq_st.dma(dbg["dbg_xp"][2, :, 2048:4096], x_fm[:, 0:2048])
            tok_chd4 = dq_st.fence()
            dq_st.wait("vector", tok_chd4)

        kb.wait(_steng, "v", V_XP)
        for g in range(8):
            dq_st.dma(xp_scr[g, :, g0:g0 + CA], x_fm[:, g * CA:(g + 1) * CA])
            if debug:
                dq_st.dma(dbg["dbg_xp"][g, :, g0:g0 + CA],
                          x_fm[:, g * CA:(g + 1) * CA])
        tok_xp = dq_st.fence()
        dq_st.wait("vector", tok_xp)

    if stage == "passA":
        for dq in (dq_st, dq_ld, dq_gl):
            for tok in dq.final_tokens():
                dq.wait("gpsimd", tok)
        _finish(nc, kb)
        es_pa.close()
        es_vx.close()
        es.close()
        return nc, dbg

    if os.environ.get("FAKEP"):
        for gb in range(8):
            bk = banks[gb]
            def fake(te, bk=bk):
                last = None
                for k in range(4):
                    last = te.matmul(bk.ap[:, 0:CA],
                                     wo_buf[:, k * 128:(k + 1) * 128],
                                     hre[:, 0:CA],
                                     start=(k == 0), stop=(k == 3))
                return last
            bk.produce(fake)
            def fcons(ve, bk=bk):
                return ve.tensor_copy(tmp8[:, 0:CA], bk.ap[:, 0:CA])
            bk.consume("vector", fcons)

    _tpad = int(os.environ.get("TPAD", "0"))
    for _i in range(_tpad):
        kb.run("tensor", "t",
               lambda te, _i=_i: te.matmul(banks[6 + _i % 2].ap[:, 0:128],
                                           ident[:], ident[:],
                                           start=True, stop=True))
        if _i % 4 == 3:
            kb.wait("tensor", "t", kb.cnt("t"))

    if os.environ.get("SAC"):
        for g in range(8):
            dq_st.dma(xp_scr[g, :, 0:CA], x_fm[:, g * CA:(g + 1) * CA])
        dq_st.fence()
    _nflush = int(os.environ.get("NFLUSH", "0"))
    for _ in range(_nflush):
        dq_st.dma(sac_scr[:, 0:2], gsc[:, 0:2])
    if _nflush:
        dq_st.fence()
    ndelay = int(os.environ.get("PADELAY", "0"))
    for _ in range(ndelay):
        kb.run("vector", "v",
               lambda ve: ve.tensor_copy(tmp8[:, 0:2048], xn[:, 0:2048]))
    if ndelay:
        kb.wait("vector", "v", kb.cnt("v"))
    for tok in dq_st.final_tokens():
        for _e in BARRIER_ENGS:
            dq_st.wait(_e, tok)
    es_pa.close()
    es_vx.close()

    es_pb = ExitStack()
    def sbb(name, p, f):
        return es_pb.enter_context(nc.sbuf_tensor(name, [p, f], F32))
    w1g = sbb("w1g", 128, 2 * 512)
    w3g = sbb("w3g", 128, 2 * 512)
    w2k = sbb("w2k", 128, 2 * D)
    xn2 = sbb("xn2", 128, 8 * TB)
    xpb = sbb("xpb", 128, 8 * TB)
    y_sb = sbb("y_sb", 128, 8 * TB)
    h_buf = sbb("h_buf", 128, HT * TB)
    t1_sb = sbb("t1_sb", 128, 2 * TB)
    srow2 = sbb("srow2", 1, 4 * TB)
    rstd2 = sbb("rstd2", 128, TB)
    mrs2 = sbb("mrs2", 128, TB)
    rw_tm = sbb("rw_tm", 128, 4 * NEXP)
    rws = sbb("rws", NEXP, TB)
    rw_b = sbb("rw_b", 128, TB)
    selP = sbb("selP", NEXP, NEXP * 128)
    b2all = sbb("b2all", NEXP, D)
    sb2row = sbb("sb2row", 1, D)
    b1row = sbb("b1row", 1, HDIM)
    b3row = sbb("b3row", 1, HDIM)
    rt1 = sbb("rt1", 128, 4 * NEXP)
    rt2 = sbb("rt2", 128, 4 * NEXP)
    rrow = sbb("rrow", 128, 8)
    rt3 = sbb("rt3", 128, 4 * NEXP)
    rt4 = sbb("rt4", 128, 4 * NEXP)
    yt_sb = sbb("yt_sb", 128, TB)

    _pbsetup = int(os.environ.get("PBSETUP", "3"))
    if _pbsetup & 1:
        dq_gl.dma(b2all[:, :], b2_h[:, :])
        dq_gl.dma(sb2row[0:1, :], sb2_h.unsqueeze(0))
        tok_bias = dq_gl.fence()
        dq_gl.wait("tensor", tok_bias)

    if _pbsetup & 2:
        kb.run("gpsimd", "g", lambda e: e.iota(h_buf[0:NEXP, 0:NEXP * 128],
                                               [[1, NEXP], [0, 128]], base=0,
                                               channel_multiplier=0,
                                               allow_small_or_imprecise_dtypes=True))
        kb.run("gpsimd", "g", lambda e: e.iota(h_buf[0:NEXP, 1024:1024 + NEXP * 128],
                                               [[0, NEXP * 128]], base=0,
                                               channel_multiplier=1,
                                               allow_small_or_imprecise_dtypes=True))
        kb.wait("vector", "g")
        kb.run("vector", "v",
               lambda ve: ve.tensor_tensor(selP[:, :], h_buf[0:NEXP, 0:NEXP * 128],
                                           h_buf[0:NEXP, 1024:1024 + NEXP * 128],
                                           ALU.is_equal))
        V_SEL = kb.cnt("v")
        kb.wait("tensor", "v", V_SEL)
        kb.wait("vector", "v", V_SEL)
        kb.wait("scalar", "v", V_SEL)

    w13_use = [0, 0]
    w13_tok = [None, None]
    w13_i = [0]
    w2_use = [0, 0]
    w2_tok = [None, None]
    w2_i = [0]
    bias_use = [0]

    MGRP = [(0, 4), (4, 4), (8, 3)]

    def stream_w13(e, m0, nm, k):
        par = w13_i[0] % 2
        w13_i[0] += 1
        if w13_use[par] > 0:
            kb.wait("sync", "t", w13_use[par])
        c0, c1 = m0 * 128, (m0 + nm) * 128
        if e < NEXP:
            dq_ld.dma(w1g[:, par * 512: par * 512 + (c1 - c0)],
                      w1_h[e, k * 128:(k + 1) * 128, c0:c1])
            dq_ld.dma(w3g[:, par * 512: par * 512 + (c1 - c0)],
                      w3_h[e, k * 128:(k + 1) * 128, c0:c1])
        else:
            dq_ld.dma(w1g[:, par * 512: par * 512 + (c1 - c0)],
                      sw1_h[k * 128:(k + 1) * 128, c0:c1])
            dq_ld.dma(w3g[:, par * 512: par * 512 + (c1 - c0)],
                      sw3_h[k * 128:(k + 1) * 128, c0:c1])
        w13_tok[par] = dq_ld.fence()
        return par

    def stream_w2(e, k):
        par = w2_i[0] % 2
        w2_i[0] += 1
        if w2_use[par] > 0:
            kb.wait("sync", "t", w2_use[par])
        if e < NEXP:
            dq_ld.dma(w2k[:, par * D:(par + 1) * D],
                      w2_h[e, k * 128:(k + 1) * 128, :])
        else:
            dq_ld.dma(w2k[:, par * D:(par + 1) * D],
                      sw2_h[k * 128:(k + 1) * 128, :])
        w2_tok[par] = dq_ld.fence()
        return par

    tok_out_prev = [None]
    _pblim = int(os.environ.get("PBLIM", str(10**9)))
    for tb in range(min(NTB, _pblim)):
        g0 = tb * TB
        if debug and os.environ.get("TRKDUMP"):
            kb.wait("gpsimd", "v")
            for g in range(8):
                dq_gl.dma(dbg["dbg_xp"][g, :, tb * 256:(tb + 1) * 256],
                          xp_scr[g, :, 3840:4096])
            dq_gl.fence()
        if tok_out_prev[0] is not None:
            dq_st.wait("gpsimd", tok_out_prev[0])
        for g in range(8):
            dq_gl.dma(xpb[:, g * TB:(g + 1) * TB], xp_scr[g, :, g0:g0 + TB])
        tok_xp = dq_gl.fence()
        dq_gl.wait("vector", tok_xp)
        dq_gl.wait("scalar", tok_xp)
        dq_gl.wait("tensor", tok_xp)
        if tb == NTB - 1:
            def patch_xpb(ve):
                dst = bass.AP(xpb, CA, [[8 * TB, 128], [TB, 8], [1, CA]])
                srcp = bass.AP(xp_keep, 0, [[8 * CA, 128], [CA, 8], [1, CA]])
                return ve.tensor_copy(dst, srcp)
            kb.run("vector", "v", patch_xpb)
            V_PB_PATCH = kb.cnt("v")
            kb.wait("scalar", "v", V_PB_PATCH)
            kb.wait("tensor", "v", V_PB_PATCH)

        kb.wait("scalar", "t")
        def sq2(se):
            return se.square(h_buf[:, 0:8 * TB], xpb[:])
        kb.run("scalar", "s", sq2)
        S_SQ2 = kb.cnt("s")
        bk6, bk7 = banks[6], banks[7]
        def sums2(te, bk=bk6):
            last = None
            for k in range(8):
                last = te.matmul(bk.ap[0:1, 0:TB], ccol(1.0),
                                 xpb[:, k * TB:(k + 1) * TB],
                                 start=(k == 0), stop=(k == 7))
            return last
        bk6.produce(sums2)
        kb.wait("tensor", "s", S_SQ2)
        def sumsq2(te, bk=bk7):
            last = None
            for k in range(8):
                last = te.matmul(bk.ap[0:1, 0:TB], ccol(1.0),
                                 h_buf[:, k * TB:(k + 1) * TB],
                                 start=(k == 0), stop=(k == 7))
            return last
        bk7.produce(sumsq2)
        def stats2_a(ve, bk=bk6, bq=bk7):
            ve.tensor_single_scalar(srow2[0:1, 0:TB], bk.ap[0:1, 0:TB],
                                    1.0 / D, ALU.mult)
            return ve.tensor_single_scalar(srow2[0:1, TB:2 * TB],
                                           bq.ap[0:1, 0:TB], 1.0 / D, ALU.mult)
        kb.wait("vector", "t", bk7.prod)
        bk6.consume("vector", stats2_a)
        bk7.cons = ("v", kb.cnt("v"))
        kb.wait("vector", "v")
        kb.run("vector", "v",
               lambda ve: ve.tensor_tensor(srow2[0:1, 2 * TB:3 * TB],
                                           srow2[0:1, 0:TB], srow2[0:1, 0:TB],
                                           ALU.mult))
        kb.wait("vector", "v")
        kb.run("vector", "v",
               lambda ve: ve.tensor_tensor(srow2[0:1, TB:2 * TB],
                                           srow2[0:1, TB:2 * TB],
                                           srow2[0:1, 2 * TB:3 * TB],
                                           ALU.subtract))
        kb.wait("scalar", "v")
        kb.run("scalar", "s",
               lambda se: se.activation(srow2[0:1, 2 * TB:3 * TB],
                                        srow2[0:1, TB:2 * TB], AF.Ln,
                                        bias=crow_ap(EPS)))
        kb.wait("scalar", "s")
        kb.run("scalar", "s",
               lambda se: se.activation(srow2[0:1, 2 * TB:3 * TB],
                                        srow2[0:1, 2 * TB:3 * TB], AF.Exp,
                                        scale=crow_ap(-0.5)))
        S_R2 = kb.cnt("s")
        kb.wait("vector", "s", S_R2)
        kb.run("vector", "v",
               lambda ve: ve.tensor_tensor(srow2[0:1, 3 * TB:4 * TB],
                                           srow2[0:1, 0:TB],
                                           srow2[0:1, 2 * TB:3 * TB], ALU.mult))
        V_ST2 = kb.cnt("v")
        def bcr(te, bk=bk6):
            return te.matmul(bk.ap[:, 0:TB], ones_row[0:1, 0:128],
                             srow2[0:1, 2 * TB:3 * TB], start=True, stop=True)
        bk6.produce(bcr, waitv=V_ST2)
        def bcm(te, bk=bk7):
            return te.matmul(bk.ap[:, 0:TB], ones_row[0:1, 0:128],
                             srow2[0:1, 3 * TB:4 * TB], start=True, stop=True)
        bk7.produce(bcm)
        def evb2(ve, bk=bk6, bq=bk7):
            ve.tensor_copy(rstd2[:], bk.ap[:, 0:TB])
            return ve.tensor_copy(mrs2[:], bq.ap[:, 0:TB])
        kb.wait("vector", "t", bk7.prod)
        bk6.consume("vector", evb2)
        bk7.cons = ("v", kb.cnt("v"))
        kb.wait("vector", "v", kb.cnt("v"))
        def apply_ln2(ve):
            rb = bass.AP(rstd2, 0, [[TB, 128], [0, 8], [1, TB]])
            mb = bass.AP(mrs2, 0, [[TB, 128], [0, 8], [1, TB]])
            x8 = bass.AP(xpb, 0, [[8 * TB, 128], [TB, 8], [1, TB]])
            xn8 = bass.AP(xn2, 0, [[8 * TB, 128], [TB, 8], [1, TB]])
            ve.tensor_tensor(xn8, x8, rb, ALU.mult)
            return ve.tensor_tensor(xn8, xn8, mb, ALU.subtract)
        kb.run("vector", "v", apply_ln2)
        V_XN2 = kb.cnt("v")

        def bdump(idx, ap, width, tb=tb):
            import os as _os3
            if not (debug and tb == int(_os3.environ.get("BDTB", "0"))):
                return
            kb.wait(_steng, "v")
            kb.wait(_steng, "s")
            dq_st.dma(dbg["dbg_pl"][idx, :, 0:width], ap)
            tok = dq_st.fence()
            dq_st.wait("vector", tok)
            dq_st.wait("scalar", tok)
        bdump(0, xpb[:, 0:2048], 2048)
        bdump(1, xn2[:, 0:2048], 2048)

        for tt in range(4):
            bk = banks[tt]
            def logits(te, tt=tt, bk=bk):
                last = None
                for k in range(8):
                    last = te.matmul(
                        bk.ap[:, 0:NEXP],
                        xn2[:, k * TB + tt * 128: k * TB + (tt + 1) * 128],
                        gwT[:, k * NEXP:(k + 1) * NEXP],
                        start=(k == 0), stop=(k == 7))
                return last
            bk.produce(logits, waitv=V_XN2)
        def router_m1(ve):
            last = None
            for tt in range(4):
                last = ve.reduce_max(rrow[:, tt:tt + 1],
                                     banks[tt].ap[:, 0:NEXP], axis=AXX)
            return last
        kb.wait("vector", "t")
        kb.run("vector", "v", router_m1)
        kb.wait("vector", "v")
        def router_v1(ve):
            last = None
            for tt in range(4):
                last = ve.tensor_scalar(rt1[:, tt * 8:(tt + 1) * 8],
                                        banks[tt].ap[:, 0:NEXP],
                                        rrow[:, tt:tt + 1], None, ALU.subtract)
            return last
        kb.run("vector", "v", router_v1)
        for tt in range(4):
            banks[tt].cons = ("v", kb.cnt("v"))
        bdump(7, rt1[:, 0:32], 32)
        bdump(8, rrow[:, 0:8], 8)
        kb.wait("scalar", "v")
        kb.run("scalar", "s",
               lambda se: se.activation(rt1[:], rt1[:], AF.Exp))
        S_REXP = kb.cnt("s")
        kb.wait("vector", "s", S_REXP)
        def router_v2a(ve):
            last = None
            for tt in range(4):
                last = ve.reduce_sum(rrow[:, 4 + tt:5 + tt],
                                     rt1[:, tt * 8:(tt + 1) * 8], axis=AXX)
            return last
        kb.run("vector", "v", router_v2a)
        kb.wait("vector", "v")
        kb.run("vector", "v",
               lambda ve: ve.tensor_copy(rt2[:, 0:4], rrow[:, 4:8]))
        V_RS = kb.cnt("v")
        kb.wait("scalar", "v", V_RS)
        kb.run("scalar", "s",
               lambda se: se.activation(rt2[:, 0:4], rt2[:, 0:4], AF.Ln))
        kb.wait("scalar", "s")
        kb.run("scalar", "s",
               lambda se: se.activation(rt2[:, 0:4], rt2[:, 0:4], AF.Exp,
                                        scale=ccol(-1.0)))
        S_RINV = kb.cnt("s")
        kb.wait("vector", "s", S_RINV)
        bdump(9, rt2[:, 0:8], 8)
        bdump(10, rt1[:, 0:32], 32)
        def phase(fn):
            kb.run("vector", "v", fn)
            kb.wait("vector", "v")

        def r_score(ve):
            last = None
            for tt in range(4):
                sl = slice(tt * 8, (tt + 1) * 8)
                last = ve.tensor_scalar(rt1[:, sl], rt1[:, sl],
                                        rt2[:, tt:tt + 1], None, ALU.mult)
            for tt in range(4):
                sl = slice(tt * 8, (tt + 1) * 8)
                last = ve.tensor_tensor(rw_tm[:, sl], rt1[:, sl], gbb[:],
                                        ALU.add)
            for tt in range(4):
                sl = slice(tt * 8, (tt + 1) * 8)
                last = ve.reduce_max(rrow[:, tt:tt + 1], rw_tm[:, sl], axis=AXX)
            return last
        phase(r_score)
        bdump(11, rt1[:, 0:32], 32)
        def r_eq(ve):
            last = None
            for tt in range(4):
                sl = slice(tt * 8, (tt + 1) * 8)
                last = ve.tensor_scalar(rt3[:, sl], rw_tm[:, sl],
                                        rrow[:, tt:tt + 1], None, ALU.is_equal)
            for tt in range(4):
                sl = slice(tt * 8, (tt + 1) * 8)
                last = ve.scalar_tensor_tensor(rt4[:, sl], rt3[:, sl], -1e30,
                                               rw_tm[:, sl], ALU.mult, ALU.add)
            for tt in range(4):
                sl = slice(tt * 8, (tt + 1) * 8)
                last = ve.reduce_max(rrow[:, 4 + tt:5 + tt], rt4[:, sl],
                                     axis=AXX)
            return last
        phase(r_eq)
        def r_mask(ve):
            last = None
            for tt in range(4):
                sl = slice(tt * 8, (tt + 1) * 8)
                last = ve.tensor_scalar(rt3[:, sl], rw_tm[:, sl],
                                        rrow[:, 4 + tt:5 + tt], None, ALU.is_ge)
            for tt in range(4):
                sl = slice(tt * 8, (tt + 1) * 8)
                last = ve.tensor_tensor(rw_tm[:, sl], rt1[:, sl],
                                        rt3[:, sl], ALU.mult)
            return last
        kb.run("vector", "v", r_mask)
        V_RW = kb.cnt("v")
        bdump(2, rw_tm[:, 0:32], 32)
        bdump(3, rt1[:, 0:32], 32)

        bk4 = banks[4]
        def rwt(te, bk=bk4):
            last = None
            for tt in range(4):
                last = te.transpose(bk.ap[0:NEXP, tt * 128:(tt + 1) * 128],
                                    rw_tm[:, tt * 8:(tt + 1) * 8],
                                    ident[:])
            return last
        bk4.produce(rwt, waitv=V_RW)
        bk4.consume("vector",
                    lambda ve, bk=bk4: ve.tensor_copy(rws[:, :],
                                                      bk.ap[0:NEXP, 0:TB]))
        V_RWS = kb.cnt("v")

        for j in range(8):
            bk = banks[4 + j % 2]
            def yinit(te, j=j, bk=bk):
                te.matmul(bk.ap[:, 0:TB], b2all[:, j * 128:(j + 1) * 128],
                          rws[:, :], start=True, stop=False)
                return te.matmul(bk.ap[:, 0:TB],
                                 sb2row[0:1, j * 128:(j + 1) * 128],
                                 ones_row[0:1, 0:TB], start=False, stop=True)
            bk.produce(yinit, waitv=V_RWS)
            def evy0(ve, j=j, bk=bk):
                return ve.tensor_copy(y_sb[:, j * TB:(j + 1) * TB],
                                      bk.ap[:, 0:TB])
            bk.consume("vector", evy0)

        for e in range(NEXP + 1):
            shared = (e == NEXP)
            if not shared:
                bk5 = banks[5]
                def rwb(te, e=e, bk=bk5):
                    return te.matmul(bk.ap[:, 0:TB],
                                     selP[:, e * 128:(e + 1) * 128],
                                     rws[:, :], start=True, stop=True)
                bk5.produce(rwb)
                bk5.consume("vector",
                            lambda ve, bk=bk5: ve.tensor_copy(rw_b[:],
                                                              bk.ap[:, 0:TB]))
            V_RWB = kb.cnt("v")

            kb.wait("gpsimd", "t", bias_use[0])
            if e < NEXP:
                dq_gl.dma(b1row[0:1, :], b1_h[e].unsqueeze(0))
                dq_gl.dma(b3row[0:1, :], b3_h[e].unsqueeze(0))
            else:
                dq_gl.dma(b1row[0:1, :], sb1_h.unsqueeze(0))
                dq_gl.dma(b3row[0:1, :], sb3_h.unsqueeze(0))
            tok_b = dq_gl.fence()
            dq_gl.wait("tensor", tok_b)

            for m0, nm in MGRP:
                for mi in range(nm):
                    for gb in (banks[mi], banks[4 + mi]):
                        nmn, c = gb.cons
                        kb.wait("tensor", nmn, c)
                if m0 == 0 and e == 0:
                    kb.wait("tensor", "v", V_XN2)
                for k in range(8):
                    par = stream_w13(e, m0, nm, k)
                    dq_ld.wait("tensor", w13_tok[par])
                    def mmh(te, k=k, nm=nm, par=par):
                        last = None
                        for mi in range(nm):
                            last = te.matmul(
                                banks[mi].ap[:, 0:TB],
                                w1g[:, par * 512 + mi * 128:
                                    par * 512 + (mi + 1) * 128],
                                xn2[:, k * TB:(k + 1) * TB],
                                start=(k == 0), stop=False)
                        for mi in range(nm):
                            last = te.matmul(
                                banks[4 + mi].ap[:, 0:TB],
                                w3g[:, par * 512 + mi * 128:
                                    par * 512 + (mi + 1) * 128],
                                xn2[:, k * TB:(k + 1) * TB],
                                start=(k == 0), stop=False)
                        return last
                    kb.run("tensor", "t", mmh)
                    w13_use[par] = kb.cnt("t")
                def mmbias(te, m0=m0, nm=nm):
                    last = None
                    for mi in range(nm):
                        m = m0 + mi
                        last = te.matmul(banks[mi].ap[:, 0:TB],
                                         b1row[0:1, m * 128:(m + 1) * 128],
                                         ones_row[0:1, 0:TB],
                                         start=False, stop=True)
                        last = te.matmul(banks[4 + mi].ap[:, 0:TB],
                                         b3row[0:1, m * 128:(m + 1) * 128],
                                         ones_row[0:1, 0:TB],
                                         start=False, stop=True)
                    return last
                kb.run("tensor", "t", mmbias)
                bias_use[0] = kb.cnt("t")
                for mi in range(nm):
                    banks[mi].prod = kb.cnt("t")
                    banks[4 + mi].prod = kb.cnt("t")
                for mi in range(nm):
                    m = m0 + mi
                    par2 = mi % 2
                    def evh1(se, mi=mi, par2=par2):
                        return se.copy(t1_sb[:, par2 * TB:(par2 + 1) * TB],
                                       banks[mi].ap[:, 0:TB])
                    banks[mi].consume("scalar", evh1)
                    S_T1 = kb.cnt("s")
                    kb.wait("vector", "s", S_T1)
                    def evh2(ve, mi=mi, par2=par2):
                        return ve.tensor_tensor(
                            t1_sb[:, par2 * TB:(par2 + 1) * TB],
                            t1_sb[:, par2 * TB:(par2 + 1) * TB],
                            banks[4 + mi].ap[:, 0:TB], ALU.mult)
                    banks[4 + mi].consume("vector", evh2)
                    V_T1 = kb.cnt("v")
                    kb.wait("scalar", "v", V_T1)
                    def evh3(se, m=m, par2=par2):
                        return se.activation(h_buf[:, m * TB:(m + 1) * TB],
                                             t1_sb[:, par2 * TB:(par2 + 1) * TB],
                                             AF.Silu)
                    kb.run("scalar", "s", evh3)
            S_H = kb.cnt("s")

            for gb in range(8):
                nmn, c = banks[gb].cons
                kb.wait("tensor", nmn, c)
            kb.wait("tensor", "s", S_H)
            for k in range(HT):
                par = stream_w2(e, k)
                dq_ld.wait("tensor", w2_tok[par])
                def mmw2(te, k=k, par=par):
                    last = None
                    for j in range(8):
                        last = te.matmul(
                            banks[j].ap[:, 0:TB],
                            w2k[:, par * D + j * 128: par * D + (j + 1) * 128],
                            h_buf[:, k * TB:(k + 1) * TB],
                            start=(k == 0), stop=(k == HT - 1))
                    return last
                kb.run("tensor", "t", mmw2)
                w2_use[par] = kb.cnt("t")
            for j in range(8):
                banks[j].prod = kb.cnt("t")
            kb.wait("vector", "t")
            if shared:
                def evy(ve):
                    last = None
                    for j in range(8):
                        last = ve.tensor_tensor(
                            y_sb[:, j * TB:(j + 1) * TB],
                            y_sb[:, j * TB:(j + 1) * TB],
                            banks[j].ap[:, 0:TB], ALU.add)
                    return last
                kb.run("vector", "v", evy)
            else:
                def evy(ve):
                    last = None
                    for j in range(8):
                        ve.tensor_tensor(yt_sb[:], banks[j].ap[:, 0:TB],
                                         rw_b[:], ALU.mult)
                        last = ve.tensor_tensor(
                            y_sb[:, j * TB:(j + 1) * TB],
                            y_sb[:, j * TB:(j + 1) * TB],
                            yt_sb[:], ALU.add)
                    return last
                kb.run("vector", "v", evy)
            for j in range(8):
                banks[j].cons = ("v", kb.cnt("v"))
            if e == 0:
                bdump(4, h_buf[:, 0:2048], 2048)
                bdump(5, y_sb[:, 0:2048], 2048)
            if e == NEXP:
                bdump(6, y_sb[:, 0:2048], 2048)
        V_Y = kb.cnt("v")

        kb.wait("gpsimd", "t")
        kb.wait("gpsimd", "s")
        for g in range(8):
            dq_gl.dma(h_buf[:, g * TB:(g + 1) * TB], xp_scr[g, :, g0:g0 + TB])
        tok_xp2 = dq_gl.fence()
        dq_gl.wait("vector", tok_xp2)
        dq_gl.wait("gpsimd", tok_xp2)
        bdump(17, h_buf[:, 0:2048], 2048)
        if tb == NTB - 1:
            def patch_hbuf(ve):
                dst = bass.AP(h_buf, CA, [[HT * TB, 128], [TB, 8], [1, CA]])
                srcp = bass.AP(xp_keep, 0, [[8 * CA, 128], [CA, 8], [1, CA]])
                return ve.tensor_copy(dst, srcp)
            kb.run("vector", "v", patch_hbuf)
        def addres(ve):
            return ve.tensor_tensor(h_buf[:, 0:8 * TB], h_buf[:, 0:8 * TB],
                                    y_sb[:, 0:8 * TB], ALU.add)
        kb.run("vector", "v", addres)
        V_OUT = kb.cnt("v")
        bdump(18, h_buf[:, 0:2048], 2048)
        for tt in range(4):
            for half in range(2):
                bk = banks[6 + half]
                def tfo(te, tt=tt, half=half, bk=bk):
                    last = None
                    for q in range(4):
                        g = half * 4 + q
                        last = te.transpose(
                            bk.ap[:, q * 128:(q + 1) * 128],
                            h_buf[:, g * TB + tt * 128: g * TB + (tt + 1) * 128],
                            ident[:])
                    return last
                bk.produce(tfo, waitv=V_OUT)
                def evt(ve, tt=tt, half=half, bk=bk):
                    return ve.tensor_copy(
                        xpb[:, tt * 1024 + half * 512: tt * 1024 + (half + 1) * 512],
                        bk.ap[:, 0:512])
                bk.consume("vector", evt)
        V_TM = kb.cnt("v")
        kb.wait(_steng, "v", V_TM)
        bdump(19, xpb[:, 2048:4096], 2048)
        for tt in range(4):
            dq_st.dma(out_flat[g0 + tt * 128: g0 + (tt + 1) * 128, :],
                      xpb[:, tt * 1024:(tt + 1) * 1024])
        tok_out = dq_st.fence()
        dq_st.wait("vector", tok_out)
        tok_out_prev[0] = tok_out

    import os as _os4
    if debug and _os4.environ.get("XPDUMP"):
        for tok in dq_st.final_tokens():
            dq_st.wait("gpsimd", tok)
        for g in range(8):
            dq_st.dma(dbg["dbg_xp"][g, :, :], xp_scr[g, :, :])
        dq_st.fence()

    if debug and os.environ.get("OCOPY"):
        for tok in dq_st.final_tokens():
            dq_st.wait("gpsimd", tok)
        for tt in range(4):
            dq_st.dma(dbg["dbg_xp"][4 + tt, :, 0:1024],
                      out_flat[3584 + tt * 128: 3584 + (tt + 1) * 128, :])
        dq_st.fence()

    for dq in (dq_st, dq_ld, dq_gl):
        for tok in dq.final_tokens():
            dq.wait("gpsimd", tok)
    _edelay = int(os.environ.get("ENDDELAY", "0"))
    if _edelay:
        kb.wait("vector", "g")
        for _ in range(_edelay):
            kb.run("vector", "v", lambda ve: ve.tensor_copy(gsc[:, 0:1],
                                                            consts[:, 0:1]))
        kb.wait("gpsimd", "v", kb.cnt("v"))
    _finish(nc, kb)
    es_pb.close()
    es.close()
    return nc, dbg


def _finish(nc, kb):
    with nc.Block() as block:
        @block.sync
        def _(eng):
            for e, fn in kb.plan:
                if e == "sync":
                    fn(eng)

        @block.vector
        def _(eng):
            for e, fn in kb.plan:
                if e == "vector":
                    fn(eng)

        @block.scalar
        def _(eng):
            for e, fn in kb.plan:
                if e == "scalar":
                    fn(eng)

        @block.tensor
        def _(eng):
            for e, fn in kb.plan:
                if e == "tensor":
                    fn(eng)

        @block.gpsimd
        def _(eng):
            for e, fn in kb.plan:
                if e == "gpsimd":
                    fn(eng)


_NAME_MAP = [
    ("x", "L0_attn_x"), ("w_a", "L0_attn_w_a"), ("w_x", "L0_attn_w_x"),
    ("w_out", "L0_attn_w_out"), ("v_re", "L0_attn_mat_v_real"),
    ("v_im", "L0_attn_mat_v_imag"), ("h0_re", "L0_attn_last_hidden_real"),
    ("h0_im", "L0_attn_last_hidden_imag"), ("gate_w", "L0_ffn_gate_w"),
    ("gate_b", "L0_ffn_gate_b"), ("w1", "L0_ffn_w1"), ("b1", "L0_ffn_b1"),
    ("w3", "L0_ffn_w3"), ("b3", "L0_ffn_b3"), ("w2", "L0_ffn_w2"),
    ("b2", "L0_ffn_b2"), ("sw1", "L0_ffn_sw1"), ("sb1", "L0_ffn_sb1"),
    ("sw3", "L0_ffn_sw3"), ("sb3", "L0_ffn_sb3"), ("sw2", "L0_ffn_sw2"),
    ("sb2", "L0_ffn_sb2"),
]


def kernel(**inputs):
    nc, _ = build_program(stage="full", debug=False)
    reps = {k: np.ascontiguousarray(np.asarray(inputs[v], dtype=np.float32))
            for k, v in _NAME_MAP if k != "x"}
    xfull = np.asarray(inputs["L0_attn_x"], dtype=np.float32)
    shards = np.split(np.ascontiguousarray(xfull), NCORES, axis=0)
    in_maps = [dict(reps, x=np.ascontiguousarray(s)) for s in shards]
    res = run_bass_kernel_spmd(nc, in_maps, core_ids=list(range(NCORES)))
    return np.concatenate([r["out"] for r in res.results], axis=0)


if __name__ == "__main__":
    nc, dbg = build_program(stage="newton", debug=True)
    n = sum(1 for _ in nc.all_instructions())
    print("built newton stage, instructions:", n)


# version 08
# speedup vs baseline: 1.0031x; 1.0031x over previous
import math
import os
from contextlib import ExitStack

import numpy as np
import concourse.bass as bass
import concourse.mybir as mybir
from concourse.bass_utils import run_bass_kernel_spmd

F32 = mybir.dt.float32
F32R = mybir.dt.float32r
AF = mybir.ActivationFunctionType
ALU = mybir.AluOpType
AXX = mybir.AxisListType.X

B, L, D = 16, 2048, 1024
INNER = 512
HDIM = 1408
NEXP = 8
NCORES = 8
BPC = B // NCORES
TPC = BPC * L
CA = 256
NCHS = L // CA
TB = 512
NTB = TPC // TB
EPS = 1e-5
NIT_R = 36
NIT_F = 6
PA_LIMIT = 10**9
WSCR_PARTS = 3
HT = HDIM // 128

LN_INV102 = float(-math.log(1.02))

CONST_VALS = [1.0, 2.0, -1.0, 0.5, -0.5, 2.0 * math.pi, math.pi,
              -2.0 * math.pi, EPS, 1e-30, 1.0 / D, -1e30, LN_INV102, 0.0]
CIDX = {v: i for i, v in enumerate(CONST_VALS)}


class KB:

    def __init__(self, nc, es):
        self.nc = nc
        self.plan = []
        self.sem = {}
        for nm, step in [("t", 1), ("v", 1), ("s", 1), ("g", 1), ("y", 1),
                         ("sy", 1)]:
            h = es.enter_context(nc.semaphore("sem_" + nm))
            self.sem[nm] = [h, step, 0]

    def emit(self, eng, fn):
        self.plan.append((eng, fn))

    def cnt(self, nm):
        return self.sem[nm][2]

    def wait(self, eng, nm, val=None):
        s = self.sem[nm]
        v = s[2] if val is None else val
        if v > 0:
            self.emit(eng, lambda e, h=s[0], v=v: e.wait_ge(h, v))

    def run(self, eng, nm, fn):
        s = self.sem[nm]
        s[2] += s[1]
        def wrapped(e, fn=fn, h=s[0], step=s[1]):
            ins = fn(e)
            assert ins is not None, "run() closure must return an instruction"
            ins.then_inc(h, step)
        self.emit(eng, wrapped)


class DQ:

    NSEM = 4

    def __init__(self, kb, es, name, eng):
        self.kb = kb
        self.eng = eng
        self.name = name
        self.sems = [es.enter_context(kb.nc.semaphore(f"dq_{name}{i}"))
                     for i in range(self.NSEM)]
        self.count = [0] * self.NSEM
        self.pend = [None] * self.NSEM
        self.cur = 0
        self.dirty = False

    def dma(self, out_ap, in_ap):
        i = self.cur
        if not self.dirty and self.pend[i] is not None:
            tgt = self.pend[i]
            self.kb.emit(self.eng,
                         lambda e, h=self.sems[i], v=tgt: e.wait_ge(h, v))
            self.pend[i] = None
        self.dirty = True
        self.count[i] += 16
        h = self.sems[i]
        self.kb.emit(self.eng, lambda e, o=out_ap, a=in_ap, h=h:
                     e.dma_start(out=o, in_=a).then_inc(h, 16))

    def fence(self):
        assert self.dirty, "fence with no dmas"
        i = self.cur
        tok = (self.sems[i], self.count[i])
        self.pend[i] = self.count[i]
        self.cur = (i + 1) % self.NSEM
        self.dirty = False
        return tok

    def wait(self, eng, tok):
        h, v = tok
        self.kb.emit(eng, lambda e, h=h, v=v: e.wait_ge(h, v))

    def final_tokens(self):
        out = []
        for i in range(self.NSEM):
            if self.count[i]:
                out.append((self.sems[i], self.count[i]))
        return out


class Bank:

    def __init__(self, kb, ap, idx):
        self.kb = kb
        self.ap = ap
        self.idx = idx
        self.prod = 0
        self.cons = ("v", 0)

    def produce(self, fn, waitv=None):
        kb = self.kb
        nm, c = self.cons
        kb.wait("tensor", nm, c)
        if waitv is not None:
            kb.wait("tensor", "v", waitv)
        kb.run("tensor", "t", fn)
        self.prod = kb.cnt("t")

    def consume(self, eng, fn, guard=None):
        kb = self.kb
        kb.wait(eng, "t", self.prod)
        nm = {"vector": "v", "scalar": "s", "gpsimd": "g"}[eng]
        if guard is not None:
            ap = self.ap
            if eng == "vector":
                kb.run(eng, nm,
                       lambda ve, ap=ap, g=guard: ve.tensor_copy(g, ap[:, 0:1]))
            else:
                kb.run(eng, nm,
                       lambda se, ap=ap, g=guard: se.square(g, ap[:, 0:1]))
            kb.wait(eng, nm, kb.cnt(nm))
        kb.run(eng, nm, fn)
        self.cons = (nm, kb.cnt(nm))


BARRIER_ENGS = ("tensor", "vector", "scalar", "gpsimd", "sync")


def barrier(kb):
    targets = {}
    for eng, nm in (("vector", "v"), ("scalar", "s"), ("gpsimd", "g"),
                    ("sync", "sy")):
        def bump(e, eng=eng):
            if eng in ("vector", "gpsimd"):
                return e.engine_nop()
            return e.nop()
        kb.run(eng, nm, bump)
        targets[eng] = (nm, kb.cnt(nm))
    targets["tensor"] = ("t", kb.cnt("t"))
    for eng in ("tensor", "vector", "scalar", "gpsimd", "sync"):
        for other, (nm, v) in targets.items():
            if other != eng:
                kb.wait(eng, nm, v)


def build_program(stage="full", debug=False):
    nc = bass.Bass("TRN2", target_bir_lowering=False,
                   detect_race_conditions=bool(
                       os.environ.get("RACEDET")))
    es = ExitStack()
    kb = KB(nc, es)

    def din(name, shape):
        return nc.dram_tensor(name, shape, F32, kind="ExternalInput").ap()

    x_in = din("x", [BPC, L, D])
    w_a_h = din("w_a", [2 * INNER, D])
    w_x_h = din("w_x", [2 * INNER, D])
    w_out_h = din("w_out", [D, INNER])
    v_re_h = din("v_re", [INNER, INNER])
    v_im_h = din("v_im", [INNER, INNER])
    h0_re_h = din("h0_re", [INNER])
    h0_im_h = din("h0_im", [INNER])
    gate_w_h = din("gate_w", [NEXP, D])
    gate_b_h = din("gate_b", [NEXP])
    w1_h = din("w1", [NEXP, D, HDIM])
    b1_h = din("b1", [NEXP, HDIM])
    w3_h = din("w3", [NEXP, D, HDIM])
    b3_h = din("b3", [NEXP, HDIM])
    w2_h = din("w2", [NEXP, HDIM, D])
    b2_h = din("b2", [NEXP, D])
    sw1_h = din("sw1", [D, HDIM])
    sb1_h = din("sb1", [HDIM])
    sw3_h = din("sw3", [D, HDIM])
    sb3_h = din("sb3", [HDIM])
    sw2_h = din("sw2", [HDIM, D])
    sb2_h = din("sb2", [D])

    out_h = nc.dram_tensor("out", [BPC, L, D], F32, kind="ExternalOutput").ap()
    out_flat = out_h.rearrange("b l d -> (b l) d")
    x_flat = x_in.rearrange("b l d -> (b l) d")

    if os.environ.get("XPOUT"):
        xp_scr = nc.dram_tensor("xp_scr", [8, 128, TPC], F32,
                                kind="ExternalOutput").ap()
    else:
        xp_scr = nc.dram_tensor("xp_scr", [8, 128, TPC], F32).ap()
    waT_scr = nc.dram_tensor("waT_scr", [8, 128, 2 * INNER], F32).ap()
    wxT_scr = nc.dram_tensor("wxT_scr", [8, 128, 2 * INNER], F32).ap()
    woT_scr = nc.dram_tensor("woT_scr", [4, 128, D], F32).ap()
    sac_scr = nc.dram_tensor("sac_scr", [128, 256], F32).ap()

    dbg = {}
    if debug:
        for nm_, shp in [("dbg_xinv_re", [4, 128, INNER]),
                         ("dbg_xinv_im", [4, 128, INNER]),
                         ("dbg_waT", [9, 128, 2 * INNER]),
                         ("dbg_pl", [20, 128, 2048]),
                         ("dbg_xp", [8, 128, TPC]),
                         ("dbg_it", [NIT_R + NIT_F + 6, 128, 2048])]:
            dbg[nm_] = nc.dram_tensor(nm_, shp, F32, kind="ExternalOutput").ap()

    def sb(name, p, f, dtype=F32):
        return es.enter_context(nc.sbuf_tensor(name, [p, f], dtype))

    ident = sb("ident", 128, 128)
    gsc = sb("gsc", 128, 2)
    xp_keep = sb("xp_keep", 128, 8 * CA)
    ones_row = sb("ones_row", 1, 1024)
    consts = sb("consts", 128, 16)
    crow = sb("crow", 1, 16)
    h0_sb = sb("h0_sb", 128, 8)
    gwT = sb("gwT", 128, 8 * NEXP)
    gbb = sb("gbb", 128, NEXP)
    gb8 = sb("gb8", 1, 8)

    g_stg = sb("g_stg", NEXP, D)
    es_vx = ExitStack()
    def sbx(name, p, f):
        return es_vx.enter_context(nc.sbuf_tensor(name, [p, f], F32))
    x_re = sbx("x_re", 128, 4 * INNER)
    x_im = sbx("x_im", 128, 4 * INNER)
    x_imn = sbx("x_imn", 128, 4 * INNER)
    vT_re = sbx("vT_re", 128, 4 * INNER)
    vT_imn = sbx("vT_imn", 128, 4 * INNER)
    es_va = ExitStack()
    ii_a = es_va.enter_context(nc.sbuf_tensor("ii_a", [128, 128], F32))
    ii_b = es_va.enter_context(nc.sbuf_tensor("ii_b", [128, 128], F32))
    vA_re = es_va.enter_context(nc.sbuf_tensor("vA_re", [128, 4 * INNER], F32))
    vA_im = es_va.enter_context(nc.sbuf_tensor("vA_im", [128, 4 * INNER], F32))
    vA_imn = es_va.enter_context(nc.sbuf_tensor("vA_imn", [128, 4 * INNER], F32))

    psb = [es.enter_context(nc.psum_tensor(f"ps{i}", [128, 512], F32))
           for i in range(8)]
    banks = [Bank(kb, psb[i], i) for i in range(8)]
    dq_ld = DQ(kb, es, "ld", "sync")
    dq_gl = DQ(kb, es, "gl", "gpsimd")
    _steng = "sync" if os.environ.get("STSYNC") else "gpsimd"
    dq_st = DQ(kb, es, "st", _steng)

    def ccol(v):
        return consts[:, CIDX[v]:CIDX[v] + 1]

    def crow_ap(v):
        return crow[0:1, CIDX[v]:CIDX[v] + 1]

    def tile(plane, W, r, c0=None, c1=None):
        if c0 is None:
            return plane[:, r * W:(r + 1) * W]
        return plane[:, r * W + c0: r * W + c1]

    def ms(ap, val):
        kb.run("vector", "v", lambda ve, a=ap, x=float(val): ve.memset(a, x))

    ms(ones_row[:], 1.0)
    for j, val in enumerate(CONST_VALS):
        ms(consts[:, j:j + 1], val)
        ms(crow[0:1, j:j + 1], val)
    kb.run("gpsimd", "g", lambda e: e.iota(ii_a[:], [[1, 128]], base=0,
                                           channel_multiplier=0,
                                           allow_small_or_imprecise_dtypes=True))
    kb.run("gpsimd", "g", lambda e: e.iota(ii_b[:], [[0, 128]], base=0,
                                           channel_multiplier=1,
                                           allow_small_or_imprecise_dtypes=True))
    kb.wait("vector", "g")
    kb.run("vector", "v",
           lambda ve: ve.tensor_tensor(ident[:], ii_a[:], ii_b[:], ALU.is_equal))
    V_CONSTS = kb.cnt("v")
    for eng in ("tensor", "scalar", "gpsimd", "sync"):
        kb.wait(eng, "v", V_CONSTS)
    kb.wait("vector", "v", V_CONSTS)

    dq_gl.dma(gb8[0:1, 0:NEXP], gate_b_h.unsqueeze(0))
    for g in range(4):
        dq_gl.dma(h0_sb[:, g:g + 1], h0_re_h[g * 128:(g + 1) * 128].unsqueeze(1))
        dq_gl.dma(h0_sb[:, 4 + g:5 + g],
                  h0_im_h[g * 128:(g + 1) * 128].unsqueeze(1))
    tok_small = dq_gl.fence()

    for g in range(4):
        dq_ld.dma(tile(vA_re, INNER, g), v_re_h[g * 128:(g + 1) * 128, :])
        dq_ld.dma(tile(vA_im, INNER, g), v_im_h[g * 128:(g + 1) * 128, :])
    tok_v = dq_ld.fence()
    dq_ld.wait("vector", tok_v)
    kb.run("vector", "v", lambda ve: ve.tensor_scalar_mul(vA_imn[:], vA_im[:], -1.0))
    V_VA = kb.cnt("v")

    dq_ld.wait("tensor", tok_v)

    def emit_transpose(src_fn, dst_fn, nrt, nct, negate=False, waitv=None):
        for c in range(nct):
            for r0 in range(0, nrt, 4):
                grp = list(range(r0, min(r0 + 4, nrt)))
                bk = banks[6 + ((c * nrt + r0) // 4) % 2]
                def tf(te, c=c, grp=grp, bk=bk):
                    last = None
                    for k, r in enumerate(grp):
                        last = te.transpose(bk.ap[:, k * 128:(k + 1) * 128],
                                            src_fn(r, c), ident[:])
                    return last
                bk.produce(tf, waitv=waitv)
                if negate:
                    def ev(ve, c=c, grp=grp, bk=bk):
                        return ve.tensor_scalar_mul(
                            dst_fn(c, grp[0], len(grp)),
                            bk.ap[:, 0:len(grp) * 128], -1.0)
                else:
                    def ev(ve, c=c, grp=grp, bk=bk):
                        return ve.tensor_copy(dst_fn(c, grp[0], len(grp)),
                                              bk.ap[:, 0:len(grp) * 128])
                bk.consume("vector", ev, guard=gsc[:, 0:1])

    def src_plane(plane, W):
        return lambda r, c: plane[:, r * W + c * 128: r * W + c * 128 + 128]

    def dst_plane(plane, W):
        return lambda c, r0, n: plane[:, c * W + r0 * 128: c * W + (r0 + n) * 128]

    emit_transpose(src_plane(vA_re, INNER), dst_plane(vT_re, INNER), 4, 4,
                   waitv=V_CONSTS)
    emit_transpose(src_plane(vA_im, INNER), dst_plane(vT_imn, INNER), 4, 4,
                   negate=True)

    dq_ld.dma(g_stg[:, :], gate_w_h[:, :])
    tok_gw = dq_ld.fence()
    dq_ld.wait("tensor", tok_gw)
    for c in range(8):
        bk = banks[6]
        def tg(te, c=c, bk=bk):
            return te.transpose(bk.ap[:, 0:NEXP],
                                g_stg[:, c * 128:(c + 1) * 128],
                                ident[0:NEXP, 0:NEXP])
        bk.produce(tg)
        def eg(ve, c=c, bk=bk):
            return ve.tensor_copy(gwT[:, c * NEXP:(c + 1) * NEXP],
                                  bk.ap[:, 0:NEXP])
        bk.consume("vector", eg)

    dq_gl.wait("tensor", tok_small)
    bk = banks[7]
    def bgb(te, bk=bk):
        return te.matmul(bk.ap[:, 0:NEXP], ones_row[0:1, 0:128],
                         gb8[0:1, 0:NEXP], start=True, stop=True)
    bk.produce(bgb)
    bk.consume("vector",
               lambda ve, bk=bk: ve.tensor_copy(gbb[:], bk.ap[:, 0:NEXP]))

    es_n = ExitStack()
    def sbn(name, p, f, dtype=F32):
        return es_n.enter_context(nc.sbuf_tensor(name, [p, f], dtype))
    vr_re = sbn("vr_re", 128, 4 * INNER, F32R)
    vr_im = sbn("vr_im", 128, 4 * INNER, F32R)
    vr_imn = sbn("vr_imn", 128, 4 * INNER, F32R)
    xr_re = sbn("xr_re", 128, 4 * INNER, F32R)
    xr_im = sbn("xr_im", 128, 4 * INNER, F32R)
    xtr_re = sbn("xtr_re", 128, 4 * INNER, F32R)
    xtr_im = sbn("xtr_im", 128, 4 * INNER, F32R)
    xtr_imn = sbn("xtr_imn", 128, 4 * INNER, F32R)
    pr_re = sbn("pr_re", 128, 4 * INNER, F32R)
    pr_im = sbn("pr_im", 128, 4 * INNER, F32R)
    xt32_re = sbn("xt32_re", 128, 4 * INNER)
    xt32_im = sbn("xt32_im", 128, 4 * INNER)
    xt32_imn = sbn("xt32_imn", 128, 4 * INNER)
    p32_re = sbn("p32_re", 128, 4 * INNER)
    p32_im = sbn("p32_im", 128, 4 * INNER)
    nsc = sbn("nsc", 128, 4)
    ones128 = sbn("ones128", 128, 128)

    kb.wait("vector", "v", V_VA)
    def cast_vr(ve):
        ve.tensor_copy(vr_re[:], vA_re[:])
        ve.tensor_copy(vr_im[:], vA_im[:])
        return ve.tensor_copy(vr_imn[:], vA_imn[:])
    kb.run("vector", "v", cast_vr)
    V_VR = kb.cnt("v")

    kb.run("vector", "v", lambda ve: ve.memset(nsc[:, 0:3], 0.0))
    V_NSC0 = kb.cnt("v")
    kb.wait("vector", "v", V_NSC0)
    V_red = None
    for mi in range(4):
        for comp in (0, 1):
            bk = banks[(2 * mi + comp) % 4]
            if comp == 0:
                pairs = [(vr_re, vr_re), (vr_im, vr_im)]
            else:
                pairs = [(vr_im, vr_re), (vr_re, vr_imn)]
            def fg(te, mi=mi, pairs=pairs, bk=bk):
                last = None
                first = True
                for lhs, rhs in pairs:
                    for k in range(4):
                        last = te.matmul(
                            bk.ap[:, 0:INNER],
                            lhs[:, k * INNER + mi * 128: k * INNER + mi * 128 + 128],
                            rhs[:, k * INNER:(k + 1) * INNER],
                            start=first, stop=(lhs is pairs[1][0] and k == 3))
                        first = False
                return last
            bk.produce(fg, waitv=V_VR if mi == 0 and comp == 0 else None)
            sl = ((2 * mi + comp) % 4) * INNER
            if V_red is not None:
                kb.wait("scalar", "v", V_red)
            def evsq(se, bk=bk, sl=sl):
                return se.square(p32_re[:, sl:sl + INNER], bk.ap[:, 0:INNER])
            bk.consume("scalar", evsq, guard=gsc[:, 1:2])
            S_GSQ = kb.cnt("s")
            kb.wait("vector", "s", S_GSQ)
            kb.run("vector", "v",
                   lambda ve, sl=sl: ve.reduce_sum(nsc[:, 2:3],
                                                   p32_re[:, sl:sl + INNER],
                                                   axis=AXX))
            kb.wait("vector", "v")
            kb.run("vector", "v",
                   lambda ve: ve.tensor_add(nsc[:, 0:1], nsc[:, 0:1],
                                            nsc[:, 2:3]))
            V_red = kb.cnt("v")
    V_GRAM = kb.cnt("v")

    kb.run("vector", "v", lambda ve: ve.memset(ones128[:], 1.0))
    V_ONES128 = kb.cnt("v")
    bk = banks[4]
    bk.produce(lambda te, bk=bk: te.matmul(bk.ap[:, 0:1], ones128[:],
                                           nsc[:, 0:1], start=True, stop=True),
               waitv=max(V_GRAM, V_ONES128))
    def alpha_fn(se, bk=bk):
        return se.activation(nsc[:, 2:3], bk.ap[:, 0:1], AF.Ln)
    bk.consume("scalar", alpha_fn, guard=gsc[:, 1:2])
    kb.wait("scalar", "s")
    kb.run("scalar", "s",
           lambda se: se.activation(nsc[:, 3:4], nsc[:, 2:3], AF.Exp,
                                    scale=ccol(-0.5),
                                    bias=ccol(LN_INV102)))
    kb.wait("scalar", "s", kb.cnt("s"))
    kb.run("scalar", "s", lambda se: se.square(gsc[:, 1:2], nsc[:, 3:4]))
    S_ALPHA = kb.cnt("s")
    kb.wait("scalar", "s", S_ALPHA)
    kb.run("scalar", "s", lambda se: se.square(gsc[:, 1:2], nsc[:, 3:4]))

    import os as _os
    if debug and not _os.environ.get("NODUMP"):
        kb.wait(_steng, "s", S_ALPHA)
        dq_st.dma(dbg["dbg_waT"][8, :, 0:4], nsc[:, 0:4])
        dq_st.fence()

    kb.wait("vector", "s", S_ALPHA)
    def x0(ve):
        al = bass.AP(nsc, 3, [[4, 128], [0, 4 * INNER]])
        ve.tensor_tensor(x_re[:], vA_re[:], al, ALU.mult)
        return ve.tensor_tensor(x_im[:], vA_imn[:], al, ALU.mult)
    kb.run("vector", "v", x0)
    kb.wait("vector", "v", kb.cnt("v"))

    def ndump(idx, plane):
        import os
        if os.environ.get("NODUMP"):
            return
        if not (debug and stage == "newton"):
            return
        kb.wait(_steng, "v")
        kb.wait(_steng, "s")
        w = plane.shape[-1]
        if w > 1024:
            dq_st.dma(dbg["dbg_waT"][idx, :, :], plane[:, 0:1024])
            dq_st.dma(dbg["dbg_waT"][idx + 1, :, :], plane[:, 1024:2048])
        else:
            dq_st.dma(dbg["dbg_waT"][idx, :, 0:w], plane)
        tok = dq_st.fence()
        dq_st.wait("vector", tok)
        dq_st.wait("scalar", tok)

    ndump(0, x_re[:])
    ndump(2, nsc[:, 0:4])

    it_counter = [0]

    def newton_iter(f32r_phase):
        if f32r_phase:
            a_re, a_im, a_imn = vr_re, vr_im, vr_imn
            tt_re, tt_im, tt_imn = xtr_re, xtr_im, xtr_imn
            pp_re, pp_im = pr_re, pr_im
        else:
            a_re, a_im, a_imn = vA_re, vA_im, vA_imn
            tt_re, tt_im, tt_imn = xt32_re, xt32_im, xt32_imn
            pp_re, pp_im = p32_re, p32_im

        if f32r_phase:
            kb.wait("vector", "v", kb.cnt("v"))
            def cx(ve):
                ve.tensor_copy(xr_re[:], x_re[:])
                return ve.tensor_copy(xr_im[:], x_im[:])
            kb.run("vector", "v", cx)
            xx_re, xx_im = xr_re, xr_im
        else:
            xx_re, xx_im = x_re, x_im
        V_X = kb.cnt("v")
        import os as _os2
        first_dbg = (debug and it_counter[0] == 0 and f32r_phase
                     and not _os2.environ.get("NODUMP"))
        if first_dbg:
            kb.wait(_steng, "v", V_X)
            dq_st.dma(dbg["dbg_it"][40, :, :], x_re[:])
            dq_st.dma(dbg["dbg_it"][44, :, :], xr_re[:])
            dq_st.fence()

        for g in range(4):
            bk = banks[6 + g % 2]
            def tf(te, g=g, bk=bk):
                last = None
                for r in range(4):
                    last = te.transpose(
                        bk.ap[:, r * 128:(r + 1) * 128],
                        x_re[:, r * INNER + g * 128: r * INNER + g * 128 + 128],
                        ident[:])
                return last
            bk.produce(tf, waitv=V_X)
            def ev(ve, g=g, bk=bk, tt_re=tt_re):
                return ve.tensor_copy(tile(tt_re, INNER, g), bk.ap[:, 0:INNER])
            bk.consume("vector", ev, guard=gsc[:, 0:1])
        for g in range(4):
            bk = banks[6 + g % 2]
            def tf(te, g=g, bk=bk):
                last = None
                for r in range(4):
                    last = te.transpose(
                        bk.ap[:, r * 128:(r + 1) * 128],
                        x_im[:, r * INNER + g * 128: r * INNER + g * 128 + 128],
                        ident[:])
                return last
            bk.produce(tf)
            def ev(ve, g=g, bk=bk, tt_im=tt_im, tt_imn=tt_imn):
                ve.tensor_copy(tile(tt_im, INNER, g), bk.ap[:, 0:INNER])
                return ve.tensor_scalar_mul(tile(tt_imn, INNER, g),
                                            bk.ap[:, 0:INNER], -1.0)
            bk.consume("vector", ev, guard=gsc[:, 0:1])
        V_TT = kb.cnt("v")
        if first_dbg:
            kb.wait(_steng, "v", V_TT)
            dq_st.dma(dbg["dbg_it"][41, :, :], x_re[:])
            dq_st.dma(dbg["dbg_it"][43, :, :], tt_re[:])
            dq_st.fence()

        for mi in range(4):
            for comp in (0, 1):
                bk = banks[(2 * mi + comp) % 4]
                if comp == 0:
                    pairs = [(a_re, xx_re), (a_imn, xx_im)]
                    dst = pp_re
                else:
                    pairs = [(a_im, xx_re), (a_re, xx_im)]
                    dst = pp_im
                def fp(te, mi=mi, pairs=pairs, bk=bk):
                    last = None
                    first = True
                    for pi, (lhs, rhs) in enumerate(pairs):
                        for k in range(4):
                            last = te.matmul(
                                bk.ap[:, 0:INNER],
                                lhs[:, k * INNER + mi * 128:
                                    k * INNER + mi * 128 + 128],
                                rhs[:, k * INNER:(k + 1) * INNER],
                                start=first, stop=(pi == 1 and k == 3))
                            first = False
                    return last
                bk.produce(fp, waitv=V_X)
                def ev(ve, mi=mi, bk=bk, dst=dst):
                    return ve.tensor_copy(tile(dst, INNER, mi),
                                          bk.ap[:, 0:INNER])
                bk.consume("vector", ev, guard=gsc[:, 0:1])
        V_P = kb.cnt("v")
        if first_dbg:
            kb.wait(_steng, "v", V_P)
            dq_st.dma(dbg["dbg_it"][42, :, :], pp_re[:])
            dq_st.dma(dbg["dbg_it"][45, :, :], x_re[:])
            dq_st.fence()
        it_counter[0] += 1

        for mi in range(4):
            for comp in (0, 1):
                bk = banks[(2 * mi + comp) % 4]
                if comp == 0:
                    pairs = [(tt_re, pp_re), (tt_imn, pp_im)]
                    dst = x_re
                else:
                    pairs = [(tt_im, pp_re), (tt_re, pp_im)]
                    dst = x_im
                def fq(te, mi=mi, pairs=pairs, bk=bk):
                    last = None
                    first = True
                    for pi, (lhs, rhs) in enumerate(pairs):
                        for k in range(4):
                            last = te.matmul(
                                bk.ap[:, 0:INNER],
                                lhs[:, k * INNER + mi * 128:
                                    k * INNER + mi * 128 + 128],
                                rhs[:, k * INNER:(k + 1) * INNER],
                                start=first, stop=(pi == 1 and k == 3))
                            first = False
                    return last
                bk.produce(fq, waitv=max(V_P, V_TT))
                def ev(ve, mi=mi, bk=bk, dst=dst):
                    return ve.scalar_tensor_tensor(
                        tile(dst, INNER, mi), tile(dst, INNER, mi), 2.0,
                        bk.ap[:, 0:INNER], ALU.mult, ALU.subtract)
                bk.consume("vector", ev, guard=gsc[:, 0:1])

    def itdump(i):
        import os
        if (not debug) or os.environ.get("NODUMP"):
            return
        dq_st.dma(dbg["dbg_it"][i, :, :], x_re[:])
        dq_st.fence()

    for _it in range(NIT_R):
        newton_iter(True)
        itdump(_it)
        if _it == 0:
            ndump(4, x_re[:])
            ndump(6, p32_re[:])
    for _it in range(NIT_F):
        newton_iter(False)
        itdump(NIT_R + _it)

    kb.wait("vector", "v", kb.cnt("v"))
    kb.run("vector", "v",
           lambda ve: ve.tensor_scalar_mul(x_imn[:], x_im[:], -1.0))
    V_NEWTON = kb.cnt("v")

    if debug:
        kb.wait(_steng, "v", V_NEWTON)
        kb.wait(_steng, "s", S_ALPHA)
        for g in range(4):
            dq_st.dma(dbg["dbg_xinv_re"][g, :, :], tile(x_re, INNER, g))
            dq_st.dma(dbg["dbg_xinv_im"][g, :, :], tile(x_im, INNER, g))
        dq_st.fence()

    if stage == "newton":
        for tok in dq_st.final_tokens():
            dq_st.wait("gpsimd", tok)
        _finish(nc, kb)
        es_n.close()
        es_va.close()
        es_vx.close()
        es.close()
        return nc, dbg

    if WSCR_PARTS >= -1:
        barrier(kb)
    es_n.close()
    es_va.close()

    es_w = ExitStack()
    stg = es_w.enter_context(nc.sbuf_tensor("stg", [128, 8 * 1024], F32))
    wstg = es_w.enter_context(nc.sbuf_tensor("wstg", [128, 2 * 1024], F32))

    wstg_toks = {}

    def transpose_to_scratch(load_fn, n_src_tiles, n_dst_tiles, scr, dstw):
        if WSCR_PARTS < 0:
            return {}
        kb.wait("sync", "t")
        for r in range(n_src_tiles):
            dq_ld.dma(stg[:, r * 1024: r * 1024 + n_dst_tiles * 128], load_fn(r))
        tok = dq_ld.fence()
        if WSCR_PARTS < 1:
            return {}
        dq_ld.wait("tensor", tok)
        store_toks = wstg_toks
        for c in range(n_dst_tiles):
            par = c % 2
            if par in store_toks:
                dq_st.wait("vector", store_toks[par])
            for r0 in range(0, n_src_tiles, 4):
                grp = list(range(r0, min(r0 + 4, n_src_tiles)))
                bk = banks[6 + (r0 // 4) % 2]
                def tf(te, c=c, grp=grp, bk=bk):
                    last = None
                    for k, r in enumerate(grp):
                        last = te.transpose(
                            bk.ap[:, k * 128:(k + 1) * 128],
                            stg[:, r * 1024 + c * 128: r * 1024 + c * 128 + 128],
                            ident[:])
                    return last
                bk.produce(tf)
                def ev(ve, grp=grp, bk=bk, par=par):
                    return ve.tensor_copy(
                        wstg[:, par * 1024 + grp[0] * 128:
                             par * 1024 + (grp[0] + len(grp)) * 128],
                        bk.ap[:, 0:len(grp) * 128])
                bk.consume("vector", ev, guard=gsc[:, 0:1])
            if WSCR_PARTS < 2:
                continue
            kb.wait(_steng, "v")
            dq_st.dma(scr[c, :, 0:n_src_tiles * 128],
                      wstg[:, par * 1024: par * 1024 + n_src_tiles * 128])
            store_toks[par] = dq_st.fence()
        return store_toks


    wa2 = w_a_h.rearrange("(c two) d -> c two d", two=2)
    wx2 = w_x_h.rearrange("(c two) d -> c two d", two=2)

    def wa_src(r):
        half, i = divmod(r, 4)
        return wa2[i * 128:(i + 1) * 128, half, :]

    def wa_src_fixed(r):
        half = r // 4
        i = r % 4
        return wa2[i * 128:(i + 1) * 128, half, :]

    def wx_src(r):
        half = r // 4
        i = r % 4
        return wx2[i * 128:(i + 1) * 128, half, :]

    toks_wa = transpose_to_scratch(wa_src_fixed, 8, 8, waT_scr, None)
    if WSCR_PARTS >= 3:
        toks_wx = transpose_to_scratch(wx_src, 8, 8, wxT_scr, None)
        toks_wo = transpose_to_scratch(
            lambda r: w_out_h[r * 128:(r + 1) * 128, :], 8, 4, woT_scr, None)
    else:
        toks_wx, toks_wo = {}, {}
    for tk in list(toks_wa.values()) + list(toks_wx.values()) + list(toks_wo.values()):
        dq_st.wait("sync", tk)
    if debug:
        for c in range(8):
            dq_gl.dma(dbg["dbg_waT"][c, :, :], waT_scr[c, :, :])
        dq_gl.fence()

    if debug and stage == "wscr" and WSCR_PARTS >= 2:
        for c in range(8):
            dq_ld.dma(stg[:, 0:1024], waT_scr[c, :, :])
            tk1 = dq_ld.fence()
            dq_ld.wait("gpsimd", tk1)
            dq_gl.dma(dbg["dbg_waT"][c, :, :], stg[:, 0:1024])
            tk2 = dq_gl.fence()
            dq_gl.wait("sync", tk2)
    if stage == "wscr":
        for dq in (dq_st, dq_ld, dq_gl):
            for tok in dq.final_tokens():
                dq.wait("gpsimd", tok)
        _finish(nc, kb)
        es_w.close()
        es_vx.close()
        es.close()
        return nc, dbg

    barrier(kb)
    es_w.close()

    es_pa = ExitStack()
    def sba(name, p, f):
        return es_pa.enter_context(nc.sbuf_tensor(name, [p, f], F32))
    x_tm = sba("x_tm", 128, 2 * 1024)
    x_fm = sba("x_fm", 128, 8 * CA)
    xn = sba("xn", 128, 8 * CA)
    tmp8 = sba("tmp8", 128, 8 * CA)
    srow = sba("srow", 1, 1024)
    rstd_b = sba("rstd_b", 128, CA)
    mrs_b = sba("mrs_b", 128, CA)
    ar_p = sba("ar_p", 128, 4 * CA)
    ai_p = sba("ai_p", 128, 4 * CA)
    m_a = sba("m_a", 128, 4 * CA)
    rho = sba("rho", 128, 4 * CA)
    t4a = sba("t4a", 128, 4 * CA)
    t4b = sba("t4b", 128, 4 * CA)
    X_s = sba("X_s", 128, 4 * (CA + 1))
    kint = es_pa.enter_context(
        nc.sbuf_tensor("kint", [128, 4 * (CA + 1)], mybir.dt.int32))
    psi = sba("psi", 128, 4 * (CA + 1))
    schalf = sba("schalf", 128, 4 * (CA + 1))
    cosS = sba("cosS", 128, 4 * (CA + 1))
    sinS = sba("sinS", 128, 4 * (CA + 1))
    xcr_s = sba("xcr_s", 128, 4 * (CA + 1))
    xci_s = sba("xci_s", 128, 4 * (CA + 1))
    d_re = sba("d_re", 128, 4 * CA)
    d_im = sba("d_im", 128, 4 * CA)
    w_re = sba("w_re", 128, 4 * (CA + 1))
    w_im = sba("w_im", 128, 4 * (CA + 1))
    hre = sba("hre", 128, 4 * CA)
    onesp = sba("onesp", 128, CA)
    wa_buf = sba("wa_buf", 128, 2 * 1024)
    wx_buf = sba("wx_buf", 128, 2 * 1024)
    _wofull = os.environ.get("WOFULL", "1") != "0"
    wo_buf = sba("wo_buf", 128, 4 * 1024 if _wofull else 2 * 1024)
    if _wofull:
        for k in range(4):
            dq_ld.dma(wo_buf[:, k * 1024:(k + 1) * 1024], woT_scr[k, :, :])
        tok_wof = dq_ld.fence()
        dq_ld.wait("tensor", tok_wof)

    W1 = CA + 1

    def ap3(t, col0, tstride, nt, w, pw):
        return bass.AP(t, col0, [[pw, 128], [tstride, nt], [1, w]])

    def pl(t, nt=4, w=CA, col0=0, pw=None):
        return ap3(t, col0, w if pw is None else None, nt, w, nt * w)

    def full4(t, w=CA):
        return bass.AP(t, 0, [[4 * w, 128], [1, 4 * w]])

    def strided4(t, col0, w, tstride):
        return bass.AP(t, col0, [[4 * tstride, 128], [tstride, 4], [1, w]])

    def cbc(v, n):
        return bass.AP(consts, CIDX[v], [[16, 128], [0, n]])

    kb.run("vector", "v", lambda ve: ve.memset(onesp[:], 1.0))
    V_ONESP = kb.cnt("v")
    kb.wait("vector", "v", V_ONESP)

    tok_x = [None] * 3
    wa_use = [0, 0]
    wx_use = [0, 0]
    wo_use = [0, 0]
    wa_tok = [None, None]
    wx_tok = [None, None]
    wo_tok = [None, None]

    def stream_tile(dq, scr, k, buf, use, toks, eng="sync"):
        par = k % 2
        if use[par] > 0:
            kb.wait(eng, "t", use[par])
        dq.dma(buf[:, par * 1024:(par + 1) * 1024], scr[k, :, :])
        toks[par] = dq.fence()

    for ch in range(min(BPC * NCHS, PA_LIMIT)):
        first = (ch % NCHS) == 0
        g0 = ch * CA

        def dbgdump(idx, ap, width, ch=ch):
            _c = int(os.environ.get("CHDUMP", "0") or 0)
            if not (debug and ch == _c):
                return
            kb.wait(_steng, "v")
            kb.wait(_steng, "s")
            dq_st.dma(dbg["dbg_pl"][idx, :, 0:width], ap)
            tok = dq_st.fence()
            dq_st.wait("vector", tok)
            dq_st.wait("scalar", tok)

        for tt in range(2):
            dq_ld.dma(x_tm[:, tt * 1024:(tt + 1) * 1024],
                      x_flat[g0 + tt * 128: g0 + (tt + 1) * 128, :])
        tokx = dq_ld.fence()
        dq_ld.wait("tensor", tokx)
        _chd = os.environ.get("CHDUMP")
        if debug and _chd is not None and ch == int(_chd):
            dq_ld.wait(_steng, tokx)
            dq_st.dma(dbg["dbg_pl"][17, :, 0:2048], x_tm[:, 0:2048])
            tok_chd = dq_st.fence()
            dq_st.wait("vector", tok_chd)
        for g in range(8):
            bk = banks[6 + g % 2]
            def tf(te, g=g, bk=bk):
                last = None
                for tt in range(2):
                    last = te.transpose(
                        bk.ap[:, tt * 128:(tt + 1) * 128],
                        x_tm[:, tt * 1024 + g * 128: tt * 1024 + g * 128 + 128],
                        ident[:])
                return last
            bk.produce(tf)
            def ev(ve, g=g, bk=bk):
                return ve.tensor_copy(x_fm[:, g * CA:(g + 1) * CA],
                                      bk.ap[:, 0:CA])
            bk.consume("vector", ev, guard=gsc[:, 0:1])
        V_XFM = kb.cnt("v")
        if debug and _chd is not None and ch == int(_chd):
            kb.wait(_steng, "v", V_XFM)
            dq_st.dma(dbg["dbg_pl"][18, :, 0:2048], x_fm[:, 0:2048])
            tok_chd2 = dq_st.fence()
            dq_st.wait("vector", tok_chd2)

        kb.wait("scalar", "v", V_XFM)
        kb.run("scalar", "s", lambda se: se.square(tmp8[:], x_fm[:]))
        S_XSQ = kb.cnt("s")
        bk6, bk7 = banks[6], banks[7]
        def sums(te, bk=bk6):
            last = None
            for k in range(8):
                last = te.matmul(bk.ap[0:1, 0:CA], ccol(1.0),
                                 x_fm[:, k * CA:(k + 1) * CA],
                                 start=(k == 0), stop=(k == 7))
            return last
        bk6.produce(sums, waitv=V_XFM)
        kb.wait("tensor", "s", S_XSQ)
        def sumsq(te, bk=bk7):
            last = None
            for k in range(8):
                last = te.matmul(bk.ap[0:1, 0:CA], ccol(1.0),
                                 tmp8[:, k * CA:(k + 1) * CA],
                                 start=(k == 0), stop=(k == 7))
            return last
        bk7.produce(sumsq)

        def stats_a(ve, bk=bk6, bq=bk7):
            ve.tensor_single_scalar(srow[0:1, 0:CA], bk.ap[0:1, 0:CA],
                                    1.0 / D, ALU.mult)
            return ve.tensor_single_scalar(srow[0:1, CA:2 * CA],
                                           bq.ap[0:1, 0:CA], 1.0 / D, ALU.mult)
        kb.wait("vector", "t", bk7.prod)
        bk6.consume("vector", stats_a)
        bk7.cons = ("v", kb.cnt("v"))
        kb.wait("vector", "v")
        kb.run("vector", "v",
               lambda ve: ve.tensor_tensor(srow[0:1, 2 * CA:3 * CA],
                                           srow[0:1, 0:CA], srow[0:1, 0:CA],
                                           ALU.mult))
        kb.wait("vector", "v")
        kb.run("vector", "v",
               lambda ve: ve.tensor_tensor(srow[0:1, CA:2 * CA],
                                           srow[0:1, CA:2 * CA],
                                           srow[0:1, 2 * CA:3 * CA],
                                           ALU.subtract))
        kb.wait("scalar", "v")
        kb.run("scalar", "s",
               lambda se: se.activation(srow[0:1, 2 * CA:3 * CA],
                                        srow[0:1, CA:2 * CA], AF.Ln,
                                        bias=crow_ap(EPS)))
        kb.wait("scalar", "s")
        kb.run("scalar", "s",
               lambda se: se.activation(srow[0:1, 2 * CA:3 * CA],
                                        srow[0:1, 2 * CA:3 * CA], AF.Exp,
                                        scale=crow_ap(-0.5)))
        S_RSTD = kb.cnt("s")
        kb.wait("vector", "s", S_RSTD)
        kb.run("vector", "v",
               lambda ve: ve.tensor_tensor(srow[0:1, 3 * CA:4 * CA],
                                           srow[0:1, 0:CA],
                                           srow[0:1, 2 * CA:3 * CA], ALU.mult))
        V_STATS = kb.cnt("v")
        def bc_rstd(te, bk=bk6):
            return te.matmul(bk.ap[:, 0:CA], ones_row[0:1, 0:128],
                             srow[0:1, 2 * CA:3 * CA], start=True, stop=True)
        bk6.produce(bc_rstd, waitv=V_STATS)
        def bc_mrs(te, bk=bk7):
            return te.matmul(bk.ap[:, 0:CA], ones_row[0:1, 0:128],
                             srow[0:1, 3 * CA:4 * CA], start=True, stop=True)
        bk7.produce(bc_mrs)
        def evb(ve, bk=bk6, bq=bk7):
            ve.tensor_copy(rstd_b[:], bk.ap[:, 0:CA])
            return ve.tensor_copy(mrs_b[:], bq.ap[:, 0:CA])
        kb.wait("vector", "t", bk7.prod)
        bk6.consume("vector", evb)
        bk7.cons = ("v", kb.cnt("v"))
        kb.wait("vector", "v", kb.cnt("v"))
        def apply_ln(ve):
            rb = bass.AP(rstd_b, 0, [[CA, 128], [0, 8], [1, CA]])
            mb = bass.AP(mrs_b, 0, [[CA, 128], [0, 8], [1, CA]])
            x8 = bass.AP(x_fm, 0, [[8 * CA, 128], [CA, 8], [1, CA]])
            xn8 = bass.AP(xn, 0, [[8 * CA, 128], [CA, 8], [1, CA]])
            t8 = bass.AP(tmp8, 0, [[8 * CA, 128], [CA, 8], [1, CA]])
            ve.tensor_tensor(t8, x8, rb, ALU.mult)
            return ve.tensor_tensor(xn8, t8, mb, ALU.subtract)
        kb.run("vector", "v", apply_ln)
        V_XN = kb.cnt("v")
        dbgdump(0, xn[:], 8 * CA)

        for k in range(8):
            stream_tile(dq_ld, waT_scr, k, wa_buf, wa_use, wa_tok)
            dq_ld.wait("tensor", wa_tok[k % 2])
            if k == 0:
                for gb in range(8):
                    nm, c = banks[gb].cons
                    kb.wait("tensor", nm, c)
                kb.wait("tensor", "v", V_XN)
            def mma(te, k=k):
                par = k % 2
                last = None
                for mi in range(8):
                    last = te.matmul(
                        banks[mi].ap[:, 0:CA],
                        wa_buf[:, par * 1024 + mi * 128: par * 1024 + (mi + 1) * 128],
                        xn[:, k * CA:(k + 1) * CA],
                        start=(k == 0), stop=(k == 7))
                return last
            kb.run("tensor", "t", mma)
            wa_use[k % 2] = kb.cnt("t")
        for gb in range(8):
            banks[gb].prod = kb.cnt("t")
        def eva(ve):
            for gb in range(4):
                ve.tensor_copy(ar_p[:, gb * CA:(gb + 1) * CA],
                               banks[gb].ap[:, 0:CA])
            last = None
            for gb in range(4):
                last = ve.tensor_copy(ai_p[:, gb * CA:(gb + 1) * CA],
                                      banks[gb + 4].ap[:, 0:CA])
            return last
        kb.wait("vector", "t")
        kb.run("vector", "v", eva)
        for gb in range(8):
            banks[gb].cons = ("v", kb.cnt("v"))

        def postp_a(ve):
            ve.tensor_tensor(t4a[:], ar_p[:], ar_p[:], ALU.mult)
            ve.tensor_tensor(t4b[:], ai_p[:], ai_p[:], ALU.mult)
            return ve.tensor_tensor(m_a[:], t4a[:], t4b[:], ALU.add)
        kb.run("vector", "v", postp_a)
        V_MA = kb.cnt("v")
        dbgdump(1, ar_p[:], 4 * CA)
        dbgdump(2, ai_p[:], 4 * CA)
        kb.wait("scalar", "v", V_MA)
        def lut_a(se):
            se.activation(t4a[:], m_a[:], AF.Ln, bias=ccol(1.0))
            se.activation(t4a[:], t4a[:], AF.Exp, scale=ccol(-1.0))
            se.activation(t4b[:], m_a[:], AF.Ln)
            return se.activation(t4b[:], t4b[:], AF.Exp, scale=ccol(0.5))
        kb.run("scalar", "s", lut_a)
        S_LUTA = kb.cnt("s")
        kb.wait("vector", "s", S_LUTA)
        def rho_den(ve):
            return ve.tensor_tensor(rho[:], m_a[:], t4a[:], ALU.mult)
        kb.run("vector", "v", rho_den)
        V_DEN = kb.cnt("v")
        dbgdump(3, rho[:], 4 * CA)
        def formsel(ve):
            ve.tensor_single_scalar(kint[:, 0:4 * CA], ar_p[:], 0.0,
                                     ALU.is_ge)
            ve.tensor_tensor(d_im[:], t4b[:], ar_p[:], ALU.subtract)
            ve.tensor_tensor(t4a[:], t4b[:], ar_p[:], ALU.add)
            ve.select(d_re[:], kint[:, 0:4 * CA], ai_p[:], d_im[:])
            return ve.select(t4b[:], kint[:, 0:4 * CA], t4a[:], ai_p[:])
        kb.run("vector", "v", formsel)
        def signs(ve):
            ve.tensor_single_scalar(m_a[:], d_re[:], 0.0, ALU.is_ge)
            ve.tensor_scalar(m_a[:], m_a[:], 2.0, -1.0, ALU.mult, ALU.add)
            ve.tensor_single_scalar(d_im[:], t4b[:], 0.0, ALU.is_ge)
            ve.tensor_scalar(d_im[:], d_im[:], 2.0, -1.0, ALU.mult, ALU.add)
            return ve.tensor_tensor(m_a[:], m_a[:], d_im[:], ALU.mult)
        kb.run("vector", "v", signs)
        V_SGN = kb.cnt("v")
        dbgdump(17, m_a[:], 4 * CA)
        dbgdump(18, d_re[:], 4 * CA)
        dbgdump(19, t4b[:], 4 * CA)
        kb.wait("scalar", "v", V_SGN)
        def lut_absnd(se):
            se.activation(d_re[:], d_re[:], AF.Abs)
            return se.activation(t4b[:], t4b[:], AF.Abs)
        kb.run("scalar", "s", lut_absnd)
        S_ABSND = kb.cnt("s")
        kb.wait("vector", "s", S_ABSND)
        kb.run("vector", "v",
               lambda ve: ve.tensor_single_scalar(t4b[:], t4b[:], 1e-12,
                                                  ALU.max))
        V_CL = kb.cnt("v")
        kb.wait("scalar", "v", V_CL)
        def lut_den(se):
            se.activation(t4b[:], t4b[:], AF.Ln)
            return se.activation(t4b[:], t4b[:], AF.Exp, scale=ccol(-1.0))
        kb.run("scalar", "s", lut_den)
        S_DEN = kb.cnt("s")
        kb.wait("vector", "s", S_DEN)
        kb.run("vector", "v",
               lambda ve: ve.tensor_tensor(t4a[:], d_re[:], t4b[:], ALU.mult))
        V_RATIO = kb.cnt("v")
        kb.wait("scalar", "v", V_RATIO)
        def lut_abs(se):
            se.activation(tmp8[:, 0:4 * CA], t4a[:], AF.Ln)
            return se.activation(tmp8[:, 0:4 * CA], tmp8[:, 0:4 * CA], AF.Exp,
                                 scale=ccol(-1.0))
        kb.run("scalar", "s", lut_abs)
        S_ABS = kb.cnt("s")
        kb.wait("vector", "s", S_ABS)
        def minmask(ve):
            ve.tensor_tensor(tmp8[:, 0:4 * CA], t4a[:], tmp8[:, 0:4 * CA],
                             ALU.min)
            return ve.tensor_single_scalar(t4a[:], t4a[:], 1.0, ALU.is_gt)
        kb.run("vector", "v", minmask)
        V_MIN = kb.cnt("v")
        kb.wait("scalar", "v", V_MIN)
        kb.run("scalar", "s",
               lambda se: se.activation(tmp8[:, 0:4 * CA], tmp8[:, 0:4 * CA],
                                        AF.Arctan))
        S_AT = kb.cnt("s")
        kb.wait("vector", "s", S_AT)
        def atanfold(ve):
            ve.tensor_scalar(t4b[:], tmp8[:, 0:4 * CA], -2.0, math.pi / 2,
                             ALU.mult, ALU.add)
            ve.tensor_tensor(t4b[:], t4a[:], t4b[:], ALU.mult)
            ve.tensor_tensor(t4a[:], tmp8[:, 0:4 * CA], t4b[:], ALU.add)
            return ve.tensor_tensor(t4a[:], t4a[:], m_a[:], ALU.mult)
        kb.run("vector", "v", atanfold)
        S_ATAN = kb.cnt("s")
        dbgdump(4, t4a[:], 4 * CA)

        def xcarry(ve, first=first):
            if first:
                return ve.tensor_copy(strided4(X_s, 0, 1, W1), cbc(0.0, 4).unsqueeze(2))
            return ve.tensor_copy(strided4(X_s, 0, 1, W1),
                                  strided4(X_s, CA, 1, W1))
        kb.wait("vector", "s", S_ATAN)
        kb.run("vector", "v", xcarry)
        kb.wait("vector", "v")
        def scans_x(ve):
            last = None
            for g in range(4):
                last = ve.tensor_tensor_scan(
                    X_s[:, g * W1 + 1: g * W1 + 1 + CA],
                    onesp[:], t4a[:, g * CA:(g + 1) * CA],
                    X_s[:, g * W1: g * W1 + 1], ALU.mult, ALU.add)
            return last
        kb.run("vector", "v", scans_x)
        def fold_psi(ve):
            a_ = bass.AP(X_s, 0, [[4 * W1, 128], [1, 4 * W1]])
            p_ = bass.AP(psi, 0, [[4 * W1, 128], [1, 4 * W1]])
            t_ = bass.AP(tmp8, 0, [[8 * CA, 128], [1, 4 * W1]])
            k_ = bass.AP(kint, 0, [[4 * W1, 128], [1, 4 * W1]])
            ve.tensor_single_scalar(t_, a_, 1.0 / math.pi, ALU.mult)
            ve.tensor_copy(k_, t_)
            ve.tensor_copy(t_, k_)
            ve.scalar_tensor_tensor(p_, t_, -math.pi, a_, ALU.mult, ALU.add)
            ve.tensor_single_scalar(p_, p_, 2.0, ALU.mult)
            ve.tensor_single_scalar(t_, p_, math.pi, ALU.is_gt)
            ve.scalar_tensor_tensor(p_, t_, -2.0 * math.pi, p_, ALU.mult, ALU.add)
            ve.tensor_single_scalar(t_, p_, -math.pi, ALU.is_lt)
            return ve.scalar_tensor_tensor(p_, t_, 2.0 * math.pi, p_,
                                           ALU.mult, ALU.add)
        kb.run("vector", "v", fold_psi)
        V_PSI = kb.cnt("v")
        kb.wait("scalar", "v", V_PSI)
        def lut_trig(se):
            se.activation(sinS[:], psi[:], AF.Sin)
            se.activation(schalf[:], psi[:], AF.Sin, scale=ccol(0.5))
            return se.square(schalf[:], schalf[:])
        kb.run("scalar", "s", lut_trig)
        S_TRIG = kb.cnt("s")
        kb.wait("vector", "s", S_TRIG)
        kb.run("vector", "v",
               lambda ve: ve.tensor_scalar(cosS[:], schalf[:], -2.0, 1.0,
                                           ALU.mult, ALU.add))
        V_TRIG = kb.cnt("v")
        dbgdump(5, X_s[:], 4 * W1)
        dbgdump(6, sinS[:], 4 * W1)
        dbgdump(7, cosS[:], 4 * W1)

        for k in range(8):
            stream_tile(dq_ld, wxT_scr, k, wx_buf, wx_use, wx_tok)
            dq_ld.wait("tensor", wx_tok[k % 2])
            if k == 0:
                for gb in range(8):
                    nm, c = banks[gb].cons
                    kb.wait("tensor", nm, c)
            def mmx(te, k=k):
                par = k % 2
                last = None
                for mi in range(8):
                    last = te.matmul(
                        banks[mi].ap[:, 0:CA],
                        wx_buf[:, par * 1024 + mi * 128: par * 1024 + (mi + 1) * 128],
                        xn[:, k * CA:(k + 1) * CA],
                        start=(k == 0), stop=(k == 7))
                return last
            kb.run("tensor", "t", mmx)
            wx_use[k % 2] = kb.cnt("t")
        for gb in range(8):
            banks[gb].prod = kb.cnt("t")
        def xccarry(ve, first=first):
            if first:
                ve.tensor_copy(strided4(xcr_s, 0, 1, W1),
                               bass.AP(h0_sb, 0, [[8, 128], [1, 4]]).unsqueeze(2))
                return ve.tensor_copy(strided4(xci_s, 0, 1, W1),
                                      bass.AP(h0_sb, 4, [[8, 128], [1, 4]]).unsqueeze(2))
            ve.tensor_copy(strided4(xcr_s, 0, 1, W1), strided4(xcr_s, CA, 1, W1))
            return ve.tensor_copy(strided4(xci_s, 0, 1, W1),
                                  strided4(xci_s, CA, 1, W1))
        kb.run("vector", "v", xccarry)
        def evx(ve):
            for gb in range(4):
                ve.tensor_copy(t4a[:, gb * CA:(gb + 1) * CA],
                               banks[gb].ap[:, 0:CA])
            last = None
            for gb in range(4):
                last = ve.tensor_copy(t4b[:, gb * CA:(gb + 1) * CA],
                                      banks[gb + 4].ap[:, 0:CA])
            return last
        kb.wait("vector", "t")
        kb.run("vector", "v", evx)
        for gb in range(8):
            banks[gb].cons = ("v", kb.cnt("v"))
        def mx_fn(ve):
            ve.tensor_tensor(m_a[:], t4a[:], t4a[:], ALU.mult)
            ve.tensor_tensor(ar_p[:], t4b[:], t4b[:], ALU.mult)
            return ve.tensor_tensor(m_a[:], m_a[:], ar_p[:], ALU.add)
        kb.run("vector", "v", mx_fn)
        V_MX = kb.cnt("v")
        kb.wait("scalar", "v", V_MX)
        def lut_sqx(se):
            se.activation(ar_p[:], m_a[:], AF.Ln)
            return se.activation(ar_p[:], ar_p[:], AF.Exp, scale=ccol(0.5))
        kb.run("scalar", "s", lut_sqx)
        S_SQX = kb.cnt("s")
        kb.wait("vector", "s", S_SQX)
        def xc_fn(ve):
            ve.tensor_tensor(
                strided4(xcr_s, 1, CA, W1), pack4(t4a), pack4(ar_p), ALU.mult)
            return ve.tensor_tensor(
                strided4(xci_s, 1, CA, W1), pack4(t4b), pack4(ar_p), ALU.mult)
        def pack4(t):
            return bass.AP(t, 0, [[4 * CA, 128], [CA, 4], [1, CA]])
        kb.run("vector", "v", xc_fn)
        V_XC = kb.cnt("v")
        dbgdump(8, xcr_s[:], 4 * W1)
        dbgdump(9, xci_s[:], 4 * W1)

        for gb in range(8):
            nm, c = banks[gb].cons
            kb.wait("tensor", nm, c)
        kb.wait("tensor", "v", V_XC)
        def mmu(te):
            last = None
            for comp in range(2):
                for g in range(4):
                    bk_ap = banks[4 * comp + g].ap
                    if comp == 0:
                        pairs = [(x_re, xcr_s), (x_imn, xci_s)]
                    else:
                        pairs = [(x_im, xcr_s), (x_re, xci_s)]
                    firstmm = True
                    for pi, (lhs, rhs) in enumerate(pairs):
                        for k in range(4):
                            last = te.matmul(
                                bk_ap[:, 0:CA],
                                lhs[:, k * INNER + g * 128: k * INNER + g * 128 + 128],
                                rhs[:, k * W1: k * W1 + CA],
                                start=firstmm, stop=(pi == 1 and k == 3))
                            firstmm = False
            return last
        kb.run("tensor", "t", mmu)
        for gb in range(8):
            banks[gb].prod = kb.cnt("t")
        T_U = kb.cnt("t")

        kb.wait("vector", "t", T_U)
        def dbuild(ve):
            last = None
            for comp in range(2):
                dst = d_re if comp == 0 else d_im
                for g in range(4):
                    u_re_ap = banks[g].ap[:, 0:CA]
                    u_im_ap = banks[4 + g].ap[:, 0:CA]
                    cS = cosS[:, g * W1: g * W1 + CA]
                    sS = sinS[:, g * W1: g * W1 + CA]
                    ta = t4a[:, g * CA:(g + 1) * CA]
                    tb = t4b[:, g * CA:(g + 1) * CA]
                    if comp == 0:
                        ve.tensor_tensor(ta, cS, u_re_ap, ALU.mult)
                        ve.tensor_tensor(tb, sS, u_im_ap, ALU.mult)
                        ve.tensor_tensor(ta, ta, tb, ALU.add)
                    else:
                        ve.tensor_tensor(ta, cS, u_im_ap, ALU.mult)
                        ve.tensor_tensor(tb, sS, u_re_ap, ALU.mult)
                        ve.tensor_tensor(ta, ta, tb, ALU.subtract)
                    last = ve.tensor_tensor(dst[:, g * CA:(g + 1) * CA], ta,
                                            rho[:, g * CA:(g + 1) * CA], ALU.mult)
            return last
        kb.run("vector", "v", dbuild)
        for gb in range(8):
            banks[gb].cons = ("v", kb.cnt("v"))
        dbgdump(10, d_re[:], 4 * CA)
        dbgdump(11, d_im[:], 4 * CA)

        def wcarry(ve, first=first):
            if first:
                ve.tensor_copy(strided4(w_re, 0, 1, W1), cbc(0.0, 4).unsqueeze(2))
                return ve.tensor_copy(strided4(w_im, 0, 1, W1),
                                      cbc(0.0, 4).unsqueeze(2))
            ve.tensor_copy(strided4(w_re, 0, 1, W1), strided4(w_re, CA, 1, W1))
            return ve.tensor_copy(strided4(w_im, 0, 1, W1),
                                  strided4(w_im, CA, 1, W1))
        kb.run("vector", "v", wcarry)
        kb.wait("vector", "v")
        def scans_w(ve):
            last = None
            for g in range(4):
                last = ve.tensor_tensor_scan(
                    w_re[:, g * W1 + 1: g * W1 + 1 + CA],
                    rho[:, g * CA:(g + 1) * CA], d_re[:, g * CA:(g + 1) * CA],
                    w_re[:, g * W1: g * W1 + 1], ALU.mult, ALU.add)
            for g in range(4):
                last = ve.tensor_tensor_scan(
                    w_im[:, g * W1 + 1: g * W1 + 1 + CA],
                    rho[:, g * CA:(g + 1) * CA], d_im[:, g * CA:(g + 1) * CA],
                    w_im[:, g * W1: g * W1 + 1], ALU.mult, ALU.add)
            return last
        kb.run("vector", "v", scans_w)
        dbgdump(12, w_re[:], 4 * W1)
        dbgdump(13, w_im[:], 4 * W1)

        def unrot(ve):
            cS = strided4(cosS, 1, CA, W1)
            sS = strided4(sinS, 1, CA, W1)
            wr = strided4(w_re, 1, CA, W1)
            wi = strided4(w_im, 1, CA, W1)
            ve.tensor_tensor(pack4(t4a), cS, wr, ALU.mult)
            ve.tensor_tensor(pack4(t4b), sS, wi, ALU.mult)
            ve.tensor_tensor(pack4(d_re), pack4(t4a), pack4(t4b), ALU.subtract)
            ve.tensor_tensor(pack4(t4a), cS, wi, ALU.mult)
            ve.tensor_tensor(pack4(t4b), sS, wr, ALU.mult)
            return ve.tensor_tensor(pack4(d_im), pack4(t4a), pack4(t4b), ALU.add)
        kb.run("vector", "v", unrot)
        V_S = kb.cnt("v")
        dbgdump(14, d_re[:], 4 * CA)
        dbgdump(15, d_im[:], 4 * CA)

        for gb in range(4):
            nm, c = banks[gb].cons
            kb.wait("tensor", nm, c)
        kb.wait("tensor", "v", V_S)
        def mmvs(te):
            last = None
            for g in range(4):
                bk_ap = banks[g].ap
                firstmm = True
                for pi, (lhs, rhs) in enumerate([(vT_re, d_re), (vT_imn, d_im)]):
                    for k in range(4):
                        last = te.matmul(
                            bk_ap[:, 0:CA],
                            lhs[:, k * INNER + g * 128: k * INNER + g * 128 + 128],
                            rhs[:, k * CA:(k + 1) * CA],
                            start=firstmm, stop=(pi == 1 and k == 3))
                        firstmm = False
            return last
        kb.run("tensor", "t", mmvs)
        for gb in range(4):
            banks[gb].prod = kb.cnt("t")
        kb.wait("vector", "t")
        def evvs(ve):
            last = None
            for g in range(4):
                last = ve.tensor_tensor(
                    hre[:, g * CA:(g + 1) * CA], banks[g].ap[:, 0:CA],
                    xcr_s[:, g * W1 + 1: g * W1 + 1 + CA], ALU.add)
            return last
        kb.run("vector", "v", evvs)
        for gb in range(4):
            banks[gb].cons = ("v", kb.cnt("v"))
        V_HRE = kb.cnt("v")
        dbgdump(16, hre[:], 4 * CA)

        for k in range(4):
            if not _wofull:
                stream_tile(dq_ld, woT_scr, k, wo_buf, wo_use, wo_tok)
                dq_ld.wait("tensor", wo_tok[k % 2])
            if k == 0:
                for gb in range(8):
                    nm, c = banks[gb].cons
                    kb.wait("tensor", nm, c)
                kb.wait("tensor", "v", V_HRE)
            def mmo(te, k=k):
                par = k if _wofull else k % 2
                last = None
                for mi in range(8):
                    last = te.matmul(
                        banks[mi].ap[:, 0:CA],
                        wo_buf[:, par * 1024 + mi * 128: par * 1024 + (mi + 1) * 128],
                        hre[:, k * CA:(k + 1) * CA],
                        start=(k == 0), stop=(k == 3))
                return last
            kb.run("tensor", "t", mmo)
            wo_use[k % 2] = kb.cnt("t")
        for gb in range(8):
            banks[gb].prod = kb.cnt("t")
        if debug and _chd is not None and ch == int(_chd):
            kb.wait(_steng, "s")
            kb.wait(_steng, "v")
            kb.wait(_steng, "t")
            dq_st.dma(dbg["dbg_pl"][19, :, 0:1024], hre[:, 0:1024])
            dq_st.dma(dbg["dbg_pl"][16, :, 0:2048], x_fm[:, 0:2048])
            dq_st.dma(dbg["dbg_xp"][0, :, 2048:4096], wo_buf[:, 0:2048])
            kb.wait("vector", "t")
            def _psevac(ve):
                last = None
                for _gb in range(8):
                    last = ve.tensor_copy(tmp8[:, _gb * CA:(_gb + 1) * CA],
                                          banks[_gb].ap[:, 0:CA])
                return last
            kb.run("vector", "v", _psevac)
            kb.wait(_steng, "v", kb.cnt("v"))
            dq_st.dma(dbg["dbg_xp"][1, :, 2048:4096], tmp8[:, 0:2048])
            tok_chd3 = dq_st.fence()
            dq_st.wait("vector", tok_chd3)
        kb.wait("vector", "t")
        def evo(ve):
            last = None
            for mi in range(8):
                last = ve.tensor_tensor(
                    x_fm[:, mi * CA:(mi + 1) * CA], x_fm[:, mi * CA:(mi + 1) * CA],
                    banks[mi].ap[:, 0:CA], ALU.add)
            return last
        kb.run("vector", "v", evo)
        for gb in range(8):
            banks[gb].cons = ("v", kb.cnt("v"))
        V_XP = kb.cnt("v")
        if ch == min(BPC * NCHS, PA_LIMIT) - 1:
            kb.wait("vector", "v", V_XP)
            kb.run("vector", "v",
                   lambda ve: ve.tensor_copy(xp_keep[:], x_fm[:]))
            V_XPKEEP = kb.cnt("v")
        if debug and _chd is not None and ch == int(_chd):
            kb.wait(_steng, "v", V_XP)
            dq_st.dma(dbg["dbg_xp"][2, :, 2048:4096], x_fm[:, 0:2048])
            tok_chd4 = dq_st.fence()
            dq_st.wait("vector", tok_chd4)

        kb.wait(_steng, "v", V_XP)
        for g in range(8):
            dq_st.dma(xp_scr[g, :, g0:g0 + CA], x_fm[:, g * CA:(g + 1) * CA])
            if debug:
                dq_st.dma(dbg["dbg_xp"][g, :, g0:g0 + CA],
                          x_fm[:, g * CA:(g + 1) * CA])
        tok_xp = dq_st.fence()
        dq_st.wait("vector", tok_xp)

    if stage == "passA":
        for dq in (dq_st, dq_ld, dq_gl):
            for tok in dq.final_tokens():
                dq.wait("gpsimd", tok)
        _finish(nc, kb)
        es_pa.close()
        es_vx.close()
        es.close()
        return nc, dbg

    if os.environ.get("FAKEP"):
        for gb in range(8):
            bk = banks[gb]
            def fake(te, bk=bk):
                last = None
                for k in range(4):
                    last = te.matmul(bk.ap[:, 0:CA],
                                     wo_buf[:, k * 128:(k + 1) * 128],
                                     hre[:, 0:CA],
                                     start=(k == 0), stop=(k == 3))
                return last
            bk.produce(fake)
            def fcons(ve, bk=bk):
                return ve.tensor_copy(tmp8[:, 0:CA], bk.ap[:, 0:CA])
            bk.consume("vector", fcons)

    _tpad = int(os.environ.get("TPAD", "0"))
    for _i in range(_tpad):
        kb.run("tensor", "t",
               lambda te, _i=_i: te.matmul(banks[6 + _i % 2].ap[:, 0:128],
                                           ident[:], ident[:],
                                           start=True, stop=True))
        if _i % 4 == 3:
            kb.wait("tensor", "t", kb.cnt("t"))

    if os.environ.get("SAC"):
        for g in range(8):
            dq_st.dma(xp_scr[g, :, 0:CA], x_fm[:, g * CA:(g + 1) * CA])
        dq_st.fence()
    _nflush = int(os.environ.get("NFLUSH", "0"))
    for _ in range(_nflush):
        dq_st.dma(sac_scr[:, 0:2], gsc[:, 0:2])
    if _nflush:
        dq_st.fence()
    ndelay = int(os.environ.get("PADELAY", "0"))
    for _ in range(ndelay):
        kb.run("vector", "v",
               lambda ve: ve.tensor_copy(tmp8[:, 0:2048], xn[:, 0:2048]))
    if ndelay:
        kb.wait("vector", "v", kb.cnt("v"))
    for tok in dq_st.final_tokens():
        for _e in BARRIER_ENGS:
            dq_st.wait(_e, tok)
    es_pa.close()
    es_vx.close()

    es_pb = ExitStack()
    def sbb(name, p, f):
        return es_pb.enter_context(nc.sbuf_tensor(name, [p, f], F32))
    w1g = sbb("w1g", 128, 2 * 512)
    w3g = sbb("w3g", 128, 2 * 512)
    w2k = sbb("w2k", 128, 2 * D)
    xn2 = sbb("xn2", 128, 8 * TB)
    xpb = sbb("xpb", 128, 8 * TB)
    y_sb = sbb("y_sb", 128, 8 * TB)
    h_buf = sbb("h_buf", 128, HT * TB)
    t1_sb = sbb("t1_sb", 128, 2 * TB)
    srow2 = sbb("srow2", 1, 4 * TB)
    rstd2 = sbb("rstd2", 128, TB)
    mrs2 = sbb("mrs2", 128, TB)
    rw_tm = sbb("rw_tm", 128, 4 * NEXP)
    rws = sbb("rws", NEXP, TB)
    rw_b = sbb("rw_b", 128, TB)
    selP = sbb("selP", NEXP, NEXP * 128)
    b2all = sbb("b2all", NEXP, D)
    sb2row = sbb("sb2row", 1, D)
    b1row = sbb("b1row", 1, HDIM)
    b3row = sbb("b3row", 1, HDIM)
    rt1 = sbb("rt1", 128, 4 * NEXP)
    rt2 = sbb("rt2", 128, 4 * NEXP)
    rrow = sbb("rrow", 128, 8)
    rt3 = sbb("rt3", 128, 4 * NEXP)
    rt4 = sbb("rt4", 128, 4 * NEXP)
    yt_sb = sbb("yt_sb", 128, TB)

    _pbsetup = int(os.environ.get("PBSETUP", "3"))
    if _pbsetup & 1:
        dq_gl.dma(b2all[:, :], b2_h[:, :])
        dq_gl.dma(sb2row[0:1, :], sb2_h.unsqueeze(0))
        tok_bias = dq_gl.fence()
        dq_gl.wait("tensor", tok_bias)

    if _pbsetup & 2:
        kb.run("gpsimd", "g", lambda e: e.iota(h_buf[0:NEXP, 0:NEXP * 128],
                                               [[1, NEXP], [0, 128]], base=0,
                                               channel_multiplier=0,
                                               allow_small_or_imprecise_dtypes=True))
        kb.run("gpsimd", "g", lambda e: e.iota(h_buf[0:NEXP, 1024:1024 + NEXP * 128],
                                               [[0, NEXP * 128]], base=0,
                                               channel_multiplier=1,
                                               allow_small_or_imprecise_dtypes=True))
        kb.wait("vector", "g")
        kb.run("vector", "v",
               lambda ve: ve.tensor_tensor(selP[:, :], h_buf[0:NEXP, 0:NEXP * 128],
                                           h_buf[0:NEXP, 1024:1024 + NEXP * 128],
                                           ALU.is_equal))
        V_SEL = kb.cnt("v")
        kb.wait("tensor", "v", V_SEL)
        kb.wait("vector", "v", V_SEL)
        kb.wait("scalar", "v", V_SEL)

    w13_use = [0, 0]
    w13_tok = [None, None]
    w13_i = [0]
    w2_use = [0, 0]
    w2_tok = [None, None]
    w2_i = [0]
    bias_use = [0]

    MGRP = [(0, 4), (4, 4), (8, 3)]

    def stream_w13(e, m0, nm, k):
        par = w13_i[0] % 2
        w13_i[0] += 1
        if w13_use[par] > 0:
            kb.wait("sync", "t", w13_use[par])
        c0, c1 = m0 * 128, (m0 + nm) * 128
        if e < NEXP:
            dq_ld.dma(w1g[:, par * 512: par * 512 + (c1 - c0)],
                      w1_h[e, k * 128:(k + 1) * 128, c0:c1])
            dq_ld.dma(w3g[:, par * 512: par * 512 + (c1 - c0)],
                      w3_h[e, k * 128:(k + 1) * 128, c0:c1])
        else:
            dq_ld.dma(w1g[:, par * 512: par * 512 + (c1 - c0)],
                      sw1_h[k * 128:(k + 1) * 128, c0:c1])
            dq_ld.dma(w3g[:, par * 512: par * 512 + (c1 - c0)],
                      sw3_h[k * 128:(k + 1) * 128, c0:c1])
        w13_tok[par] = dq_ld.fence()
        return par

    def stream_w2(e, k):
        par = w2_i[0] % 2
        w2_i[0] += 1
        if w2_use[par] > 0:
            kb.wait("sync", "t", w2_use[par])
        if e < NEXP:
            dq_ld.dma(w2k[:, par * D:(par + 1) * D],
                      w2_h[e, k * 128:(k + 1) * 128, :])
        else:
            dq_ld.dma(w2k[:, par * D:(par + 1) * D],
                      sw2_h[k * 128:(k + 1) * 128, :])
        w2_tok[par] = dq_ld.fence()
        return par

    tok_out_prev = [None]
    _pblim = int(os.environ.get("PBLIM", str(10**9)))
    for tb in range(min(NTB, _pblim)):
        g0 = tb * TB
        if debug and os.environ.get("TRKDUMP"):
            kb.wait("gpsimd", "v")
            for g in range(8):
                dq_gl.dma(dbg["dbg_xp"][g, :, tb * 256:(tb + 1) * 256],
                          xp_scr[g, :, 3840:4096])
            dq_gl.fence()
        if tok_out_prev[0] is not None:
            dq_st.wait("gpsimd", tok_out_prev[0])
        for g in range(8):
            dq_gl.dma(xpb[:, g * TB:(g + 1) * TB], xp_scr[g, :, g0:g0 + TB])
        tok_xp = dq_gl.fence()
        dq_gl.wait("vector", tok_xp)
        dq_gl.wait("scalar", tok_xp)
        dq_gl.wait("tensor", tok_xp)
        if tb == NTB - 1:
            def patch_xpb(ve):
                dst = bass.AP(xpb, CA, [[8 * TB, 128], [TB, 8], [1, CA]])
                srcp = bass.AP(xp_keep, 0, [[8 * CA, 128], [CA, 8], [1, CA]])
                return ve.tensor_copy(dst, srcp)
            kb.run("vector", "v", patch_xpb)
            V_PB_PATCH = kb.cnt("v")
            kb.wait("scalar", "v", V_PB_PATCH)
            kb.wait("tensor", "v", V_PB_PATCH)

        kb.wait("scalar", "t")
        def sq2(se):
            return se.square(h_buf[:, 0:8 * TB], xpb[:])
        kb.run("scalar", "s", sq2)
        S_SQ2 = kb.cnt("s")
        bk6, bk7 = banks[6], banks[7]
        def sums2(te, bk=bk6):
            last = None
            for k in range(8):
                last = te.matmul(bk.ap[0:1, 0:TB], ccol(1.0),
                                 xpb[:, k * TB:(k + 1) * TB],
                                 start=(k == 0), stop=(k == 7))
            return last
        bk6.produce(sums2)
        kb.wait("tensor", "s", S_SQ2)
        def sumsq2(te, bk=bk7):
            last = None
            for k in range(8):
                last = te.matmul(bk.ap[0:1, 0:TB], ccol(1.0),
                                 h_buf[:, k * TB:(k + 1) * TB],
                                 start=(k == 0), stop=(k == 7))
            return last
        bk7.produce(sumsq2)
        def stats2_a(ve, bk=bk6, bq=bk7):
            ve.tensor_single_scalar(srow2[0:1, 0:TB], bk.ap[0:1, 0:TB],
                                    1.0 / D, ALU.mult)
            return ve.tensor_single_scalar(srow2[0:1, TB:2 * TB],
                                           bq.ap[0:1, 0:TB], 1.0 / D, ALU.mult)
        kb.wait("vector", "t", bk7.prod)
        bk6.consume("vector", stats2_a)
        bk7.cons = ("v", kb.cnt("v"))
        kb.wait("vector", "v")
        kb.run("vector", "v",
               lambda ve: ve.tensor_tensor(srow2[0:1, 2 * TB:3 * TB],
                                           srow2[0:1, 0:TB], srow2[0:1, 0:TB],
                                           ALU.mult))
        kb.wait("vector", "v")
        kb.run("vector", "v",
               lambda ve: ve.tensor_tensor(srow2[0:1, TB:2 * TB],
                                           srow2[0:1, TB:2 * TB],
                                           srow2[0:1, 2 * TB:3 * TB],
                                           ALU.subtract))
        kb.wait("scalar", "v")
        kb.run("scalar", "s",
               lambda se: se.activation(srow2[0:1, 2 * TB:3 * TB],
                                        srow2[0:1, TB:2 * TB], AF.Ln,
                                        bias=crow_ap(EPS)))
        kb.wait("scalar", "s")
        kb.run("scalar", "s",
               lambda se: se.activation(srow2[0:1, 2 * TB:3 * TB],
                                        srow2[0:1, 2 * TB:3 * TB], AF.Exp,
                                        scale=crow_ap(-0.5)))
        S_R2 = kb.cnt("s")
        kb.wait("vector", "s", S_R2)
        kb.run("vector", "v",
               lambda ve: ve.tensor_tensor(srow2[0:1, 3 * TB:4 * TB],
                                           srow2[0:1, 0:TB],
                                           srow2[0:1, 2 * TB:3 * TB], ALU.mult))
        V_ST2 = kb.cnt("v")
        def bcr(te, bk=bk6):
            return te.matmul(bk.ap[:, 0:TB], ones_row[0:1, 0:128],
                             srow2[0:1, 2 * TB:3 * TB], start=True, stop=True)
        bk6.produce(bcr, waitv=V_ST2)
        def bcm(te, bk=bk7):
            return te.matmul(bk.ap[:, 0:TB], ones_row[0:1, 0:128],
                             srow2[0:1, 3 * TB:4 * TB], start=True, stop=True)
        bk7.produce(bcm)
        def evb2(ve, bk=bk6, bq=bk7):
            ve.tensor_copy(rstd2[:], bk.ap[:, 0:TB])
            return ve.tensor_copy(mrs2[:], bq.ap[:, 0:TB])
        kb.wait("vector", "t", bk7.prod)
        bk6.consume("vector", evb2)
        bk7.cons = ("v", kb.cnt("v"))
        kb.wait("vector", "v", kb.cnt("v"))
        def apply_ln2(ve):
            rb = bass.AP(rstd2, 0, [[TB, 128], [0, 8], [1, TB]])
            mb = bass.AP(mrs2, 0, [[TB, 128], [0, 8], [1, TB]])
            x8 = bass.AP(xpb, 0, [[8 * TB, 128], [TB, 8], [1, TB]])
            xn8 = bass.AP(xn2, 0, [[8 * TB, 128], [TB, 8], [1, TB]])
            ve.tensor_tensor(xn8, x8, rb, ALU.mult)
            return ve.tensor_tensor(xn8, xn8, mb, ALU.subtract)
        kb.run("vector", "v", apply_ln2)
        V_XN2 = kb.cnt("v")

        def bdump(idx, ap, width, tb=tb):
            import os as _os3
            if not (debug and tb == int(_os3.environ.get("BDTB", "0"))):
                return
            kb.wait(_steng, "v")
            kb.wait(_steng, "s")
            dq_st.dma(dbg["dbg_pl"][idx, :, 0:width], ap)
            tok = dq_st.fence()
            dq_st.wait("vector", tok)
            dq_st.wait("scalar", tok)
        bdump(0, xpb[:, 0:2048], 2048)
        bdump(1, xn2[:, 0:2048], 2048)

        for tt in range(4):
            bk = banks[tt]
            def logits(te, tt=tt, bk=bk):
                last = None
                for k in range(8):
                    last = te.matmul(
                        bk.ap[:, 0:NEXP],
                        xn2[:, k * TB + tt * 128: k * TB + (tt + 1) * 128],
                        gwT[:, k * NEXP:(k + 1) * NEXP],
                        start=(k == 0), stop=(k == 7))
                return last
            bk.produce(logits, waitv=V_XN2)
        def router_m1(ve):
            last = None
            for tt in range(4):
                last = ve.reduce_max(rrow[:, tt:tt + 1],
                                     banks[tt].ap[:, 0:NEXP], axis=AXX)
            return last
        kb.wait("vector", "t")
        kb.run("vector", "v", router_m1)
        kb.wait("vector", "v")
        def router_v1(ve):
            last = None
            for tt in range(4):
                last = ve.tensor_scalar(rt1[:, tt * 8:(tt + 1) * 8],
                                        banks[tt].ap[:, 0:NEXP],
                                        rrow[:, tt:tt + 1], None, ALU.subtract)
            return last
        kb.run("vector", "v", router_v1)
        for tt in range(4):
            banks[tt].cons = ("v", kb.cnt("v"))
        bdump(7, rt1[:, 0:32], 32)
        bdump(8, rrow[:, 0:8], 8)
        kb.wait("scalar", "v")
        kb.run("scalar", "s",
               lambda se: se.activation(rt1[:], rt1[:], AF.Exp))
        S_REXP = kb.cnt("s")
        kb.wait("vector", "s", S_REXP)
        def router_v2a(ve):
            last = None
            for tt in range(4):
                last = ve.reduce_sum(rrow[:, 4 + tt:5 + tt],
                                     rt1[:, tt * 8:(tt + 1) * 8], axis=AXX)
            return last
        kb.run("vector", "v", router_v2a)
        kb.wait("vector", "v")
        kb.run("vector", "v",
               lambda ve: ve.tensor_copy(rt2[:, 0:4], rrow[:, 4:8]))
        V_RS = kb.cnt("v")
        kb.wait("scalar", "v", V_RS)
        kb.run("scalar", "s",
               lambda se: se.activation(rt2[:, 0:4], rt2[:, 0:4], AF.Ln))
        kb.wait("scalar", "s")
        kb.run("scalar", "s",
               lambda se: se.activation(rt2[:, 0:4], rt2[:, 0:4], AF.Exp,
                                        scale=ccol(-1.0)))
        S_RINV = kb.cnt("s")
        kb.wait("vector", "s", S_RINV)
        bdump(9, rt2[:, 0:8], 8)
        bdump(10, rt1[:, 0:32], 32)
        def phase(fn):
            kb.run("vector", "v", fn)
            kb.wait("vector", "v")

        def r_score(ve):
            last = None
            for tt in range(4):
                sl = slice(tt * 8, (tt + 1) * 8)
                last = ve.tensor_scalar(rt1[:, sl], rt1[:, sl],
                                        rt2[:, tt:tt + 1], None, ALU.mult)
            for tt in range(4):
                sl = slice(tt * 8, (tt + 1) * 8)
                last = ve.tensor_tensor(rw_tm[:, sl], rt1[:, sl], gbb[:],
                                        ALU.add)
            for tt in range(4):
                sl = slice(tt * 8, (tt + 1) * 8)
                last = ve.reduce_max(rrow[:, tt:tt + 1], rw_tm[:, sl], axis=AXX)
            return last
        phase(r_score)
        bdump(11, rt1[:, 0:32], 32)
        def r_eq(ve):
            last = None
            for tt in range(4):
                sl = slice(tt * 8, (tt + 1) * 8)
                last = ve.tensor_scalar(rt3[:, sl], rw_tm[:, sl],
                                        rrow[:, tt:tt + 1], None, ALU.is_equal)
            for tt in range(4):
                sl = slice(tt * 8, (tt + 1) * 8)
                last = ve.scalar_tensor_tensor(rt4[:, sl], rt3[:, sl], -1e30,
                                               rw_tm[:, sl], ALU.mult, ALU.add)
            for tt in range(4):
                sl = slice(tt * 8, (tt + 1) * 8)
                last = ve.reduce_max(rrow[:, 4 + tt:5 + tt], rt4[:, sl],
                                     axis=AXX)
            return last
        phase(r_eq)
        def r_mask(ve):
            last = None
            for tt in range(4):
                sl = slice(tt * 8, (tt + 1) * 8)
                last = ve.tensor_scalar(rt3[:, sl], rw_tm[:, sl],
                                        rrow[:, 4 + tt:5 + tt], None, ALU.is_ge)
            for tt in range(4):
                sl = slice(tt * 8, (tt + 1) * 8)
                last = ve.tensor_tensor(rw_tm[:, sl], rt1[:, sl],
                                        rt3[:, sl], ALU.mult)
            return last
        kb.run("vector", "v", r_mask)
        V_RW = kb.cnt("v")
        bdump(2, rw_tm[:, 0:32], 32)
        bdump(3, rt1[:, 0:32], 32)

        bk4 = banks[4]
        def rwt(te, bk=bk4):
            last = None
            for tt in range(4):
                last = te.transpose(bk.ap[0:NEXP, tt * 128:(tt + 1) * 128],
                                    rw_tm[:, tt * 8:(tt + 1) * 8],
                                    ident[:])
            return last
        bk4.produce(rwt, waitv=V_RW)
        bk4.consume("vector",
                    lambda ve, bk=bk4: ve.tensor_copy(rws[:, :],
                                                      bk.ap[0:NEXP, 0:TB]))
        V_RWS = kb.cnt("v")

        for j in range(8):
            bk = banks[4 + j % 2]
            def yinit(te, j=j, bk=bk):
                te.matmul(bk.ap[:, 0:TB], b2all[:, j * 128:(j + 1) * 128],
                          rws[:, :], start=True, stop=False)
                return te.matmul(bk.ap[:, 0:TB],
                                 sb2row[0:1, j * 128:(j + 1) * 128],
                                 ones_row[0:1, 0:TB], start=False, stop=True)
            bk.produce(yinit, waitv=V_RWS)
            def evy0(ve, j=j, bk=bk):
                return ve.tensor_copy(y_sb[:, j * TB:(j + 1) * TB],
                                      bk.ap[:, 0:TB])
            bk.consume("vector", evy0)

        for e in range(NEXP + 1):
            shared = (e == NEXP)
            if not shared:
                bk5 = banks[5]
                def rwb(te, e=e, bk=bk5):
                    return te.matmul(bk.ap[:, 0:TB],
                                     selP[:, e * 128:(e + 1) * 128],
                                     rws[:, :], start=True, stop=True)
                bk5.produce(rwb)
                bk5.consume("vector",
                            lambda ve, bk=bk5: ve.tensor_copy(rw_b[:],
                                                              bk.ap[:, 0:TB]))
            V_RWB = kb.cnt("v")

            kb.wait("gpsimd", "t", bias_use[0])
            if e < NEXP:
                dq_gl.dma(b1row[0:1, :], b1_h[e].unsqueeze(0))
                dq_gl.dma(b3row[0:1, :], b3_h[e].unsqueeze(0))
            else:
                dq_gl.dma(b1row[0:1, :], sb1_h.unsqueeze(0))
                dq_gl.dma(b3row[0:1, :], sb3_h.unsqueeze(0))
            tok_b = dq_gl.fence()
            dq_gl.wait("tensor", tok_b)

            for m0, nm in MGRP:
                for mi in range(nm):
                    for gb in (banks[mi], banks[4 + mi]):
                        nmn, c = gb.cons
                        kb.wait("tensor", nmn, c)
                if m0 == 0 and e == 0:
                    kb.wait("tensor", "v", V_XN2)
                for k in range(8):
                    par = stream_w13(e, m0, nm, k)
                    dq_ld.wait("tensor", w13_tok[par])
                    def mmh(te, k=k, nm=nm, par=par):
                        last = None
                        for mi in range(nm):
                            last = te.matmul(
                                banks[mi].ap[:, 0:TB],
                                w1g[:, par * 512 + mi * 128:
                                    par * 512 + (mi + 1) * 128],
                                xn2[:, k * TB:(k + 1) * TB],
                                start=(k == 0), stop=False)
                        for mi in range(nm):
                            last = te.matmul(
                                banks[4 + mi].ap[:, 0:TB],
                                w3g[:, par * 512 + mi * 128:
                                    par * 512 + (mi + 1) * 128],
                                xn2[:, k * TB:(k + 1) * TB],
                                start=(k == 0), stop=False)
                        return last
                    kb.run("tensor", "t", mmh)
                    w13_use[par] = kb.cnt("t")
                def mmbias(te, m0=m0, nm=nm):
                    last = None
                    for mi in range(nm):
                        m = m0 + mi
                        last = te.matmul(banks[mi].ap[:, 0:TB],
                                         b1row[0:1, m * 128:(m + 1) * 128],
                                         ones_row[0:1, 0:TB],
                                         start=False, stop=True)
                        last = te.matmul(banks[4 + mi].ap[:, 0:TB],
                                         b3row[0:1, m * 128:(m + 1) * 128],
                                         ones_row[0:1, 0:TB],
                                         start=False, stop=True)
                    return last
                kb.run("tensor", "t", mmbias)
                bias_use[0] = kb.cnt("t")
                for mi in range(nm):
                    banks[mi].prod = kb.cnt("t")
                    banks[4 + mi].prod = kb.cnt("t")
                for mi in range(nm):
                    m = m0 + mi
                    par2 = mi % 2
                    def evh1(se, mi=mi, par2=par2):
                        return se.copy(t1_sb[:, par2 * TB:(par2 + 1) * TB],
                                       banks[mi].ap[:, 0:TB])
                    banks[mi].consume("scalar", evh1)
                    S_T1 = kb.cnt("s")
                    kb.wait("vector", "s", S_T1)
                    def evh2(ve, mi=mi, par2=par2):
                        return ve.tensor_tensor(
                            t1_sb[:, par2 * TB:(par2 + 1) * TB],
                            t1_sb[:, par2 * TB:(par2 + 1) * TB],
                            banks[4 + mi].ap[:, 0:TB], ALU.mult)
                    banks[4 + mi].consume("vector", evh2)
                    V_T1 = kb.cnt("v")
                    kb.wait("scalar", "v", V_T1)
                    def evh3(se, m=m, par2=par2):
                        return se.activation(h_buf[:, m * TB:(m + 1) * TB],
                                             t1_sb[:, par2 * TB:(par2 + 1) * TB],
                                             AF.Silu)
                    kb.run("scalar", "s", evh3)
            S_H = kb.cnt("s")

            for gb in range(8):
                nmn, c = banks[gb].cons
                kb.wait("tensor", nmn, c)
            kb.wait("tensor", "s", S_H)
            for k in range(HT):
                par = stream_w2(e, k)
                dq_ld.wait("tensor", w2_tok[par])
                def mmw2(te, k=k, par=par):
                    last = None
                    for j in range(8):
                        last = te.matmul(
                            banks[j].ap[:, 0:TB],
                            w2k[:, par * D + j * 128: par * D + (j + 1) * 128],
                            h_buf[:, k * TB:(k + 1) * TB],
                            start=(k == 0), stop=(k == HT - 1))
                    return last
                kb.run("tensor", "t", mmw2)
                w2_use[par] = kb.cnt("t")
            for j in range(8):
                banks[j].prod = kb.cnt("t")
            kb.wait("vector", "t")
            if shared:
                def evy(ve):
                    last = None
                    for j in range(8):
                        last = ve.tensor_tensor(
                            y_sb[:, j * TB:(j + 1) * TB],
                            y_sb[:, j * TB:(j + 1) * TB],
                            banks[j].ap[:, 0:TB], ALU.add)
                    return last
                kb.run("vector", "v", evy)
            else:
                def evy(ve):
                    last = None
                    for j in range(8):
                        ve.tensor_tensor(yt_sb[:], banks[j].ap[:, 0:TB],
                                         rw_b[:], ALU.mult)
                        last = ve.tensor_tensor(
                            y_sb[:, j * TB:(j + 1) * TB],
                            y_sb[:, j * TB:(j + 1) * TB],
                            yt_sb[:], ALU.add)
                    return last
                kb.run("vector", "v", evy)
            for j in range(8):
                banks[j].cons = ("v", kb.cnt("v"))
            if e == 0:
                bdump(4, h_buf[:, 0:2048], 2048)
                bdump(5, y_sb[:, 0:2048], 2048)
            if e == NEXP:
                bdump(6, y_sb[:, 0:2048], 2048)
        V_Y = kb.cnt("v")

        kb.wait("gpsimd", "t")
        kb.wait("gpsimd", "s")
        for g in range(8):
            dq_gl.dma(h_buf[:, g * TB:(g + 1) * TB], xp_scr[g, :, g0:g0 + TB])
        tok_xp2 = dq_gl.fence()
        dq_gl.wait("vector", tok_xp2)
        dq_gl.wait("gpsimd", tok_xp2)
        bdump(17, h_buf[:, 0:2048], 2048)
        if tb == NTB - 1:
            def patch_hbuf(ve):
                dst = bass.AP(h_buf, CA, [[HT * TB, 128], [TB, 8], [1, CA]])
                srcp = bass.AP(xp_keep, 0, [[8 * CA, 128], [CA, 8], [1, CA]])
                return ve.tensor_copy(dst, srcp)
            kb.run("vector", "v", patch_hbuf)
        def addres(ve):
            return ve.tensor_tensor(h_buf[:, 0:8 * TB], h_buf[:, 0:8 * TB],
                                    y_sb[:, 0:8 * TB], ALU.add)
        kb.run("vector", "v", addres)
        V_OUT = kb.cnt("v")
        bdump(18, h_buf[:, 0:2048], 2048)
        for tt in range(4):
            for half in range(2):
                bk = banks[6 + half]
                def tfo(te, tt=tt, half=half, bk=bk):
                    last = None
                    for q in range(4):
                        g = half * 4 + q
                        last = te.transpose(
                            bk.ap[:, q * 128:(q + 1) * 128],
                            h_buf[:, g * TB + tt * 128: g * TB + (tt + 1) * 128],
                            ident[:])
                    return last
                bk.produce(tfo, waitv=V_OUT)
                def evt(ve, tt=tt, half=half, bk=bk):
                    return ve.tensor_copy(
                        xpb[:, tt * 1024 + half * 512: tt * 1024 + (half + 1) * 512],
                        bk.ap[:, 0:512])
                bk.consume("vector", evt)
        V_TM = kb.cnt("v")
        kb.wait(_steng, "v", V_TM)
        bdump(19, xpb[:, 2048:4096], 2048)
        for tt in range(4):
            dq_st.dma(out_flat[g0 + tt * 128: g0 + (tt + 1) * 128, :],
                      xpb[:, tt * 1024:(tt + 1) * 1024])
        tok_out = dq_st.fence()
        dq_st.wait("vector", tok_out)
        tok_out_prev[0] = tok_out

    import os as _os4
    if debug and _os4.environ.get("XPDUMP"):
        for tok in dq_st.final_tokens():
            dq_st.wait("gpsimd", tok)
        for g in range(8):
            dq_st.dma(dbg["dbg_xp"][g, :, :], xp_scr[g, :, :])
        dq_st.fence()

    if debug and os.environ.get("OCOPY"):
        for tok in dq_st.final_tokens():
            dq_st.wait("gpsimd", tok)
        for tt in range(4):
            dq_st.dma(dbg["dbg_xp"][4 + tt, :, 0:1024],
                      out_flat[3584 + tt * 128: 3584 + (tt + 1) * 128, :])
        dq_st.fence()

    for dq in (dq_st, dq_ld, dq_gl):
        for tok in dq.final_tokens():
            dq.wait("gpsimd", tok)
    _edelay = int(os.environ.get("ENDDELAY", "0"))
    if _edelay:
        kb.wait("vector", "g")
        for _ in range(_edelay):
            kb.run("vector", "v", lambda ve: ve.tensor_copy(gsc[:, 0:1],
                                                            consts[:, 0:1]))
        kb.wait("gpsimd", "v", kb.cnt("v"))
    _finish(nc, kb)
    es_pb.close()
    es.close()
    return nc, dbg


def _finish(nc, kb):
    with nc.Block() as block:
        @block.sync
        def _(eng):
            for e, fn in kb.plan:
                if e == "sync":
                    fn(eng)

        @block.vector
        def _(eng):
            for e, fn in kb.plan:
                if e == "vector":
                    fn(eng)

        @block.scalar
        def _(eng):
            for e, fn in kb.plan:
                if e == "scalar":
                    fn(eng)

        @block.tensor
        def _(eng):
            for e, fn in kb.plan:
                if e == "tensor":
                    fn(eng)

        @block.gpsimd
        def _(eng):
            for e, fn in kb.plan:
                if e == "gpsimd":
                    fn(eng)


_NAME_MAP = [
    ("x", "L0_attn_x"), ("w_a", "L0_attn_w_a"), ("w_x", "L0_attn_w_x"),
    ("w_out", "L0_attn_w_out"), ("v_re", "L0_attn_mat_v_real"),
    ("v_im", "L0_attn_mat_v_imag"), ("h0_re", "L0_attn_last_hidden_real"),
    ("h0_im", "L0_attn_last_hidden_imag"), ("gate_w", "L0_ffn_gate_w"),
    ("gate_b", "L0_ffn_gate_b"), ("w1", "L0_ffn_w1"), ("b1", "L0_ffn_b1"),
    ("w3", "L0_ffn_w3"), ("b3", "L0_ffn_b3"), ("w2", "L0_ffn_w2"),
    ("b2", "L0_ffn_b2"), ("sw1", "L0_ffn_sw1"), ("sb1", "L0_ffn_sb1"),
    ("sw3", "L0_ffn_sw3"), ("sb3", "L0_ffn_sb3"), ("sw2", "L0_ffn_sw2"),
    ("sb2", "L0_ffn_sb2"),
]


def kernel(**inputs):
    nc, _ = build_program(stage="full", debug=False)
    reps = {k: np.ascontiguousarray(np.asarray(inputs[v], dtype=np.float32))
            for k, v in _NAME_MAP if k != "x"}
    xfull = np.asarray(inputs["L0_attn_x"], dtype=np.float32)
    shards = np.split(np.ascontiguousarray(xfull), NCORES, axis=0)
    in_maps = [dict(reps, x=np.ascontiguousarray(s)) for s in shards]
    res = run_bass_kernel_spmd(nc, in_maps, core_ids=list(range(NCORES)))
    return np.concatenate([r["out"] for r in res.results], axis=0)


if __name__ == "__main__":
    nc, dbg = build_program(stage="newton", debug=True)
    n = sum(1 for _ in nc.all_instructions())
    print("built newton stage, instructions:", n)


# version 09 (final)
# speedup vs baseline: 1.0117x; 1.0087x over previous
import math
import os
from contextlib import ExitStack

import numpy as np
import concourse.bass as bass
import concourse.mybir as mybir
from concourse.bass_utils import run_bass_kernel_spmd

F32 = mybir.dt.float32
F32R = mybir.dt.float32r
AF = mybir.ActivationFunctionType
ALU = mybir.AluOpType
AXX = mybir.AxisListType.X

B, L, D = 16, 2048, 1024
INNER = 512
HDIM = 1408
NEXP = 8
NCORES = 8
BPC = B // NCORES
TPC = BPC * L
CA = 256
NCHS = L // CA
TB = 512
NTB = TPC // TB
EPS = 1e-5
NIT_R = 36
NIT_F = 6
PA_LIMIT = 10**9
WSCR_PARTS = 3
HT = HDIM // 128

LN_INV102 = float(-math.log(1.02))

CONST_VALS = [1.0, 2.0, -1.0, 0.5, -0.5, 2.0 * math.pi, math.pi,
              -2.0 * math.pi, EPS, 1e-30, 1.0 / D, -1e30, LN_INV102, 0.0]
CIDX = {v: i for i, v in enumerate(CONST_VALS)}


class KB:

    def __init__(self, nc, es):
        self.nc = nc
        self.plan = []
        self.sem = {}
        for nm, step in [("t", 1), ("v", 1), ("s", 1), ("g", 1), ("y", 1),
                         ("sy", 1)]:
            h = es.enter_context(nc.semaphore("sem_" + nm))
            self.sem[nm] = [h, step, 0]

    def emit(self, eng, fn):
        self.plan.append((eng, fn))

    def cnt(self, nm):
        return self.sem[nm][2]

    def wait(self, eng, nm, val=None):
        s = self.sem[nm]
        v = s[2] if val is None else val
        if v > 0:
            self.emit(eng, lambda e, h=s[0], v=v: e.wait_ge(h, v))

    def run(self, eng, nm, fn):
        s = self.sem[nm]
        s[2] += s[1]
        def wrapped(e, fn=fn, h=s[0], step=s[1]):
            ins = fn(e)
            assert ins is not None, "run() closure must return an instruction"
            ins.then_inc(h, step)
        self.emit(eng, wrapped)


class DQ:

    NSEM = 4

    def __init__(self, kb, es, name, eng):
        self.kb = kb
        self.eng = eng
        self.name = name
        self.sems = [es.enter_context(kb.nc.semaphore(f"dq_{name}{i}"))
                     for i in range(self.NSEM)]
        self.count = [0] * self.NSEM
        self.pend = [None] * self.NSEM
        self.cur = 0
        self.dirty = False

    def dma(self, out_ap, in_ap):
        i = self.cur
        if not self.dirty and self.pend[i] is not None:
            tgt = self.pend[i]
            self.kb.emit(self.eng,
                         lambda e, h=self.sems[i], v=tgt: e.wait_ge(h, v))
            self.pend[i] = None
        self.dirty = True
        self.count[i] += 16
        h = self.sems[i]
        self.kb.emit(self.eng, lambda e, o=out_ap, a=in_ap, h=h:
                     e.dma_start(out=o, in_=a).then_inc(h, 16))

    def fence(self):
        assert self.dirty, "fence with no dmas"
        i = self.cur
        tok = (self.sems[i], self.count[i])
        self.pend[i] = self.count[i]
        self.cur = (i + 1) % self.NSEM
        self.dirty = False
        return tok

    def wait(self, eng, tok):
        h, v = tok
        self.kb.emit(eng, lambda e, h=h, v=v: e.wait_ge(h, v))

    def final_tokens(self):
        out = []
        for i in range(self.NSEM):
            if self.count[i]:
                out.append((self.sems[i], self.count[i]))
        return out


class Bank:

    def __init__(self, kb, ap, idx):
        self.kb = kb
        self.ap = ap
        self.idx = idx
        self.prod = 0
        self.cons = ("v", 0)

    def produce(self, fn, waitv=None):
        kb = self.kb
        nm, c = self.cons
        kb.wait("tensor", nm, c)
        if waitv is not None:
            kb.wait("tensor", "v", waitv)
        kb.run("tensor", "t", fn)
        self.prod = kb.cnt("t")

    def consume(self, eng, fn, guard=None):
        kb = self.kb
        kb.wait(eng, "t", self.prod)
        nm = {"vector": "v", "scalar": "s", "gpsimd": "g"}[eng]
        if guard is not None:
            ap = self.ap
            if eng == "vector":
                kb.run(eng, nm,
                       lambda ve, ap=ap, g=guard: ve.tensor_copy(g, ap[:, 0:1]))
            else:
                kb.run(eng, nm,
                       lambda se, ap=ap, g=guard: se.square(g, ap[:, 0:1]))
            kb.wait(eng, nm, kb.cnt(nm))
        kb.run(eng, nm, fn)
        self.cons = (nm, kb.cnt(nm))


BARRIER_ENGS = ("tensor", "vector", "scalar", "gpsimd", "sync")


def barrier(kb):
    targets = {}
    for eng, nm in (("vector", "v"), ("scalar", "s"), ("gpsimd", "g"),
                    ("sync", "sy")):
        def bump(e, eng=eng):
            if eng in ("vector", "gpsimd"):
                return e.engine_nop()
            return e.nop()
        kb.run(eng, nm, bump)
        targets[eng] = (nm, kb.cnt(nm))
    targets["tensor"] = ("t", kb.cnt("t"))
    for eng in ("tensor", "vector", "scalar", "gpsimd", "sync"):
        for other, (nm, v) in targets.items():
            if other != eng:
                kb.wait(eng, nm, v)


def build_program(stage="full", debug=False):
    nc = bass.Bass("TRN2", target_bir_lowering=False,
                   detect_race_conditions=bool(
                       os.environ.get("RACEDET")))
    es = ExitStack()
    kb = KB(nc, es)

    def din(name, shape):
        return nc.dram_tensor(name, shape, F32, kind="ExternalInput").ap()

    x_in = din("x", [BPC, L, D])
    w_a_h = din("w_a", [2 * INNER, D])
    w_x_h = din("w_x", [2 * INNER, D])
    w_out_h = din("w_out", [D, INNER])
    v_re_h = din("v_re", [INNER, INNER])
    v_im_h = din("v_im", [INNER, INNER])
    h0_re_h = din("h0_re", [INNER])
    h0_im_h = din("h0_im", [INNER])
    gate_w_h = din("gate_w", [NEXP, D])
    gate_b_h = din("gate_b", [NEXP])
    w1_h = din("w1", [NEXP, D, HDIM])
    b1_h = din("b1", [NEXP, HDIM])
    w3_h = din("w3", [NEXP, D, HDIM])
    b3_h = din("b3", [NEXP, HDIM])
    w2_h = din("w2", [NEXP, HDIM, D])
    b2_h = din("b2", [NEXP, D])
    sw1_h = din("sw1", [D, HDIM])
    sb1_h = din("sb1", [HDIM])
    sw3_h = din("sw3", [D, HDIM])
    sb3_h = din("sb3", [HDIM])
    sw2_h = din("sw2", [HDIM, D])
    sb2_h = din("sb2", [D])

    out_h = nc.dram_tensor("out", [BPC, L, D], F32, kind="ExternalOutput").ap()
    out_flat = out_h.rearrange("b l d -> (b l) d")
    x_flat = x_in.rearrange("b l d -> (b l) d")

    if os.environ.get("XPOUT"):
        xp_scr = nc.dram_tensor("xp_scr", [8, 128, TPC], F32,
                                kind="ExternalOutput").ap()
    else:
        xp_scr = nc.dram_tensor("xp_scr", [8, 128, TPC], F32).ap()
    waT_scr = nc.dram_tensor("waT_scr", [8, 128, 2 * INNER], F32).ap()
    wxT_scr = nc.dram_tensor("wxT_scr", [8, 128, 2 * INNER], F32).ap()
    woT_scr = nc.dram_tensor("woT_scr", [4, 128, D], F32).ap()
    sac_scr = nc.dram_tensor("sac_scr", [128, 256], F32).ap()

    dbg = {}
    if debug:
        for nm_, shp in [("dbg_xinv_re", [4, 128, INNER]),
                         ("dbg_xinv_im", [4, 128, INNER]),
                         ("dbg_waT", [9, 128, 2 * INNER]),
                         ("dbg_pl", [20, 128, 2048]),
                         ("dbg_xp", [8, 128, TPC]),
                         ("dbg_it", [NIT_R + NIT_F + 6, 128, 2048])]:
            dbg[nm_] = nc.dram_tensor(nm_, shp, F32, kind="ExternalOutput").ap()

    def sb(name, p, f, dtype=F32):
        return es.enter_context(nc.sbuf_tensor(name, [p, f], dtype))

    ident = sb("ident", 128, 128)
    gsc = sb("gsc", 128, 2)
    xp_keep = sb("xp_keep", 128, 8 * CA)
    ones_row = sb("ones_row", 1, 1024)
    consts = sb("consts", 128, 16)
    crow = sb("crow", 1, 16)
    h0_sb = sb("h0_sb", 128, 8)
    gwT = sb("gwT", 128, 8 * NEXP)
    gbb = sb("gbb", 128, NEXP)
    gb8 = sb("gb8", 1, 8)

    g_stg = sb("g_stg", NEXP, D)
    es_vx = ExitStack()
    def sbx(name, p, f):
        return es_vx.enter_context(nc.sbuf_tensor(name, [p, f], F32))
    x_re = sbx("x_re", 128, 4 * INNER)
    x_im = sbx("x_im", 128, 4 * INNER)
    x_imn = sbx("x_imn", 128, 4 * INNER)
    vT_re = sbx("vT_re", 128, 4 * INNER)
    vT_imn = sbx("vT_imn", 128, 4 * INNER)
    es_va = ExitStack()
    ii_a = es_va.enter_context(nc.sbuf_tensor("ii_a", [128, 128], F32))
    ii_b = es_va.enter_context(nc.sbuf_tensor("ii_b", [128, 128], F32))
    vA_re = es_va.enter_context(nc.sbuf_tensor("vA_re", [128, 4 * INNER], F32))
    vA_im = es_va.enter_context(nc.sbuf_tensor("vA_im", [128, 4 * INNER], F32))
    vA_imn = es_va.enter_context(nc.sbuf_tensor("vA_imn", [128, 4 * INNER], F32))

    psb = [es.enter_context(nc.psum_tensor(f"ps{i}", [128, 512], F32))
           for i in range(8)]
    banks = [Bank(kb, psb[i], i) for i in range(8)]
    dq_ld = DQ(kb, es, "ld", "sync")
    dq_gl = DQ(kb, es, "gl", "gpsimd")
    _steng = "sync" if os.environ.get("STSYNC") else "gpsimd"
    dq_st = DQ(kb, es, "st", _steng)

    def ccol(v):
        return consts[:, CIDX[v]:CIDX[v] + 1]

    def crow_ap(v):
        return crow[0:1, CIDX[v]:CIDX[v] + 1]

    def tile(plane, W, r, c0=None, c1=None):
        if c0 is None:
            return plane[:, r * W:(r + 1) * W]
        return plane[:, r * W + c0: r * W + c1]

    def ms(ap, val):
        kb.run("vector", "v", lambda ve, a=ap, x=float(val): ve.memset(a, x))

    ms(ones_row[:], 1.0)
    for j, val in enumerate(CONST_VALS):
        ms(consts[:, j:j + 1], val)
        ms(crow[0:1, j:j + 1], val)
    kb.run("gpsimd", "g", lambda e: e.iota(ii_a[:], [[1, 128]], base=0,
                                           channel_multiplier=0,
                                           allow_small_or_imprecise_dtypes=True))
    kb.run("gpsimd", "g", lambda e: e.iota(ii_b[:], [[0, 128]], base=0,
                                           channel_multiplier=1,
                                           allow_small_or_imprecise_dtypes=True))
    kb.wait("vector", "g")
    kb.run("vector", "v",
           lambda ve: ve.tensor_tensor(ident[:], ii_a[:], ii_b[:], ALU.is_equal))
    V_CONSTS = kb.cnt("v")
    for eng in ("tensor", "scalar", "gpsimd", "sync"):
        kb.wait(eng, "v", V_CONSTS)
    kb.wait("vector", "v", V_CONSTS)

    dq_gl.dma(gb8[0:1, 0:NEXP], gate_b_h.unsqueeze(0))
    for g in range(4):
        dq_gl.dma(h0_sb[:, g:g + 1], h0_re_h[g * 128:(g + 1) * 128].unsqueeze(1))
        dq_gl.dma(h0_sb[:, 4 + g:5 + g],
                  h0_im_h[g * 128:(g + 1) * 128].unsqueeze(1))
    tok_small = dq_gl.fence()

    for g in range(4):
        dq_ld.dma(tile(vA_re, INNER, g), v_re_h[g * 128:(g + 1) * 128, :])
        dq_ld.dma(tile(vA_im, INNER, g), v_im_h[g * 128:(g + 1) * 128, :])
    tok_v = dq_ld.fence()
    dq_ld.wait("vector", tok_v)
    kb.run("vector", "v", lambda ve: ve.tensor_scalar_mul(vA_imn[:], vA_im[:], -1.0))
    V_VA = kb.cnt("v")

    dq_ld.wait("tensor", tok_v)

    def emit_transpose(src_fn, dst_fn, nrt, nct, negate=False, waitv=None):
        for c in range(nct):
            for r0 in range(0, nrt, 4):
                grp = list(range(r0, min(r0 + 4, nrt)))
                bk = banks[6 + ((c * nrt + r0) // 4) % 2]
                def tf(te, c=c, grp=grp, bk=bk):
                    last = None
                    for k, r in enumerate(grp):
                        last = te.transpose(bk.ap[:, k * 128:(k + 1) * 128],
                                            src_fn(r, c), ident[:])
                    return last
                bk.produce(tf, waitv=waitv)
                if negate:
                    def ev(ve, c=c, grp=grp, bk=bk):
                        return ve.tensor_scalar_mul(
                            dst_fn(c, grp[0], len(grp)),
                            bk.ap[:, 0:len(grp) * 128], -1.0)
                else:
                    def ev(ve, c=c, grp=grp, bk=bk):
                        return ve.tensor_copy(dst_fn(c, grp[0], len(grp)),
                                              bk.ap[:, 0:len(grp) * 128])
                bk.consume("vector", ev, guard=gsc[:, 0:1])

    def src_plane(plane, W):
        return lambda r, c: plane[:, r * W + c * 128: r * W + c * 128 + 128]

    def dst_plane(plane, W):
        return lambda c, r0, n: plane[:, c * W + r0 * 128: c * W + (r0 + n) * 128]

    emit_transpose(src_plane(vA_re, INNER), dst_plane(vT_re, INNER), 4, 4,
                   waitv=V_CONSTS)
    emit_transpose(src_plane(vA_im, INNER), dst_plane(vT_imn, INNER), 4, 4,
                   negate=True)

    dq_ld.dma(g_stg[:, :], gate_w_h[:, :])
    tok_gw = dq_ld.fence()
    dq_ld.wait("tensor", tok_gw)
    for c in range(8):
        bk = banks[6]
        def tg(te, c=c, bk=bk):
            return te.transpose(bk.ap[:, 0:NEXP],
                                g_stg[:, c * 128:(c + 1) * 128],
                                ident[0:NEXP, 0:NEXP])
        bk.produce(tg)
        def eg(ve, c=c, bk=bk):
            return ve.tensor_copy(gwT[:, c * NEXP:(c + 1) * NEXP],
                                  bk.ap[:, 0:NEXP])
        bk.consume("vector", eg)

    dq_gl.wait("tensor", tok_small)
    bk = banks[7]
    def bgb(te, bk=bk):
        return te.matmul(bk.ap[:, 0:NEXP], ones_row[0:1, 0:128],
                         gb8[0:1, 0:NEXP], start=True, stop=True)
    bk.produce(bgb)
    bk.consume("vector",
               lambda ve, bk=bk: ve.tensor_copy(gbb[:], bk.ap[:, 0:NEXP]))

    es_n = ExitStack()
    def sbn(name, p, f, dtype=F32):
        return es_n.enter_context(nc.sbuf_tensor(name, [p, f], dtype))
    vr_re = sbn("vr_re", 128, 4 * INNER, F32R)
    vr_im = sbn("vr_im", 128, 4 * INNER, F32R)
    vr_imn = sbn("vr_imn", 128, 4 * INNER, F32R)
    xr_re = sbn("xr_re", 128, 4 * INNER, F32R)
    xr_im = sbn("xr_im", 128, 4 * INNER, F32R)
    xtr_re = sbn("xtr_re", 128, 4 * INNER, F32R)
    xtr_im = sbn("xtr_im", 128, 4 * INNER, F32R)
    xtr_imn = sbn("xtr_imn", 128, 4 * INNER, F32R)
    pr_re = sbn("pr_re", 128, 4 * INNER, F32R)
    pr_im = sbn("pr_im", 128, 4 * INNER, F32R)
    xt32_re = sbn("xt32_re", 128, 4 * INNER)
    xt32_im = sbn("xt32_im", 128, 4 * INNER)
    xt32_imn = sbn("xt32_imn", 128, 4 * INNER)
    p32_re = sbn("p32_re", 128, 4 * INNER)
    p32_im = sbn("p32_im", 128, 4 * INNER)
    nsc = sbn("nsc", 128, 4)
    ones128 = sbn("ones128", 128, 128)

    kb.wait("vector", "v", V_VA)
    def cast_vr(ve):
        ve.tensor_copy(vr_re[:], vA_re[:])
        ve.tensor_copy(vr_im[:], vA_im[:])
        return ve.tensor_copy(vr_imn[:], vA_imn[:])
    kb.run("vector", "v", cast_vr)
    V_VR = kb.cnt("v")

    kb.run("vector", "v", lambda ve: ve.memset(nsc[:, 0:3], 0.0))
    V_NSC0 = kb.cnt("v")
    kb.wait("vector", "v", V_NSC0)
    V_red = None
    for mi in range(4):
        for comp in (0, 1):
            bk = banks[(2 * mi + comp) % 4]
            if comp == 0:
                pairs = [(vr_re, vr_re), (vr_im, vr_im)]
            else:
                pairs = [(vr_im, vr_re), (vr_re, vr_imn)]
            def fg(te, mi=mi, pairs=pairs, bk=bk):
                last = None
                first = True
                for lhs, rhs in pairs:
                    for k in range(4):
                        last = te.matmul(
                            bk.ap[:, 0:INNER],
                            lhs[:, k * INNER + mi * 128: k * INNER + mi * 128 + 128],
                            rhs[:, k * INNER:(k + 1) * INNER],
                            start=first, stop=(lhs is pairs[1][0] and k == 3))
                        first = False
                return last
            bk.produce(fg, waitv=V_VR if mi == 0 and comp == 0 else None)
            sl = ((2 * mi + comp) % 4) * INNER
            if V_red is not None:
                kb.wait("scalar", "v", V_red)
            def evsq(se, bk=bk, sl=sl):
                return se.square(p32_re[:, sl:sl + INNER], bk.ap[:, 0:INNER])
            bk.consume("scalar", evsq, guard=gsc[:, 1:2])
            S_GSQ = kb.cnt("s")
            kb.wait("vector", "s", S_GSQ)
            kb.run("vector", "v",
                   lambda ve, sl=sl: ve.reduce_sum(nsc[:, 2:3],
                                                   p32_re[:, sl:sl + INNER],
                                                   axis=AXX))
            kb.wait("vector", "v")
            kb.run("vector", "v",
                   lambda ve: ve.tensor_add(nsc[:, 0:1], nsc[:, 0:1],
                                            nsc[:, 2:3]))
            V_red = kb.cnt("v")
    V_GRAM = kb.cnt("v")

    kb.run("vector", "v", lambda ve: ve.memset(ones128[:], 1.0))
    V_ONES128 = kb.cnt("v")
    bk = banks[4]
    bk.produce(lambda te, bk=bk: te.matmul(bk.ap[:, 0:1], ones128[:],
                                           nsc[:, 0:1], start=True, stop=True),
               waitv=max(V_GRAM, V_ONES128))
    def alpha_fn(se, bk=bk):
        return se.activation(nsc[:, 2:3], bk.ap[:, 0:1], AF.Ln)
    bk.consume("scalar", alpha_fn, guard=gsc[:, 1:2])
    kb.wait("scalar", "s")
    kb.run("scalar", "s",
           lambda se: se.activation(nsc[:, 3:4], nsc[:, 2:3], AF.Exp,
                                    scale=ccol(-0.5),
                                    bias=ccol(LN_INV102)))
    kb.wait("scalar", "s", kb.cnt("s"))
    kb.run("scalar", "s", lambda se: se.square(gsc[:, 1:2], nsc[:, 3:4]))
    S_ALPHA = kb.cnt("s")
    kb.wait("scalar", "s", S_ALPHA)
    kb.run("scalar", "s", lambda se: se.square(gsc[:, 1:2], nsc[:, 3:4]))

    import os as _os
    if debug and not _os.environ.get("NODUMP"):
        kb.wait(_steng, "s", S_ALPHA)
        dq_st.dma(dbg["dbg_waT"][8, :, 0:4], nsc[:, 0:4])
        dq_st.fence()

    kb.wait("vector", "s", S_ALPHA)
    def x0(ve):
        al = bass.AP(nsc, 3, [[4, 128], [0, 4 * INNER]])
        ve.tensor_tensor(x_re[:], vA_re[:], al, ALU.mult)
        return ve.tensor_tensor(x_im[:], vA_imn[:], al, ALU.mult)
    kb.run("vector", "v", x0)
    kb.wait("vector", "v", kb.cnt("v"))

    def ndump(idx, plane):
        import os
        if os.environ.get("NODUMP"):
            return
        if not (debug and stage == "newton"):
            return
        kb.wait(_steng, "v")
        kb.wait(_steng, "s")
        w = plane.shape[-1]
        if w > 1024:
            dq_st.dma(dbg["dbg_waT"][idx, :, :], plane[:, 0:1024])
            dq_st.dma(dbg["dbg_waT"][idx + 1, :, :], plane[:, 1024:2048])
        else:
            dq_st.dma(dbg["dbg_waT"][idx, :, 0:w], plane)
        tok = dq_st.fence()
        dq_st.wait("vector", tok)
        dq_st.wait("scalar", tok)

    ndump(0, x_re[:])
    ndump(2, nsc[:, 0:4])

    it_counter = [0]

    def newton_iter(f32r_phase):
        if f32r_phase:
            a_re, a_im, a_imn = vr_re, vr_im, vr_imn
            tt_re, tt_im, tt_imn = xtr_re, xtr_im, xtr_imn
            pp_re, pp_im = pr_re, pr_im
        else:
            a_re, a_im, a_imn = vA_re, vA_im, vA_imn
            tt_re, tt_im, tt_imn = xt32_re, xt32_im, xt32_imn
            pp_re, pp_im = p32_re, p32_im

        if f32r_phase:
            kb.wait("vector", "v", kb.cnt("v"))
            def cx(ve):
                ve.tensor_copy(xr_re[:], x_re[:])
                return ve.tensor_copy(xr_im[:], x_im[:])
            kb.run("vector", "v", cx)
            xx_re, xx_im = xr_re, xr_im
        else:
            xx_re, xx_im = x_re, x_im
        V_X = kb.cnt("v")
        import os as _os2
        first_dbg = (debug and it_counter[0] == 0 and f32r_phase
                     and not _os2.environ.get("NODUMP"))
        if first_dbg:
            kb.wait(_steng, "v", V_X)
            dq_st.dma(dbg["dbg_it"][40, :, :], x_re[:])
            dq_st.dma(dbg["dbg_it"][44, :, :], xr_re[:])
            dq_st.fence()

        for g in range(4):
            bk = banks[6 + g % 2]
            def tf(te, g=g, bk=bk):
                last = None
                for r in range(4):
                    last = te.transpose(
                        bk.ap[:, r * 128:(r + 1) * 128],
                        x_re[:, r * INNER + g * 128: r * INNER + g * 128 + 128],
                        ident[:])
                return last
            bk.produce(tf, waitv=V_X)
            def ev(ve, g=g, bk=bk, tt_re=tt_re):
                return ve.tensor_copy(tile(tt_re, INNER, g), bk.ap[:, 0:INNER])
            bk.consume("vector", ev, guard=gsc[:, 0:1])
        for g in range(4):
            bk = banks[6 + g % 2]
            def tf(te, g=g, bk=bk):
                last = None
                for r in range(4):
                    last = te.transpose(
                        bk.ap[:, r * 128:(r + 1) * 128],
                        x_im[:, r * INNER + g * 128: r * INNER + g * 128 + 128],
                        ident[:])
                return last
            bk.produce(tf)
            def ev(ve, g=g, bk=bk, tt_im=tt_im, tt_imn=tt_imn):
                ve.tensor_copy(tile(tt_im, INNER, g), bk.ap[:, 0:INNER])
                return ve.tensor_scalar_mul(tile(tt_imn, INNER, g),
                                            bk.ap[:, 0:INNER], -1.0)
            bk.consume("vector", ev, guard=gsc[:, 0:1])
        V_TT = kb.cnt("v")
        if first_dbg:
            kb.wait(_steng, "v", V_TT)
            dq_st.dma(dbg["dbg_it"][41, :, :], x_re[:])
            dq_st.dma(dbg["dbg_it"][43, :, :], tt_re[:])
            dq_st.fence()

        for mi in range(4):
            for comp in (0, 1):
                bk = banks[(2 * mi + comp) % 4]
                if comp == 0:
                    pairs = [(a_re, xx_re), (a_imn, xx_im)]
                    dst = pp_re
                else:
                    pairs = [(a_im, xx_re), (a_re, xx_im)]
                    dst = pp_im
                def fp(te, mi=mi, pairs=pairs, bk=bk):
                    last = None
                    first = True
                    for pi, (lhs, rhs) in enumerate(pairs):
                        for k in range(4):
                            last = te.matmul(
                                bk.ap[:, 0:INNER],
                                lhs[:, k * INNER + mi * 128:
                                    k * INNER + mi * 128 + 128],
                                rhs[:, k * INNER:(k + 1) * INNER],
                                start=first, stop=(pi == 1 and k == 3))
                            first = False
                    return last
                bk.produce(fp, waitv=V_X)
                def ev(ve, mi=mi, bk=bk, dst=dst):
                    return ve.tensor_copy(tile(dst, INNER, mi),
                                          bk.ap[:, 0:INNER])
                bk.consume("vector", ev, guard=gsc[:, 0:1])
        V_P = kb.cnt("v")
        if first_dbg:
            kb.wait(_steng, "v", V_P)
            dq_st.dma(dbg["dbg_it"][42, :, :], pp_re[:])
            dq_st.dma(dbg["dbg_it"][45, :, :], x_re[:])
            dq_st.fence()
        it_counter[0] += 1

        for mi in range(4):
            for comp in (0, 1):
                bk = banks[(2 * mi + comp) % 4]
                if comp == 0:
                    pairs = [(tt_re, pp_re), (tt_imn, pp_im)]
                    dst = x_re
                else:
                    pairs = [(tt_im, pp_re), (tt_re, pp_im)]
                    dst = x_im
                def fq(te, mi=mi, pairs=pairs, bk=bk):
                    last = None
                    first = True
                    for pi, (lhs, rhs) in enumerate(pairs):
                        for k in range(4):
                            last = te.matmul(
                                bk.ap[:, 0:INNER],
                                lhs[:, k * INNER + mi * 128:
                                    k * INNER + mi * 128 + 128],
                                rhs[:, k * INNER:(k + 1) * INNER],
                                start=first, stop=(pi == 1 and k == 3))
                            first = False
                    return last
                bk.produce(fq, waitv=max(V_P, V_TT))
                def ev(ve, mi=mi, bk=bk, dst=dst):
                    return ve.scalar_tensor_tensor(
                        tile(dst, INNER, mi), tile(dst, INNER, mi), 2.0,
                        bk.ap[:, 0:INNER], ALU.mult, ALU.subtract)
                bk.consume("vector", ev, guard=gsc[:, 0:1])

    def itdump(i):
        import os
        if (not debug) or os.environ.get("NODUMP"):
            return
        dq_st.dma(dbg["dbg_it"][i, :, :], x_re[:])
        dq_st.fence()

    for _it in range(NIT_R):
        newton_iter(True)
        itdump(_it)
        if _it == 0:
            ndump(4, x_re[:])
            ndump(6, p32_re[:])
    for _it in range(NIT_F):
        newton_iter(False)
        itdump(NIT_R + _it)

    kb.wait("vector", "v", kb.cnt("v"))
    kb.run("vector", "v",
           lambda ve: ve.tensor_scalar_mul(x_imn[:], x_im[:], -1.0))
    V_NEWTON = kb.cnt("v")

    if debug:
        kb.wait(_steng, "v", V_NEWTON)
        kb.wait(_steng, "s", S_ALPHA)
        for g in range(4):
            dq_st.dma(dbg["dbg_xinv_re"][g, :, :], tile(x_re, INNER, g))
            dq_st.dma(dbg["dbg_xinv_im"][g, :, :], tile(x_im, INNER, g))
        dq_st.fence()

    if stage == "newton":
        for tok in dq_st.final_tokens():
            dq_st.wait("gpsimd", tok)
        _finish(nc, kb)
        es_n.close()
        es_va.close()
        es_vx.close()
        es.close()
        return nc, dbg

    if WSCR_PARTS >= -1:
        barrier(kb)
    es_n.close()
    es_va.close()

    es_w = ExitStack()
    stg = es_w.enter_context(nc.sbuf_tensor("stg", [128, 8 * 1024], F32))
    wstg = es_w.enter_context(nc.sbuf_tensor("wstg", [128, 2 * 1024], F32))

    wstg_toks = {}

    def transpose_to_scratch(load_fn, n_src_tiles, n_dst_tiles, scr, dstw):
        if WSCR_PARTS < 0:
            return {}
        kb.wait("sync", "t")
        for r in range(n_src_tiles):
            dq_ld.dma(stg[:, r * 1024: r * 1024 + n_dst_tiles * 128], load_fn(r))
        tok = dq_ld.fence()
        if WSCR_PARTS < 1:
            return {}
        dq_ld.wait("tensor", tok)
        store_toks = wstg_toks
        for c in range(n_dst_tiles):
            par = c % 2
            if par in store_toks:
                dq_st.wait("vector", store_toks[par])
            for r0 in range(0, n_src_tiles, 4):
                grp = list(range(r0, min(r0 + 4, n_src_tiles)))
                bk = banks[6 + (r0 // 4) % 2]
                def tf(te, c=c, grp=grp, bk=bk):
                    last = None
                    for k, r in enumerate(grp):
                        last = te.transpose(
                            bk.ap[:, k * 128:(k + 1) * 128],
                            stg[:, r * 1024 + c * 128: r * 1024 + c * 128 + 128],
                            ident[:])
                    return last
                bk.produce(tf)
                def ev(ve, grp=grp, bk=bk, par=par):
                    return ve.tensor_copy(
                        wstg[:, par * 1024 + grp[0] * 128:
                             par * 1024 + (grp[0] + len(grp)) * 128],
                        bk.ap[:, 0:len(grp) * 128])
                bk.consume("vector", ev, guard=gsc[:, 0:1])
            if WSCR_PARTS < 2:
                continue
            kb.wait(_steng, "v")
            dq_st.dma(scr[c, :, 0:n_src_tiles * 128],
                      wstg[:, par * 1024: par * 1024 + n_src_tiles * 128])
            store_toks[par] = dq_st.fence()
        return store_toks


    wa2 = w_a_h.rearrange("(c two) d -> c two d", two=2)
    wx2 = w_x_h.rearrange("(c two) d -> c two d", two=2)

    def wa_src(r):
        half, i = divmod(r, 4)
        return wa2[i * 128:(i + 1) * 128, half, :]

    def wa_src_fixed(r):
        half = r // 4
        i = r % 4
        return wa2[i * 128:(i + 1) * 128, half, :]

    def wx_src(r):
        half = r // 4
        i = r % 4
        return wx2[i * 128:(i + 1) * 128, half, :]

    toks_wa = transpose_to_scratch(wa_src_fixed, 8, 8, waT_scr, None)
    if WSCR_PARTS >= 3:
        toks_wx = transpose_to_scratch(wx_src, 8, 8, wxT_scr, None)
        toks_wo = transpose_to_scratch(
            lambda r: w_out_h[r * 128:(r + 1) * 128, :], 8, 4, woT_scr, None)
    else:
        toks_wx, toks_wo = {}, {}
    for tk in list(toks_wa.values()) + list(toks_wx.values()) + list(toks_wo.values()):
        dq_st.wait("sync", tk)
    if debug:
        for c in range(8):
            dq_gl.dma(dbg["dbg_waT"][c, :, :], waT_scr[c, :, :])
        dq_gl.fence()

    if debug and stage == "wscr" and WSCR_PARTS >= 2:
        for c in range(8):
            dq_ld.dma(stg[:, 0:1024], waT_scr[c, :, :])
            tk1 = dq_ld.fence()
            dq_ld.wait("gpsimd", tk1)
            dq_gl.dma(dbg["dbg_waT"][c, :, :], stg[:, 0:1024])
            tk2 = dq_gl.fence()
            dq_gl.wait("sync", tk2)
    if stage == "wscr":
        for dq in (dq_st, dq_ld, dq_gl):
            for tok in dq.final_tokens():
                dq.wait("gpsimd", tok)
        _finish(nc, kb)
        es_w.close()
        es_vx.close()
        es.close()
        return nc, dbg

    barrier(kb)
    es_w.close()

    es_pa = ExitStack()
    def sba(name, p, f):
        return es_pa.enter_context(nc.sbuf_tensor(name, [p, f], F32))
    x_tm = sba("x_tm", 128, 2 * 1024)
    x_fm = sba("x_fm", 128, 8 * CA)
    xn = sba("xn", 128, 8 * CA)
    tmp8 = sba("tmp8", 128, 8 * CA)
    srow = sba("srow", 1, 1024)
    rstd_b = sba("rstd_b", 128, CA)
    mrs_b = sba("mrs_b", 128, CA)
    ar_p = sba("ar_p", 128, 4 * CA)
    ai_p = sba("ai_p", 128, 4 * CA)
    m_a = sba("m_a", 128, 4 * CA)
    rho = sba("rho", 128, 4 * CA)
    t4a = sba("t4a", 128, 4 * CA)
    t4b = sba("t4b", 128, 4 * CA)
    X_s = sba("X_s", 128, 4 * (CA + 1))
    kint = es_pa.enter_context(
        nc.sbuf_tensor("kint", [128, 4 * (CA + 1)], mybir.dt.int32))
    psi = sba("psi", 128, 4 * (CA + 1))
    schalf = sba("schalf", 128, 4 * (CA + 1))
    cosS = sba("cosS", 128, 4 * (CA + 1))
    sinS = sba("sinS", 128, 4 * (CA + 1))
    xcr_s = sba("xcr_s", 128, 4 * (CA + 1))
    xci_s = sba("xci_s", 128, 4 * (CA + 1))
    d_re = sba("d_re", 128, 4 * CA)
    d_im = sba("d_im", 128, 4 * CA)
    w_re = sba("w_re", 128, 4 * (CA + 1))
    w_im = sba("w_im", 128, 4 * (CA + 1))
    hre = sba("hre", 128, 4 * CA)
    onesp = sba("onesp", 128, CA)
    wa_buf = sba("wa_buf", 128, 2 * 1024)
    wx_buf = sba("wx_buf", 128, 2 * 1024)
    _wofull = os.environ.get("WOFULL", "1") != "0"
    wo_buf = sba("wo_buf", 128, 4 * 1024 if _wofull else 2 * 1024)
    if _wofull:
        for k in range(4):
            dq_ld.dma(wo_buf[:, k * 1024:(k + 1) * 1024], woT_scr[k, :, :])
        tok_wof = dq_ld.fence()
        dq_ld.wait("tensor", tok_wof)

    W1 = CA + 1

    def ap3(t, col0, tstride, nt, w, pw):
        return bass.AP(t, col0, [[pw, 128], [tstride, nt], [1, w]])

    def pl(t, nt=4, w=CA, col0=0, pw=None):
        return ap3(t, col0, w if pw is None else None, nt, w, nt * w)

    def full4(t, w=CA):
        return bass.AP(t, 0, [[4 * w, 128], [1, 4 * w]])

    def strided4(t, col0, w, tstride):
        return bass.AP(t, col0, [[4 * tstride, 128], [tstride, 4], [1, w]])

    def cbc(v, n):
        return bass.AP(consts, CIDX[v], [[16, 128], [0, n]])

    kb.run("vector", "v", lambda ve: ve.memset(onesp[:], 1.0))
    V_ONESP = kb.cnt("v")
    kb.wait("vector", "v", V_ONESP)

    tok_x = [None] * 3
    wa_use = [0, 0]
    wx_use = [0, 0]
    wo_use = [0, 0]
    wa_tok = [None, None]
    wx_tok = [None, None]
    wo_tok = [None, None]

    def stream_tile(dq, scr, k, buf, use, toks, eng="sync"):
        par = k % 2
        if use[par] > 0:
            kb.wait(eng, "t", use[par])
        dq.dma(buf[:, par * 1024:(par + 1) * 1024], scr[k, :, :])
        toks[par] = dq.fence()

    for ch in range(min(BPC * NCHS, PA_LIMIT)):
        first = (ch % NCHS) == 0
        g0 = ch * CA

        def dbgdump(idx, ap, width, ch=ch):
            _c = int(os.environ.get("CHDUMP", "0") or 0)
            if not (debug and ch == _c):
                return
            kb.wait(_steng, "v")
            kb.wait(_steng, "s")
            dq_st.dma(dbg["dbg_pl"][idx, :, 0:width], ap)
            tok = dq_st.fence()
            dq_st.wait("vector", tok)
            dq_st.wait("scalar", tok)

        for tt in range(2):
            dq_ld.dma(x_tm[:, tt * 1024:(tt + 1) * 1024],
                      x_flat[g0 + tt * 128: g0 + (tt + 1) * 128, :])
        tokx = dq_ld.fence()
        dq_ld.wait("tensor", tokx)
        _chd = os.environ.get("CHDUMP")
        if debug and _chd is not None and ch == int(_chd):
            dq_ld.wait(_steng, tokx)
            dq_st.dma(dbg["dbg_pl"][17, :, 0:2048], x_tm[:, 0:2048])
            tok_chd = dq_st.fence()
            dq_st.wait("vector", tok_chd)
        for g in range(8):
            bk = banks[6 + g % 2]
            def tf(te, g=g, bk=bk):
                last = None
                for tt in range(2):
                    last = te.transpose(
                        bk.ap[:, tt * 128:(tt + 1) * 128],
                        x_tm[:, tt * 1024 + g * 128: tt * 1024 + g * 128 + 128],
                        ident[:])
                return last
            bk.produce(tf)
            def ev(ve, g=g, bk=bk):
                return ve.tensor_copy(x_fm[:, g * CA:(g + 1) * CA],
                                      bk.ap[:, 0:CA])
            bk.consume("vector", ev, guard=gsc[:, 0:1])
        V_XFM = kb.cnt("v")
        if debug and _chd is not None and ch == int(_chd):
            kb.wait(_steng, "v", V_XFM)
            dq_st.dma(dbg["dbg_pl"][18, :, 0:2048], x_fm[:, 0:2048])
            tok_chd2 = dq_st.fence()
            dq_st.wait("vector", tok_chd2)

        kb.wait("scalar", "v", V_XFM)
        kb.run("scalar", "s", lambda se: se.square(tmp8[:], x_fm[:]))
        S_XSQ = kb.cnt("s")
        bk6, bk7 = banks[6], banks[7]
        def sums(te, bk=bk6):
            last = None
            for k in range(8):
                last = te.matmul(bk.ap[0:1, 0:CA], ccol(1.0),
                                 x_fm[:, k * CA:(k + 1) * CA],
                                 start=(k == 0), stop=(k == 7))
            return last
        bk6.produce(sums, waitv=V_XFM)
        kb.wait("tensor", "s", S_XSQ)
        def sumsq(te, bk=bk7):
            last = None
            for k in range(8):
                last = te.matmul(bk.ap[0:1, 0:CA], ccol(1.0),
                                 tmp8[:, k * CA:(k + 1) * CA],
                                 start=(k == 0), stop=(k == 7))
            return last
        bk7.produce(sumsq)

        def stats_a(ve, bk=bk6, bq=bk7):
            ve.tensor_single_scalar(srow[0:1, 0:CA], bk.ap[0:1, 0:CA],
                                    1.0 / D, ALU.mult)
            return ve.tensor_single_scalar(srow[0:1, CA:2 * CA],
                                           bq.ap[0:1, 0:CA], 1.0 / D, ALU.mult)
        kb.wait("vector", "t", bk7.prod)
        bk6.consume("vector", stats_a)
        bk7.cons = ("v", kb.cnt("v"))
        kb.wait("vector", "v")
        kb.run("vector", "v",
               lambda ve: ve.tensor_tensor(srow[0:1, 2 * CA:3 * CA],
                                           srow[0:1, 0:CA], srow[0:1, 0:CA],
                                           ALU.mult))
        kb.wait("vector", "v")
        kb.run("vector", "v",
               lambda ve: ve.tensor_tensor(srow[0:1, CA:2 * CA],
                                           srow[0:1, CA:2 * CA],
                                           srow[0:1, 2 * CA:3 * CA],
                                           ALU.subtract))
        kb.wait("scalar", "v")
        kb.run("scalar", "s",
               lambda se: se.activation(srow[0:1, 2 * CA:3 * CA],
                                        srow[0:1, CA:2 * CA], AF.Ln,
                                        bias=crow_ap(EPS)))
        kb.wait("scalar", "s")
        kb.run("scalar", "s",
               lambda se: se.activation(srow[0:1, 2 * CA:3 * CA],
                                        srow[0:1, 2 * CA:3 * CA], AF.Exp,
                                        scale=crow_ap(-0.5)))
        S_RSTD = kb.cnt("s")
        kb.wait("vector", "s", S_RSTD)
        kb.run("vector", "v",
               lambda ve: ve.tensor_tensor(srow[0:1, 3 * CA:4 * CA],
                                           srow[0:1, 0:CA],
                                           srow[0:1, 2 * CA:3 * CA], ALU.mult))
        V_STATS = kb.cnt("v")
        def bc_rstd(te, bk=bk6):
            return te.matmul(bk.ap[:, 0:CA], ones_row[0:1, 0:128],
                             srow[0:1, 2 * CA:3 * CA], start=True, stop=True)
        bk6.produce(bc_rstd, waitv=V_STATS)
        def bc_mrs(te, bk=bk7):
            return te.matmul(bk.ap[:, 0:CA], ones_row[0:1, 0:128],
                             srow[0:1, 3 * CA:4 * CA], start=True, stop=True)
        bk7.produce(bc_mrs)
        def evb(ve, bk=bk6, bq=bk7):
            ve.tensor_copy(rstd_b[:], bk.ap[:, 0:CA])
            return ve.tensor_copy(mrs_b[:], bq.ap[:, 0:CA])
        kb.wait("vector", "t", bk7.prod)
        bk6.consume("vector", evb)
        bk7.cons = ("v", kb.cnt("v"))
        kb.wait("vector", "v", kb.cnt("v"))
        def apply_ln(ve):
            rb = bass.AP(rstd_b, 0, [[CA, 128], [0, 8], [1, CA]])
            mb = bass.AP(mrs_b, 0, [[CA, 128], [0, 8], [1, CA]])
            x8 = bass.AP(x_fm, 0, [[8 * CA, 128], [CA, 8], [1, CA]])
            xn8 = bass.AP(xn, 0, [[8 * CA, 128], [CA, 8], [1, CA]])
            t8 = bass.AP(tmp8, 0, [[8 * CA, 128], [CA, 8], [1, CA]])
            ve.tensor_tensor(t8, x8, rb, ALU.mult)
            return ve.tensor_tensor(xn8, t8, mb, ALU.subtract)
        kb.run("vector", "v", apply_ln)
        V_XN = kb.cnt("v")
        dbgdump(0, xn[:], 8 * CA)

        for k in range(8):
            stream_tile(dq_ld, waT_scr, k, wa_buf, wa_use, wa_tok)
            dq_ld.wait("tensor", wa_tok[k % 2])
            if k == 0:
                for gb in range(8):
                    nm, c = banks[gb].cons
                    kb.wait("tensor", nm, c)
                kb.wait("tensor", "v", V_XN)
            def mma(te, k=k):
                par = k % 2
                last = None
                for mi in range(8):
                    last = te.matmul(
                        banks[mi].ap[:, 0:CA],
                        wa_buf[:, par * 1024 + mi * 128: par * 1024 + (mi + 1) * 128],
                        xn[:, k * CA:(k + 1) * CA],
                        start=(k == 0), stop=(k == 7))
                return last
            kb.run("tensor", "t", mma)
            wa_use[k % 2] = kb.cnt("t")
        for gb in range(8):
            banks[gb].prod = kb.cnt("t")
        def eva(ve):
            for gb in range(4):
                ve.tensor_copy(ar_p[:, gb * CA:(gb + 1) * CA],
                               banks[gb].ap[:, 0:CA])
            last = None
            for gb in range(4):
                last = ve.tensor_copy(ai_p[:, gb * CA:(gb + 1) * CA],
                                      banks[gb + 4].ap[:, 0:CA])
            return last
        kb.wait("vector", "t")
        kb.run("vector", "v", eva)
        for gb in range(8):
            banks[gb].cons = ("v", kb.cnt("v"))

        def postp_a(ve):
            ve.tensor_tensor(t4a[:], ar_p[:], ar_p[:], ALU.mult)
            ve.tensor_tensor(t4b[:], ai_p[:], ai_p[:], ALU.mult)
            return ve.tensor_tensor(m_a[:], t4a[:], t4b[:], ALU.add)
        kb.run("vector", "v", postp_a)
        V_MA = kb.cnt("v")
        dbgdump(1, ar_p[:], 4 * CA)
        dbgdump(2, ai_p[:], 4 * CA)
        kb.wait("scalar", "v", V_MA)
        def lut_a(se):
            se.activation(t4a[:], m_a[:], AF.Ln, bias=ccol(1.0))
            se.activation(t4a[:], t4a[:], AF.Exp, scale=ccol(-1.0))
            se.activation(t4b[:], m_a[:], AF.Ln)
            return se.activation(t4b[:], t4b[:], AF.Exp, scale=ccol(0.5))
        kb.run("scalar", "s", lut_a)
        S_LUTA = kb.cnt("s")
        kb.wait("vector", "s", S_LUTA)
        def rho_den(ve):
            return ve.tensor_tensor(rho[:], m_a[:], t4a[:], ALU.mult)
        kb.run("vector", "v", rho_den)
        V_DEN = kb.cnt("v")
        dbgdump(3, rho[:], 4 * CA)
        def formsel(ve):
            ve.tensor_single_scalar(kint[:, 0:4 * CA], ar_p[:], 0.0,
                                     ALU.is_ge)
            ve.tensor_tensor(d_im[:], t4b[:], ar_p[:], ALU.subtract)
            ve.tensor_tensor(t4a[:], t4b[:], ar_p[:], ALU.add)
            ve.select(d_re[:], kint[:, 0:4 * CA], ai_p[:], d_im[:])
            return ve.select(t4b[:], kint[:, 0:4 * CA], t4a[:], ai_p[:])
        kb.run("vector", "v", formsel)
        def signs(ve):
            ve.tensor_single_scalar(m_a[:], d_re[:], 0.0, ALU.is_ge)
            ve.tensor_scalar(m_a[:], m_a[:], 2.0, -1.0, ALU.mult, ALU.add)
            ve.tensor_single_scalar(d_im[:], t4b[:], 0.0, ALU.is_ge)
            ve.tensor_scalar(d_im[:], d_im[:], 2.0, -1.0, ALU.mult, ALU.add)
            return ve.tensor_tensor(m_a[:], m_a[:], d_im[:], ALU.mult)
        kb.run("vector", "v", signs)
        V_SGN = kb.cnt("v")
        dbgdump(17, m_a[:], 4 * CA)
        dbgdump(18, d_re[:], 4 * CA)
        dbgdump(19, t4b[:], 4 * CA)
        kb.wait("scalar", "v", V_SGN)
        def lut_absnd(se):
            se.activation(d_re[:], d_re[:], AF.Abs)
            return se.activation(t4b[:], t4b[:], AF.Abs)
        kb.run("scalar", "s", lut_absnd)
        S_ABSND = kb.cnt("s")
        kb.wait("vector", "s", S_ABSND)
        kb.run("vector", "v",
               lambda ve: ve.tensor_single_scalar(t4b[:], t4b[:], 1e-12,
                                                  ALU.max))
        V_CL = kb.cnt("v")
        kb.wait("scalar", "v", V_CL)
        def lut_den(se):
            se.activation(t4b[:], t4b[:], AF.Ln)
            return se.activation(t4b[:], t4b[:], AF.Exp, scale=ccol(-1.0))
        kb.run("scalar", "s", lut_den)
        S_DEN = kb.cnt("s")
        kb.wait("vector", "s", S_DEN)
        kb.run("vector", "v",
               lambda ve: ve.tensor_tensor(t4a[:], d_re[:], t4b[:], ALU.mult))
        V_RATIO = kb.cnt("v")
        kb.wait("scalar", "v", V_RATIO)
        def lut_abs(se):
            se.activation(tmp8[:, 0:4 * CA], t4a[:], AF.Ln)
            return se.activation(tmp8[:, 0:4 * CA], tmp8[:, 0:4 * CA], AF.Exp,
                                 scale=ccol(-1.0))
        kb.run("scalar", "s", lut_abs)
        S_ABS = kb.cnt("s")
        kb.wait("vector", "s", S_ABS)
        def minmask(ve):
            ve.tensor_tensor(tmp8[:, 0:4 * CA], t4a[:], tmp8[:, 0:4 * CA],
                             ALU.min)
            return ve.tensor_single_scalar(t4a[:], t4a[:], 1.0, ALU.is_gt)
        kb.run("vector", "v", minmask)
        V_MIN = kb.cnt("v")
        kb.wait("scalar", "v", V_MIN)
        kb.run("scalar", "s",
               lambda se: se.activation(tmp8[:, 0:4 * CA], tmp8[:, 0:4 * CA],
                                        AF.Arctan))
        S_AT = kb.cnt("s")
        kb.wait("vector", "s", S_AT)
        def atanfold(ve):
            ve.tensor_scalar(t4b[:], tmp8[:, 0:4 * CA], -2.0, math.pi / 2,
                             ALU.mult, ALU.add)
            ve.tensor_tensor(t4b[:], t4a[:], t4b[:], ALU.mult)
            ve.tensor_tensor(t4a[:], tmp8[:, 0:4 * CA], t4b[:], ALU.add)
            return ve.tensor_tensor(t4a[:], t4a[:], m_a[:], ALU.mult)
        kb.run("vector", "v", atanfold)
        S_ATAN = kb.cnt("s")
        dbgdump(4, t4a[:], 4 * CA)

        def xcarry(ve, first=first):
            if first:
                return ve.tensor_copy(strided4(X_s, 0, 1, W1), cbc(0.0, 4).unsqueeze(2))
            return ve.tensor_copy(strided4(X_s, 0, 1, W1),
                                  strided4(X_s, CA, 1, W1))
        kb.wait("vector", "s", S_ATAN)
        kb.run("vector", "v", xcarry)
        kb.wait("vector", "v")
        def scans_x(ve):
            last = None
            for g in range(4):
                last = ve.tensor_tensor_scan(
                    X_s[:, g * W1 + 1: g * W1 + 1 + CA],
                    onesp[:], t4a[:, g * CA:(g + 1) * CA],
                    X_s[:, g * W1: g * W1 + 1], ALU.mult, ALU.add)
            return last
        kb.run("vector", "v", scans_x)
        def fold_psi(ve):
            a_ = bass.AP(X_s, 0, [[4 * W1, 128], [1, 4 * W1]])
            p_ = bass.AP(psi, 0, [[4 * W1, 128], [1, 4 * W1]])
            t_ = bass.AP(tmp8, 0, [[8 * CA, 128], [1, 4 * W1]])
            k_ = bass.AP(kint, 0, [[4 * W1, 128], [1, 4 * W1]])
            ve.tensor_single_scalar(t_, a_, 1.0 / math.pi, ALU.mult)
            ve.tensor_copy(k_, t_)
            ve.tensor_copy(t_, k_)
            ve.scalar_tensor_tensor(p_, t_, -math.pi, a_, ALU.mult, ALU.add)
            ve.tensor_single_scalar(p_, p_, 2.0, ALU.mult)
            ve.tensor_single_scalar(t_, p_, math.pi, ALU.is_gt)
            ve.scalar_tensor_tensor(p_, t_, -2.0 * math.pi, p_, ALU.mult, ALU.add)
            ve.tensor_single_scalar(t_, p_, -math.pi, ALU.is_lt)
            return ve.scalar_tensor_tensor(p_, t_, 2.0 * math.pi, p_,
                                           ALU.mult, ALU.add)
        kb.run("vector", "v", fold_psi)
        V_PSI = kb.cnt("v")
        kb.wait("scalar", "v", V_PSI)
        def lut_trig(se):
            se.activation(sinS[:], psi[:], AF.Sin)
            se.activation(schalf[:], psi[:], AF.Sin, scale=ccol(0.5))
            return se.square(schalf[:], schalf[:])
        kb.run("scalar", "s", lut_trig)
        S_TRIG = kb.cnt("s")
        kb.wait("vector", "s", S_TRIG)
        kb.run("vector", "v",
               lambda ve: ve.tensor_scalar(cosS[:], schalf[:], -2.0, 1.0,
                                           ALU.mult, ALU.add))
        V_TRIG = kb.cnt("v")
        dbgdump(5, X_s[:], 4 * W1)
        dbgdump(6, sinS[:], 4 * W1)
        dbgdump(7, cosS[:], 4 * W1)

        for k in range(8):
            stream_tile(dq_ld, wxT_scr, k, wx_buf, wx_use, wx_tok)
            dq_ld.wait("tensor", wx_tok[k % 2])
            if k == 0:
                for gb in range(8):
                    nm, c = banks[gb].cons
                    kb.wait("tensor", nm, c)
            def mmx(te, k=k):
                par = k % 2
                last = None
                for mi in range(8):
                    last = te.matmul(
                        banks[mi].ap[:, 0:CA],
                        wx_buf[:, par * 1024 + mi * 128: par * 1024 + (mi + 1) * 128],
                        xn[:, k * CA:(k + 1) * CA],
                        start=(k == 0), stop=(k == 7))
                return last
            kb.run("tensor", "t", mmx)
            wx_use[k % 2] = kb.cnt("t")
        for gb in range(8):
            banks[gb].prod = kb.cnt("t")
        def xccarry(ve, first=first):
            if first:
                ve.tensor_copy(strided4(xcr_s, 0, 1, W1),
                               bass.AP(h0_sb, 0, [[8, 128], [1, 4]]).unsqueeze(2))
                return ve.tensor_copy(strided4(xci_s, 0, 1, W1),
                                      bass.AP(h0_sb, 4, [[8, 128], [1, 4]]).unsqueeze(2))
            ve.tensor_copy(strided4(xcr_s, 0, 1, W1), strided4(xcr_s, CA, 1, W1))
            return ve.tensor_copy(strided4(xci_s, 0, 1, W1),
                                  strided4(xci_s, CA, 1, W1))
        kb.run("vector", "v", xccarry)
        def evx(ve):
            for gb in range(4):
                ve.tensor_copy(t4a[:, gb * CA:(gb + 1) * CA],
                               banks[gb].ap[:, 0:CA])
            last = None
            for gb in range(4):
                last = ve.tensor_copy(t4b[:, gb * CA:(gb + 1) * CA],
                                      banks[gb + 4].ap[:, 0:CA])
            return last
        kb.wait("vector", "t")
        kb.run("vector", "v", evx)
        for gb in range(8):
            banks[gb].cons = ("v", kb.cnt("v"))
        def mx_fn(ve):
            ve.tensor_tensor(m_a[:], t4a[:], t4a[:], ALU.mult)
            ve.tensor_tensor(ar_p[:], t4b[:], t4b[:], ALU.mult)
            return ve.tensor_tensor(m_a[:], m_a[:], ar_p[:], ALU.add)
        kb.run("vector", "v", mx_fn)
        V_MX = kb.cnt("v")
        kb.wait("scalar", "v", V_MX)
        def lut_sqx(se):
            se.activation(ar_p[:], m_a[:], AF.Ln)
            return se.activation(ar_p[:], ar_p[:], AF.Exp, scale=ccol(0.5))
        kb.run("scalar", "s", lut_sqx)
        S_SQX = kb.cnt("s")
        kb.wait("vector", "s", S_SQX)
        def xc_fn(ve):
            ve.tensor_tensor(
                strided4(xcr_s, 1, CA, W1), pack4(t4a), pack4(ar_p), ALU.mult)
            return ve.tensor_tensor(
                strided4(xci_s, 1, CA, W1), pack4(t4b), pack4(ar_p), ALU.mult)
        def pack4(t):
            return bass.AP(t, 0, [[4 * CA, 128], [CA, 4], [1, CA]])
        kb.run("vector", "v", xc_fn)
        V_XC = kb.cnt("v")
        dbgdump(8, xcr_s[:], 4 * W1)
        dbgdump(9, xci_s[:], 4 * W1)

        for gb in range(8):
            nm, c = banks[gb].cons
            kb.wait("tensor", nm, c)
        kb.wait("tensor", "v", V_XC)
        def mmu(te):
            last = None
            for comp in range(2):
                for g in range(4):
                    bk_ap = banks[4 * comp + g].ap
                    if comp == 0:
                        pairs = [(x_re, xcr_s), (x_imn, xci_s)]
                    else:
                        pairs = [(x_im, xcr_s), (x_re, xci_s)]
                    firstmm = True
                    for pi, (lhs, rhs) in enumerate(pairs):
                        for k in range(4):
                            last = te.matmul(
                                bk_ap[:, 0:CA],
                                lhs[:, k * INNER + g * 128: k * INNER + g * 128 + 128],
                                rhs[:, k * W1: k * W1 + CA],
                                start=firstmm, stop=(pi == 1 and k == 3))
                            firstmm = False
            return last
        kb.run("tensor", "t", mmu)
        for gb in range(8):
            banks[gb].prod = kb.cnt("t")
        T_U = kb.cnt("t")

        kb.wait("vector", "t", T_U)
        def dbuild(ve):
            last = None
            for comp in range(2):
                dst = d_re if comp == 0 else d_im
                for g in range(4):
                    u_re_ap = banks[g].ap[:, 0:CA]
                    u_im_ap = banks[4 + g].ap[:, 0:CA]
                    cS = cosS[:, g * W1: g * W1 + CA]
                    sS = sinS[:, g * W1: g * W1 + CA]
                    ta = t4a[:, g * CA:(g + 1) * CA]
                    tb = t4b[:, g * CA:(g + 1) * CA]
                    if comp == 0:
                        ve.tensor_tensor(ta, cS, u_re_ap, ALU.mult)
                        ve.tensor_tensor(tb, sS, u_im_ap, ALU.mult)
                        ve.tensor_tensor(ta, ta, tb, ALU.add)
                    else:
                        ve.tensor_tensor(ta, cS, u_im_ap, ALU.mult)
                        ve.tensor_tensor(tb, sS, u_re_ap, ALU.mult)
                        ve.tensor_tensor(ta, ta, tb, ALU.subtract)
                    last = ve.tensor_tensor(dst[:, g * CA:(g + 1) * CA], ta,
                                            rho[:, g * CA:(g + 1) * CA], ALU.mult)
            return last
        kb.run("vector", "v", dbuild)
        for gb in range(8):
            banks[gb].cons = ("v", kb.cnt("v"))
        dbgdump(10, d_re[:], 4 * CA)
        dbgdump(11, d_im[:], 4 * CA)

        def wcarry(ve, first=first):
            if first:
                ve.tensor_copy(strided4(w_re, 0, 1, W1), cbc(0.0, 4).unsqueeze(2))
                return ve.tensor_copy(strided4(w_im, 0, 1, W1),
                                      cbc(0.0, 4).unsqueeze(2))
            ve.tensor_copy(strided4(w_re, 0, 1, W1), strided4(w_re, CA, 1, W1))
            return ve.tensor_copy(strided4(w_im, 0, 1, W1),
                                  strided4(w_im, CA, 1, W1))
        kb.run("vector", "v", wcarry)
        kb.wait("vector", "v")
        def scans_w(ve):
            last = None
            for g in range(4):
                last = ve.tensor_tensor_scan(
                    w_re[:, g * W1 + 1: g * W1 + 1 + CA],
                    rho[:, g * CA:(g + 1) * CA], d_re[:, g * CA:(g + 1) * CA],
                    w_re[:, g * W1: g * W1 + 1], ALU.mult, ALU.add)
            for g in range(4):
                last = ve.tensor_tensor_scan(
                    w_im[:, g * W1 + 1: g * W1 + 1 + CA],
                    rho[:, g * CA:(g + 1) * CA], d_im[:, g * CA:(g + 1) * CA],
                    w_im[:, g * W1: g * W1 + 1], ALU.mult, ALU.add)
            return last
        kb.run("vector", "v", scans_w)
        dbgdump(12, w_re[:], 4 * W1)
        dbgdump(13, w_im[:], 4 * W1)

        def unrot(ve):
            cS = strided4(cosS, 1, CA, W1)
            sS = strided4(sinS, 1, CA, W1)
            wr = strided4(w_re, 1, CA, W1)
            wi = strided4(w_im, 1, CA, W1)
            ve.tensor_tensor(pack4(t4a), cS, wr, ALU.mult)
            ve.tensor_tensor(pack4(t4b), sS, wi, ALU.mult)
            ve.tensor_tensor(pack4(d_re), pack4(t4a), pack4(t4b), ALU.subtract)
            ve.tensor_tensor(pack4(t4a), cS, wi, ALU.mult)
            ve.tensor_tensor(pack4(t4b), sS, wr, ALU.mult)
            return ve.tensor_tensor(pack4(d_im), pack4(t4a), pack4(t4b), ALU.add)
        kb.run("vector", "v", unrot)
        V_S = kb.cnt("v")
        dbgdump(14, d_re[:], 4 * CA)
        dbgdump(15, d_im[:], 4 * CA)

        for gb in range(4):
            nm, c = banks[gb].cons
            kb.wait("tensor", nm, c)
        kb.wait("tensor", "v", V_S)
        def mmvs(te):
            last = None
            for g in range(4):
                bk_ap = banks[g].ap
                firstmm = True
                for pi, (lhs, rhs) in enumerate([(vT_re, d_re), (vT_imn, d_im)]):
                    for k in range(4):
                        last = te.matmul(
                            bk_ap[:, 0:CA],
                            lhs[:, k * INNER + g * 128: k * INNER + g * 128 + 128],
                            rhs[:, k * CA:(k + 1) * CA],
                            start=firstmm, stop=(pi == 1 and k == 3))
                        firstmm = False
            return last
        kb.run("tensor", "t", mmvs)
        for gb in range(4):
            banks[gb].prod = kb.cnt("t")
        kb.wait("vector", "t")
        def evvs(ve):
            last = None
            for g in range(4):
                last = ve.tensor_tensor(
                    hre[:, g * CA:(g + 1) * CA], banks[g].ap[:, 0:CA],
                    xcr_s[:, g * W1 + 1: g * W1 + 1 + CA], ALU.add)
            return last
        kb.run("vector", "v", evvs)
        for gb in range(4):
            banks[gb].cons = ("v", kb.cnt("v"))
        V_HRE = kb.cnt("v")
        dbgdump(16, hre[:], 4 * CA)

        for k in range(4):
            if not _wofull:
                stream_tile(dq_ld, woT_scr, k, wo_buf, wo_use, wo_tok)
                dq_ld.wait("tensor", wo_tok[k % 2])
            if k == 0:
                for gb in range(8):
                    nm, c = banks[gb].cons
                    kb.wait("tensor", nm, c)
                kb.wait("tensor", "v", V_HRE)
            def mmo(te, k=k):
                par = k if _wofull else k % 2
                last = None
                for mi in range(8):
                    last = te.matmul(
                        banks[mi].ap[:, 0:CA],
                        wo_buf[:, par * 1024 + mi * 128: par * 1024 + (mi + 1) * 128],
                        hre[:, k * CA:(k + 1) * CA],
                        start=(k == 0), stop=(k == 3))
                return last
            kb.run("tensor", "t", mmo)
            wo_use[k % 2] = kb.cnt("t")
        for gb in range(8):
            banks[gb].prod = kb.cnt("t")
        if debug and _chd is not None and ch == int(_chd):
            kb.wait(_steng, "s")
            kb.wait(_steng, "v")
            kb.wait(_steng, "t")
            dq_st.dma(dbg["dbg_pl"][19, :, 0:1024], hre[:, 0:1024])
            dq_st.dma(dbg["dbg_pl"][16, :, 0:2048], x_fm[:, 0:2048])
            dq_st.dma(dbg["dbg_xp"][0, :, 2048:4096], wo_buf[:, 0:2048])
            kb.wait("vector", "t")
            def _psevac(ve):
                last = None
                for _gb in range(8):
                    last = ve.tensor_copy(tmp8[:, _gb * CA:(_gb + 1) * CA],
                                          banks[_gb].ap[:, 0:CA])
                return last
            kb.run("vector", "v", _psevac)
            kb.wait(_steng, "v", kb.cnt("v"))
            dq_st.dma(dbg["dbg_xp"][1, :, 2048:4096], tmp8[:, 0:2048])
            tok_chd3 = dq_st.fence()
            dq_st.wait("vector", tok_chd3)
        kb.wait("vector", "t")
        def evo(ve):
            last = None
            for mi in range(8):
                last = ve.tensor_tensor(
                    x_fm[:, mi * CA:(mi + 1) * CA], x_fm[:, mi * CA:(mi + 1) * CA],
                    banks[mi].ap[:, 0:CA], ALU.add)
            return last
        kb.run("vector", "v", evo)
        for gb in range(8):
            banks[gb].cons = ("v", kb.cnt("v"))
        V_XP = kb.cnt("v")
        if ch == min(BPC * NCHS, PA_LIMIT) - 1:
            kb.wait("vector", "v", V_XP)
            kb.run("vector", "v",
                   lambda ve: ve.tensor_copy(xp_keep[:], x_fm[:]))
            V_XPKEEP = kb.cnt("v")
        if debug and _chd is not None and ch == int(_chd):
            kb.wait(_steng, "v", V_XP)
            dq_st.dma(dbg["dbg_xp"][2, :, 2048:4096], x_fm[:, 0:2048])
            tok_chd4 = dq_st.fence()
            dq_st.wait("vector", tok_chd4)

        kb.wait(_steng, "v", V_XP)
        for g in range(8):
            dq_st.dma(xp_scr[g, :, g0:g0 + CA], x_fm[:, g * CA:(g + 1) * CA])
            if debug:
                dq_st.dma(dbg["dbg_xp"][g, :, g0:g0 + CA],
                          x_fm[:, g * CA:(g + 1) * CA])
        tok_xp = dq_st.fence()
        dq_st.wait("vector", tok_xp)

    if stage == "passA":
        for dq in (dq_st, dq_ld, dq_gl):
            for tok in dq.final_tokens():
                dq.wait("gpsimd", tok)
        _finish(nc, kb)
        es_pa.close()
        es_vx.close()
        es.close()
        return nc, dbg

    if os.environ.get("FAKEP"):
        for gb in range(8):
            bk = banks[gb]
            def fake(te, bk=bk):
                last = None
                for k in range(4):
                    last = te.matmul(bk.ap[:, 0:CA],
                                     wo_buf[:, k * 128:(k + 1) * 128],
                                     hre[:, 0:CA],
                                     start=(k == 0), stop=(k == 3))
                return last
            bk.produce(fake)
            def fcons(ve, bk=bk):
                return ve.tensor_copy(tmp8[:, 0:CA], bk.ap[:, 0:CA])
            bk.consume("vector", fcons)

    _tpad = int(os.environ.get("TPAD", "0"))
    for _i in range(_tpad):
        kb.run("tensor", "t",
               lambda te, _i=_i: te.matmul(banks[6 + _i % 2].ap[:, 0:128],
                                           ident[:], ident[:],
                                           start=True, stop=True))
        if _i % 4 == 3:
            kb.wait("tensor", "t", kb.cnt("t"))

    if os.environ.get("SAC"):
        for g in range(8):
            dq_st.dma(xp_scr[g, :, 0:CA], x_fm[:, g * CA:(g + 1) * CA])
        dq_st.fence()
    _nflush = int(os.environ.get("NFLUSH", "0"))
    for _ in range(_nflush):
        dq_st.dma(sac_scr[:, 0:2], gsc[:, 0:2])
    if _nflush:
        dq_st.fence()
    ndelay = int(os.environ.get("PADELAY", "0"))
    for _ in range(ndelay):
        kb.run("vector", "v",
               lambda ve: ve.tensor_copy(tmp8[:, 0:2048], xn[:, 0:2048]))
    if ndelay:
        kb.wait("vector", "v", kb.cnt("v"))
    for tok in dq_st.final_tokens():
        for _e in BARRIER_ENGS:
            dq_st.wait(_e, tok)
    es_pa.close()
    es_vx.close()

    es_pb = ExitStack()
    def sbb(name, p, f):
        return es_pb.enter_context(nc.sbuf_tensor(name, [p, f], F32))
    w1g = sbb("w1g", 128, 2 * 512)
    w3g = sbb("w3g", 128, 2 * 512)
    w2k = sbb("w2k", 128, 2 * D)
    xn2 = sbb("xn2", 128, 8 * TB)
    xpb = sbb("xpb", 128, 8 * TB)
    y_sb = sbb("y_sb", 128, 8 * TB)
    h_buf = sbb("h_buf", 128, HT * TB)
    t1_sb = sbb("t1_sb", 128, 2 * TB)
    srow2 = sbb("srow2", 1, 4 * TB)
    rstd2 = sbb("rstd2", 128, TB)
    mrs2 = sbb("mrs2", 128, TB)
    rw_tm = sbb("rw_tm", 128, 4 * NEXP)
    rws = sbb("rws", NEXP, TB)
    rw_b = sbb("rw_b", 128, TB)
    selP = sbb("selP", NEXP, NEXP * 128)
    b2all = sbb("b2all", NEXP, D)
    sb2row = sbb("sb2row", 1, D)
    b1row = sbb("b1row", 1, 2 * HDIM)
    b3row = sbb("b3row", 1, 2 * HDIM)
    rt1 = sbb("rt1", 128, 4 * NEXP)
    rt2 = sbb("rt2", 128, 4 * NEXP)
    rrow = sbb("rrow", 128, 8)
    rt3 = sbb("rt3", 128, 4 * NEXP)
    rt4 = sbb("rt4", 128, 4 * NEXP)
    yt_sb = sbb("yt_sb", 128, TB)

    _pbsetup = int(os.environ.get("PBSETUP", "3"))
    if _pbsetup & 1:
        dq_gl.dma(b2all[:, :], b2_h[:, :])
        dq_gl.dma(sb2row[0:1, :], sb2_h.unsqueeze(0))
        tok_bias = dq_gl.fence()
        dq_gl.wait("tensor", tok_bias)

    if _pbsetup & 2:
        kb.run("gpsimd", "g", lambda e: e.iota(h_buf[0:NEXP, 0:NEXP * 128],
                                               [[1, NEXP], [0, 128]], base=0,
                                               channel_multiplier=0,
                                               allow_small_or_imprecise_dtypes=True))
        kb.run("gpsimd", "g", lambda e: e.iota(h_buf[0:NEXP, 1024:1024 + NEXP * 128],
                                               [[0, NEXP * 128]], base=0,
                                               channel_multiplier=1,
                                               allow_small_or_imprecise_dtypes=True))
        kb.wait("vector", "g")
        kb.run("vector", "v",
               lambda ve: ve.tensor_tensor(selP[:, :], h_buf[0:NEXP, 0:NEXP * 128],
                                           h_buf[0:NEXP, 1024:1024 + NEXP * 128],
                                           ALU.is_equal))
        V_SEL = kb.cnt("v")
        kb.wait("tensor", "v", V_SEL)
        kb.wait("vector", "v", V_SEL)
        kb.wait("scalar", "v", V_SEL)

    w13_use = [0, 0]
    w13_tok = [None, None]
    w13_i = [0]
    w2_use = [0, 0]
    w2_tok = [None, None]
    w2_i = [0]
    bias_use = [0, 0]
    bias_tok = [None, None]

    def load_bias(e):
        par = e % 2
        kb.wait("gpsimd", "t", bias_use[par])
        if e < NEXP:
            dq_gl.dma(b1row[0:1, par * HDIM:(par + 1) * HDIM],
                      b1_h[e].unsqueeze(0))
            dq_gl.dma(b3row[0:1, par * HDIM:(par + 1) * HDIM],
                      b3_h[e].unsqueeze(0))
        else:
            dq_gl.dma(b1row[0:1, par * HDIM:(par + 1) * HDIM],
                      sb1_h.unsqueeze(0))
            dq_gl.dma(b3row[0:1, par * HDIM:(par + 1) * HDIM],
                      sb3_h.unsqueeze(0))
        bias_tok[par] = dq_gl.fence()

    MGRP = [(0, 4), (4, 4), (8, 3)]

    def stream_w13(e, m0, nm, k):
        par = w13_i[0] % 2
        w13_i[0] += 1
        if w13_use[par] > 0:
            kb.wait("sync", "t", w13_use[par])
        c0, c1 = m0 * 128, (m0 + nm) * 128
        if e < NEXP:
            dq_ld.dma(w1g[:, par * 512: par * 512 + (c1 - c0)],
                      w1_h[e, k * 128:(k + 1) * 128, c0:c1])
            dq_ld.dma(w3g[:, par * 512: par * 512 + (c1 - c0)],
                      w3_h[e, k * 128:(k + 1) * 128, c0:c1])
        else:
            dq_ld.dma(w1g[:, par * 512: par * 512 + (c1 - c0)],
                      sw1_h[k * 128:(k + 1) * 128, c0:c1])
            dq_ld.dma(w3g[:, par * 512: par * 512 + (c1 - c0)],
                      sw3_h[k * 128:(k + 1) * 128, c0:c1])
        w13_tok[par] = dq_ld.fence()
        return par

    def stream_w2(e, k):
        par = w2_i[0] % 2
        w2_i[0] += 1
        if w2_use[par] > 0:
            kb.wait("sync", "t", w2_use[par])
        if e < NEXP:
            dq_ld.dma(w2k[:, par * D:(par + 1) * D],
                      w2_h[e, k * 128:(k + 1) * 128, :])
        else:
            dq_ld.dma(w2k[:, par * D:(par + 1) * D],
                      sw2_h[k * 128:(k + 1) * 128, :])
        w2_tok[par] = dq_ld.fence()
        return par

    tok_out_prev = [None]
    _pblim = int(os.environ.get("PBLIM", str(10**9)))
    for tb in range(min(NTB, _pblim)):
        g0 = tb * TB
        if debug and os.environ.get("TRKDUMP"):
            kb.wait("gpsimd", "v")
            for g in range(8):
                dq_gl.dma(dbg["dbg_xp"][g, :, tb * 256:(tb + 1) * 256],
                          xp_scr[g, :, 3840:4096])
            dq_gl.fence()
        if tok_out_prev[0] is not None:
            dq_st.wait("gpsimd", tok_out_prev[0])
        for g in range(8):
            dq_gl.dma(xpb[:, g * TB:(g + 1) * TB], xp_scr[g, :, g0:g0 + TB])
        tok_xp = dq_gl.fence()
        dq_gl.wait("vector", tok_xp)
        dq_gl.wait("scalar", tok_xp)
        dq_gl.wait("tensor", tok_xp)
        if tb == NTB - 1:
            def patch_xpb(ve):
                dst = bass.AP(xpb, CA, [[8 * TB, 128], [TB, 8], [1, CA]])
                srcp = bass.AP(xp_keep, 0, [[8 * CA, 128], [CA, 8], [1, CA]])
                return ve.tensor_copy(dst, srcp)
            kb.run("vector", "v", patch_xpb)
            V_PB_PATCH = kb.cnt("v")
            kb.wait("scalar", "v", V_PB_PATCH)
            kb.wait("tensor", "v", V_PB_PATCH)

        kb.wait("scalar", "t")
        def sq2(se):
            return se.square(h_buf[:, 0:8 * TB], xpb[:])
        kb.run("scalar", "s", sq2)
        S_SQ2 = kb.cnt("s")
        bk6, bk7 = banks[6], banks[7]
        def sums2(te, bk=bk6):
            last = None
            for k in range(8):
                last = te.matmul(bk.ap[0:1, 0:TB], ccol(1.0),
                                 xpb[:, k * TB:(k + 1) * TB],
                                 start=(k == 0), stop=(k == 7))
            return last
        bk6.produce(sums2)
        kb.wait("tensor", "s", S_SQ2)
        def sumsq2(te, bk=bk7):
            last = None
            for k in range(8):
                last = te.matmul(bk.ap[0:1, 0:TB], ccol(1.0),
                                 h_buf[:, k * TB:(k + 1) * TB],
                                 start=(k == 0), stop=(k == 7))
            return last
        bk7.produce(sumsq2)
        def stats2_a(ve, bk=bk6, bq=bk7):
            ve.tensor_single_scalar(srow2[0:1, 0:TB], bk.ap[0:1, 0:TB],
                                    1.0 / D, ALU.mult)
            return ve.tensor_single_scalar(srow2[0:1, TB:2 * TB],
                                           bq.ap[0:1, 0:TB], 1.0 / D, ALU.mult)
        kb.wait("vector", "t", bk7.prod)
        bk6.consume("vector", stats2_a)
        bk7.cons = ("v", kb.cnt("v"))
        kb.wait("vector", "v")
        kb.run("vector", "v",
               lambda ve: ve.tensor_tensor(srow2[0:1, 2 * TB:3 * TB],
                                           srow2[0:1, 0:TB], srow2[0:1, 0:TB],
                                           ALU.mult))
        kb.wait("vector", "v")
        kb.run("vector", "v",
               lambda ve: ve.tensor_tensor(srow2[0:1, TB:2 * TB],
                                           srow2[0:1, TB:2 * TB],
                                           srow2[0:1, 2 * TB:3 * TB],
                                           ALU.subtract))
        kb.wait("scalar", "v")
        kb.run("scalar", "s",
               lambda se: se.activation(srow2[0:1, 2 * TB:3 * TB],
                                        srow2[0:1, TB:2 * TB], AF.Ln,
                                        bias=crow_ap(EPS)))
        kb.wait("scalar", "s")
        kb.run("scalar", "s",
               lambda se: se.activation(srow2[0:1, 2 * TB:3 * TB],
                                        srow2[0:1, 2 * TB:3 * TB], AF.Exp,
                                        scale=crow_ap(-0.5)))
        S_R2 = kb.cnt("s")
        kb.wait("vector", "s", S_R2)
        kb.run("vector", "v",
               lambda ve: ve.tensor_tensor(srow2[0:1, 3 * TB:4 * TB],
                                           srow2[0:1, 0:TB],
                                           srow2[0:1, 2 * TB:3 * TB], ALU.mult))
        V_ST2 = kb.cnt("v")
        def bcr(te, bk=bk6):
            return te.matmul(bk.ap[:, 0:TB], ones_row[0:1, 0:128],
                             srow2[0:1, 2 * TB:3 * TB], start=True, stop=True)
        bk6.produce(bcr, waitv=V_ST2)
        def bcm(te, bk=bk7):
            return te.matmul(bk.ap[:, 0:TB], ones_row[0:1, 0:128],
                             srow2[0:1, 3 * TB:4 * TB], start=True, stop=True)
        bk7.produce(bcm)
        def evb2(ve, bk=bk6, bq=bk7):
            ve.tensor_copy(rstd2[:], bk.ap[:, 0:TB])
            return ve.tensor_copy(mrs2[:], bq.ap[:, 0:TB])
        kb.wait("vector", "t", bk7.prod)
        bk6.consume("vector", evb2)
        bk7.cons = ("v", kb.cnt("v"))
        kb.wait("vector", "v", kb.cnt("v"))
        def apply_ln2(ve):
            rb = bass.AP(rstd2, 0, [[TB, 128], [0, 8], [1, TB]])
            mb = bass.AP(mrs2, 0, [[TB, 128], [0, 8], [1, TB]])
            x8 = bass.AP(xpb, 0, [[8 * TB, 128], [TB, 8], [1, TB]])
            xn8 = bass.AP(xn2, 0, [[8 * TB, 128], [TB, 8], [1, TB]])
            ve.tensor_tensor(xn8, x8, rb, ALU.mult)
            return ve.tensor_tensor(xn8, xn8, mb, ALU.subtract)
        kb.run("vector", "v", apply_ln2)
        V_XN2 = kb.cnt("v")

        def bdump(idx, ap, width, tb=tb):
            import os as _os3
            if not (debug and tb == int(_os3.environ.get("BDTB", "0"))):
                return
            kb.wait(_steng, "v")
            kb.wait(_steng, "s")
            dq_st.dma(dbg["dbg_pl"][idx, :, 0:width], ap)
            tok = dq_st.fence()
            dq_st.wait("vector", tok)
            dq_st.wait("scalar", tok)
        bdump(0, xpb[:, 0:2048], 2048)
        bdump(1, xn2[:, 0:2048], 2048)

        for tt in range(4):
            bk = banks[tt]
            def logits(te, tt=tt, bk=bk):
                last = None
                for k in range(8):
                    last = te.matmul(
                        bk.ap[:, 0:NEXP],
                        xn2[:, k * TB + tt * 128: k * TB + (tt + 1) * 128],
                        gwT[:, k * NEXP:(k + 1) * NEXP],
                        start=(k == 0), stop=(k == 7))
                return last
            bk.produce(logits, waitv=V_XN2)
        def router_m1(ve):
            last = None
            for tt in range(4):
                last = ve.reduce_max(rrow[:, tt:tt + 1],
                                     banks[tt].ap[:, 0:NEXP], axis=AXX)
            return last
        kb.wait("vector", "t")
        kb.run("vector", "v", router_m1)
        kb.wait("vector", "v")
        def router_v1(ve):
            last = None
            for tt in range(4):
                last = ve.tensor_scalar(rt1[:, tt * 8:(tt + 1) * 8],
                                        banks[tt].ap[:, 0:NEXP],
                                        rrow[:, tt:tt + 1], None, ALU.subtract)
            return last
        kb.run("vector", "v", router_v1)
        for tt in range(4):
            banks[tt].cons = ("v", kb.cnt("v"))
        bdump(7, rt1[:, 0:32], 32)
        bdump(8, rrow[:, 0:8], 8)
        kb.wait("scalar", "v")
        kb.run("scalar", "s",
               lambda se: se.activation(rt1[:], rt1[:], AF.Exp))
        S_REXP = kb.cnt("s")
        kb.wait("vector", "s", S_REXP)
        def router_v2a(ve):
            last = None
            for tt in range(4):
                last = ve.reduce_sum(rrow[:, 4 + tt:5 + tt],
                                     rt1[:, tt * 8:(tt + 1) * 8], axis=AXX)
            return last
        kb.run("vector", "v", router_v2a)
        kb.wait("vector", "v")
        kb.run("vector", "v",
               lambda ve: ve.tensor_copy(rt2[:, 0:4], rrow[:, 4:8]))
        V_RS = kb.cnt("v")
        kb.wait("scalar", "v", V_RS)
        kb.run("scalar", "s",
               lambda se: se.activation(rt2[:, 0:4], rt2[:, 0:4], AF.Ln))
        kb.wait("scalar", "s")
        kb.run("scalar", "s",
               lambda se: se.activation(rt2[:, 0:4], rt2[:, 0:4], AF.Exp,
                                        scale=ccol(-1.0)))
        S_RINV = kb.cnt("s")
        kb.wait("vector", "s", S_RINV)
        bdump(9, rt2[:, 0:8], 8)
        bdump(10, rt1[:, 0:32], 32)
        def phase(fn):
            kb.run("vector", "v", fn)
            kb.wait("vector", "v")

        def r_score(ve):
            last = None
            for tt in range(4):
                sl = slice(tt * 8, (tt + 1) * 8)
                last = ve.tensor_scalar(rt1[:, sl], rt1[:, sl],
                                        rt2[:, tt:tt + 1], None, ALU.mult)
            for tt in range(4):
                sl = slice(tt * 8, (tt + 1) * 8)
                last = ve.tensor_tensor(rw_tm[:, sl], rt1[:, sl], gbb[:],
                                        ALU.add)
            for tt in range(4):
                sl = slice(tt * 8, (tt + 1) * 8)
                last = ve.reduce_max(rrow[:, tt:tt + 1], rw_tm[:, sl], axis=AXX)
            return last
        phase(r_score)
        bdump(11, rt1[:, 0:32], 32)
        def r_eq(ve):
            last = None
            for tt in range(4):
                sl = slice(tt * 8, (tt + 1) * 8)
                last = ve.tensor_scalar(rt3[:, sl], rw_tm[:, sl],
                                        rrow[:, tt:tt + 1], None, ALU.is_equal)
            for tt in range(4):
                sl = slice(tt * 8, (tt + 1) * 8)
                last = ve.scalar_tensor_tensor(rt4[:, sl], rt3[:, sl], -1e30,
                                               rw_tm[:, sl], ALU.mult, ALU.add)
            for tt in range(4):
                sl = slice(tt * 8, (tt + 1) * 8)
                last = ve.reduce_max(rrow[:, 4 + tt:5 + tt], rt4[:, sl],
                                     axis=AXX)
            return last
        phase(r_eq)
        def r_mask(ve):
            last = None
            for tt in range(4):
                sl = slice(tt * 8, (tt + 1) * 8)
                last = ve.tensor_scalar(rt3[:, sl], rw_tm[:, sl],
                                        rrow[:, 4 + tt:5 + tt], None, ALU.is_ge)
            for tt in range(4):
                sl = slice(tt * 8, (tt + 1) * 8)
                last = ve.tensor_tensor(rw_tm[:, sl], rt1[:, sl],
                                        rt3[:, sl], ALU.mult)
            return last
        kb.run("vector", "v", r_mask)
        V_RW = kb.cnt("v")
        bdump(2, rw_tm[:, 0:32], 32)
        bdump(3, rt1[:, 0:32], 32)

        bk4 = banks[4]
        def rwt(te, bk=bk4):
            last = None
            for tt in range(4):
                last = te.transpose(bk.ap[0:NEXP, tt * 128:(tt + 1) * 128],
                                    rw_tm[:, tt * 8:(tt + 1) * 8],
                                    ident[:])
            return last
        bk4.produce(rwt, waitv=V_RW)
        bk4.consume("vector",
                    lambda ve, bk=bk4: ve.tensor_copy(rws[:, :],
                                                      bk.ap[0:NEXP, 0:TB]))
        V_RWS = kb.cnt("v")

        for j in range(8):
            bk = banks[4 + j % 2]
            def yinit(te, j=j, bk=bk):
                te.matmul(bk.ap[:, 0:TB], b2all[:, j * 128:(j + 1) * 128],
                          rws[:, :], start=True, stop=False)
                return te.matmul(bk.ap[:, 0:TB],
                                 sb2row[0:1, j * 128:(j + 1) * 128],
                                 ones_row[0:1, 0:TB], start=False, stop=True)
            bk.produce(yinit, waitv=V_RWS)
            def evy0(ve, j=j, bk=bk):
                return ve.tensor_copy(y_sb[:, j * TB:(j + 1) * TB],
                                      bk.ap[:, 0:TB])
            bk.consume("vector", evy0)

        for e in range(NEXP + 1):
            shared = (e == NEXP)
            if not shared:
                bk5 = banks[5]
                def rwb(te, e=e, bk=bk5):
                    return te.matmul(bk.ap[:, 0:TB],
                                     selP[:, e * 128:(e + 1) * 128],
                                     rws[:, :], start=True, stop=True)
                bk5.produce(rwb)
                bk5.consume("vector",
                            lambda ve, bk=bk5: ve.tensor_copy(rw_b[:],
                                                              bk.ap[:, 0:TB]))
            V_RWB = kb.cnt("v")

            if e == 0:
                load_bias(0)
            _bpar = e % 2
            dq_gl.wait("tensor", bias_tok[_bpar])

            for m0, nm in MGRP:
                for mi in range(nm):
                    for gb in (banks[mi], banks[4 + mi]):
                        nmn, c = gb.cons
                        kb.wait("tensor", nmn, c)
                if m0 == 0 and e == 0:
                    kb.wait("tensor", "v", V_XN2)
                for k in range(8):
                    par = stream_w13(e, m0, nm, k)
                    dq_ld.wait("tensor", w13_tok[par])
                    def mmh(te, k=k, nm=nm, par=par):
                        last = None
                        for mi in range(nm):
                            last = te.matmul(
                                banks[mi].ap[:, 0:TB],
                                w1g[:, par * 512 + mi * 128:
                                    par * 512 + (mi + 1) * 128],
                                xn2[:, k * TB:(k + 1) * TB],
                                start=(k == 0), stop=False)
                        for mi in range(nm):
                            last = te.matmul(
                                banks[4 + mi].ap[:, 0:TB],
                                w3g[:, par * 512 + mi * 128:
                                    par * 512 + (mi + 1) * 128],
                                xn2[:, k * TB:(k + 1) * TB],
                                start=(k == 0), stop=False)
                        return last
                    kb.run("tensor", "t", mmh)
                    w13_use[par] = kb.cnt("t")
                def mmbias(te, m0=m0, nm=nm, _bpar=_bpar):
                    last = None
                    for mi in range(nm):
                        m = m0 + mi
                        last = te.matmul(banks[mi].ap[:, 0:TB],
                                         b1row[0:1, _bpar * HDIM + m * 128:
                                               _bpar * HDIM + (m + 1) * 128],
                                         ones_row[0:1, 0:TB],
                                         start=False, stop=True)
                        last = te.matmul(banks[4 + mi].ap[:, 0:TB],
                                         b3row[0:1, _bpar * HDIM + m * 128:
                                               _bpar * HDIM + (m + 1) * 128],
                                         ones_row[0:1, 0:TB],
                                         start=False, stop=True)
                    return last
                kb.run("tensor", "t", mmbias)
                bias_use[_bpar] = kb.cnt("t")
                for mi in range(nm):
                    banks[mi].prod = kb.cnt("t")
                    banks[4 + mi].prod = kb.cnt("t")
                for mi in range(nm):
                    m = m0 + mi
                    par2 = mi % 2
                    def evh1(se, mi=mi, par2=par2):
                        return se.copy(t1_sb[:, par2 * TB:(par2 + 1) * TB],
                                       banks[mi].ap[:, 0:TB])
                    banks[mi].consume("scalar", evh1)
                    S_T1 = kb.cnt("s")
                    kb.wait("vector", "s", S_T1)
                    def evh2(ve, mi=mi, par2=par2):
                        return ve.tensor_tensor(
                            t1_sb[:, par2 * TB:(par2 + 1) * TB],
                            t1_sb[:, par2 * TB:(par2 + 1) * TB],
                            banks[4 + mi].ap[:, 0:TB], ALU.mult)
                    banks[4 + mi].consume("vector", evh2)
                    V_T1 = kb.cnt("v")
                    kb.wait("scalar", "v", V_T1)
                    def evh3(se, m=m, par2=par2):
                        return se.activation(h_buf[:, m * TB:(m + 1) * TB],
                                             t1_sb[:, par2 * TB:(par2 + 1) * TB],
                                             AF.Silu)
                    kb.run("scalar", "s", evh3)
            S_H = kb.cnt("s")

            if e < NEXP:
                load_bias(e + 1)

            for gb in range(8):
                nmn, c = banks[gb].cons
                kb.wait("tensor", nmn, c)
            kb.wait("tensor", "s", S_H)
            for k in range(HT):
                par = stream_w2(e, k)
                dq_ld.wait("tensor", w2_tok[par])
                def mmw2(te, k=k, par=par):
                    last = None
                    for j in range(8):
                        last = te.matmul(
                            banks[j].ap[:, 0:TB],
                            w2k[:, par * D + j * 128: par * D + (j + 1) * 128],
                            h_buf[:, k * TB:(k + 1) * TB],
                            start=(k == 0), stop=(k == HT - 1))
                    return last
                kb.run("tensor", "t", mmw2)
                w2_use[par] = kb.cnt("t")
            for j in range(8):
                banks[j].prod = kb.cnt("t")
            kb.wait("vector", "t")
            if shared:
                def evy(ve):
                    last = None
                    for j in range(8):
                        last = ve.tensor_tensor(
                            y_sb[:, j * TB:(j + 1) * TB],
                            y_sb[:, j * TB:(j + 1) * TB],
                            banks[j].ap[:, 0:TB], ALU.add)
                    return last
                kb.run("vector", "v", evy)
            else:
                def evy(ve):
                    last = None
                    for j in range(8):
                        ve.tensor_tensor(yt_sb[:], banks[j].ap[:, 0:TB],
                                         rw_b[:], ALU.mult)
                        last = ve.tensor_tensor(
                            y_sb[:, j * TB:(j + 1) * TB],
                            y_sb[:, j * TB:(j + 1) * TB],
                            yt_sb[:], ALU.add)
                    return last
                kb.run("vector", "v", evy)
            for j in range(8):
                banks[j].cons = ("v", kb.cnt("v"))
            if e == 0:
                bdump(4, h_buf[:, 0:2048], 2048)
                bdump(5, y_sb[:, 0:2048], 2048)
            if e == NEXP:
                bdump(6, y_sb[:, 0:2048], 2048)
        V_Y = kb.cnt("v")

        kb.wait("gpsimd", "t")
        kb.wait("gpsimd", "s")
        for g in range(8):
            dq_gl.dma(h_buf[:, g * TB:(g + 1) * TB], xp_scr[g, :, g0:g0 + TB])
        tok_xp2 = dq_gl.fence()
        dq_gl.wait("vector", tok_xp2)
        dq_gl.wait("gpsimd", tok_xp2)
        bdump(17, h_buf[:, 0:2048], 2048)
        if tb == NTB - 1:
            def patch_hbuf(ve):
                dst = bass.AP(h_buf, CA, [[HT * TB, 128], [TB, 8], [1, CA]])
                srcp = bass.AP(xp_keep, 0, [[8 * CA, 128], [CA, 8], [1, CA]])
                return ve.tensor_copy(dst, srcp)
            kb.run("vector", "v", patch_hbuf)
        def addres(ve):
            return ve.tensor_tensor(h_buf[:, 0:8 * TB], h_buf[:, 0:8 * TB],
                                    y_sb[:, 0:8 * TB], ALU.add)
        kb.run("vector", "v", addres)
        V_OUT = kb.cnt("v")
        bdump(18, h_buf[:, 0:2048], 2048)
        for tt in range(4):
            for half in range(2):
                bk = banks[6 + half]
                def tfo(te, tt=tt, half=half, bk=bk):
                    last = None
                    for q in range(4):
                        g = half * 4 + q
                        last = te.transpose(
                            bk.ap[:, q * 128:(q + 1) * 128],
                            h_buf[:, g * TB + tt * 128: g * TB + (tt + 1) * 128],
                            ident[:])
                    return last
                bk.produce(tfo, waitv=V_OUT)
                def evt(ve, tt=tt, half=half, bk=bk):
                    return ve.tensor_copy(
                        xpb[:, tt * 1024 + half * 512: tt * 1024 + (half + 1) * 512],
                        bk.ap[:, 0:512])
                bk.consume("vector", evt)
        V_TM = kb.cnt("v")
        kb.wait(_steng, "v", V_TM)
        bdump(19, xpb[:, 2048:4096], 2048)
        for tt in range(4):
            dq_st.dma(out_flat[g0 + tt * 128: g0 + (tt + 1) * 128, :],
                      xpb[:, tt * 1024:(tt + 1) * 1024])
        tok_out = dq_st.fence()
        dq_st.wait("vector", tok_out)
        tok_out_prev[0] = tok_out

    import os as _os4
    if debug and _os4.environ.get("XPDUMP"):
        for tok in dq_st.final_tokens():
            dq_st.wait("gpsimd", tok)
        for g in range(8):
            dq_st.dma(dbg["dbg_xp"][g, :, :], xp_scr[g, :, :])
        dq_st.fence()

    if debug and os.environ.get("OCOPY"):
        for tok in dq_st.final_tokens():
            dq_st.wait("gpsimd", tok)
        for tt in range(4):
            dq_st.dma(dbg["dbg_xp"][4 + tt, :, 0:1024],
                      out_flat[3584 + tt * 128: 3584 + (tt + 1) * 128, :])
        dq_st.fence()

    for dq in (dq_st, dq_ld, dq_gl):
        for tok in dq.final_tokens():
            dq.wait("gpsimd", tok)
    _edelay = int(os.environ.get("ENDDELAY", "0"))
    if _edelay:
        kb.wait("vector", "g")
        for _ in range(_edelay):
            kb.run("vector", "v", lambda ve: ve.tensor_copy(gsc[:, 0:1],
                                                            consts[:, 0:1]))
        kb.wait("gpsimd", "v", kb.cnt("v"))
    _finish(nc, kb)
    es_pb.close()
    es.close()
    return nc, dbg


def _finish(nc, kb):
    with nc.Block() as block:
        @block.sync
        def _(eng):
            for e, fn in kb.plan:
                if e == "sync":
                    fn(eng)

        @block.vector
        def _(eng):
            for e, fn in kb.plan:
                if e == "vector":
                    fn(eng)

        @block.scalar
        def _(eng):
            for e, fn in kb.plan:
                if e == "scalar":
                    fn(eng)

        @block.tensor
        def _(eng):
            for e, fn in kb.plan:
                if e == "tensor":
                    fn(eng)

        @block.gpsimd
        def _(eng):
            for e, fn in kb.plan:
                if e == "gpsimd":
                    fn(eng)


_NAME_MAP = [
    ("x", "L0_attn_x"), ("w_a", "L0_attn_w_a"), ("w_x", "L0_attn_w_x"),
    ("w_out", "L0_attn_w_out"), ("v_re", "L0_attn_mat_v_real"),
    ("v_im", "L0_attn_mat_v_imag"), ("h0_re", "L0_attn_last_hidden_real"),
    ("h0_im", "L0_attn_last_hidden_imag"), ("gate_w", "L0_ffn_gate_w"),
    ("gate_b", "L0_ffn_gate_b"), ("w1", "L0_ffn_w1"), ("b1", "L0_ffn_b1"),
    ("w3", "L0_ffn_w3"), ("b3", "L0_ffn_b3"), ("w2", "L0_ffn_w2"),
    ("b2", "L0_ffn_b2"), ("sw1", "L0_ffn_sw1"), ("sb1", "L0_ffn_sb1"),
    ("sw3", "L0_ffn_sw3"), ("sb3", "L0_ffn_sb3"), ("sw2", "L0_ffn_sw2"),
    ("sb2", "L0_ffn_sb2"),
]


def kernel(**inputs):
    nc, _ = build_program(stage="full", debug=False)
    reps = {k: np.ascontiguousarray(np.asarray(inputs[v], dtype=np.float32))
            for k, v in _NAME_MAP if k != "x"}
    xfull = np.asarray(inputs["L0_attn_x"], dtype=np.float32)
    shards = np.split(np.ascontiguousarray(xfull), NCORES, axis=0)
    in_maps = [dict(reps, x=np.ascontiguousarray(s)) for s in shards]
    res = run_bass_kernel_spmd(nc, in_maps, core_ids=list(range(NCORES)))
    return np.concatenate([r["out"] for r in res.results], axis=0)


if __name__ == "__main__":
    nc, dbg = build_program(stage="newton", debug=True)
    n = sum(1 for _ in nc.all_instructions())
    print("built newton stage, instructions:", n)


# version 10
# speedup vs baseline: 1.0182x; 1.0063x over previous
import math
import os
from contextlib import ExitStack

import numpy as np
import concourse.bass as bass
import concourse.mybir as mybir
from concourse.bass_utils import run_bass_kernel_spmd

F32 = mybir.dt.float32
F32R = mybir.dt.float32r
AF = mybir.ActivationFunctionType
ALU = mybir.AluOpType
AXX = mybir.AxisListType.X

B, L, D = 16, 2048, 1024
INNER = 512
HDIM = 1408
NEXP = 8
NCORES = 8
BPC = B // NCORES
TPC = BPC * L
CA = 256
NCHS = L // CA
TB = 512
NTB = TPC // TB
EPS = 1e-5
NIT_R = 36
NIT_F = 6
PA_LIMIT = 10**9
WSCR_PARTS = 3
HT = HDIM // 128

LN_INV102 = float(-math.log(1.02))

CONST_VALS = [1.0, 2.0, -1.0, 0.5, -0.5, 2.0 * math.pi, math.pi,
              -2.0 * math.pi, EPS, 1e-30, 1.0 / D, -1e30, LN_INV102, 0.0]
CIDX = {v: i for i, v in enumerate(CONST_VALS)}


class KB:

    def __init__(self, nc, es):
        self.nc = nc
        self.plan = []
        self.sem = {}
        for nm, step in [("t", 1), ("v", 1), ("s", 1), ("g", 1), ("y", 1),
                         ("sy", 1)]:
            h = es.enter_context(nc.semaphore("sem_" + nm))
            self.sem[nm] = [h, step, 0]

    def emit(self, eng, fn):
        self.plan.append((eng, fn))

    def cnt(self, nm):
        return self.sem[nm][2]

    def wait(self, eng, nm, val=None):
        s = self.sem[nm]
        v = s[2] if val is None else val
        if v > 0:
            self.emit(eng, lambda e, h=s[0], v=v: e.wait_ge(h, v))

    def run(self, eng, nm, fn):
        s = self.sem[nm]
        s[2] += s[1]
        def wrapped(e, fn=fn, h=s[0], step=s[1]):
            ins = fn(e)
            assert ins is not None, "run() closure must return an instruction"
            ins.then_inc(h, step)
        self.emit(eng, wrapped)


class DQ:

    NSEM = 4

    def __init__(self, kb, es, name, eng):
        self.kb = kb
        self.eng = eng
        self.name = name
        self.sems = [es.enter_context(kb.nc.semaphore(f"dq_{name}{i}"))
                     for i in range(self.NSEM)]
        self.count = [0] * self.NSEM
        self.pend = [None] * self.NSEM
        self.cur = 0
        self.dirty = False

    def dma(self, out_ap, in_ap):
        i = self.cur
        if not self.dirty and self.pend[i] is not None:
            tgt = self.pend[i]
            self.kb.emit(self.eng,
                         lambda e, h=self.sems[i], v=tgt: e.wait_ge(h, v))
            self.pend[i] = None
        self.dirty = True
        self.count[i] += 16
        h = self.sems[i]
        self.kb.emit(self.eng, lambda e, o=out_ap, a=in_ap, h=h:
                     e.dma_start(out=o, in_=a).then_inc(h, 16))

    def fence(self):
        assert self.dirty, "fence with no dmas"
        i = self.cur
        tok = (self.sems[i], self.count[i])
        self.pend[i] = self.count[i]
        self.cur = (i + 1) % self.NSEM
        self.dirty = False
        return tok

    def wait(self, eng, tok):
        h, v = tok
        self.kb.emit(eng, lambda e, h=h, v=v: e.wait_ge(h, v))

    def final_tokens(self):
        out = []
        for i in range(self.NSEM):
            if self.count[i]:
                out.append((self.sems[i], self.count[i]))
        return out


class Bank:

    def __init__(self, kb, ap, idx):
        self.kb = kb
        self.ap = ap
        self.idx = idx
        self.prod = 0
        self.cons = ("v", 0)

    def produce(self, fn, waitv=None):
        kb = self.kb
        nm, c = self.cons
        kb.wait("tensor", nm, c)
        if waitv is not None:
            kb.wait("tensor", "v", waitv)
        kb.run("tensor", "t", fn)
        self.prod = kb.cnt("t")

    def consume(self, eng, fn, guard=None):
        kb = self.kb
        kb.wait(eng, "t", self.prod)
        nm = {"vector": "v", "scalar": "s", "gpsimd": "g"}[eng]
        if guard is not None:
            ap = self.ap
            if eng == "vector":
                kb.run(eng, nm,
                       lambda ve, ap=ap, g=guard: ve.tensor_copy(g, ap[:, 0:1]))
            else:
                kb.run(eng, nm,
                       lambda se, ap=ap, g=guard: se.square(g, ap[:, 0:1]))
            kb.wait(eng, nm, kb.cnt(nm))
        kb.run(eng, nm, fn)
        self.cons = (nm, kb.cnt(nm))


BARRIER_ENGS = ("tensor", "vector", "scalar", "gpsimd", "sync")


def barrier(kb):
    targets = {}
    for eng, nm in (("vector", "v"), ("scalar", "s"), ("gpsimd", "g"),
                    ("sync", "sy")):
        def bump(e, eng=eng):
            if eng in ("vector", "gpsimd"):
                return e.engine_nop()
            return e.nop()
        kb.run(eng, nm, bump)
        targets[eng] = (nm, kb.cnt(nm))
    targets["tensor"] = ("t", kb.cnt("t"))
    for eng in ("tensor", "vector", "scalar", "gpsimd", "sync"):
        for other, (nm, v) in targets.items():
            if other != eng:
                kb.wait(eng, nm, v)


def build_program(stage="full", debug=False):
    nc = bass.Bass("TRN2", target_bir_lowering=False,
                   detect_race_conditions=bool(
                       os.environ.get("RACEDET")))
    es = ExitStack()
    kb = KB(nc, es)

    def din(name, shape):
        return nc.dram_tensor(name, shape, F32, kind="ExternalInput").ap()

    x_in = din("x", [BPC, L, D])
    w_a_h = din("w_a", [2 * INNER, D])
    w_x_h = din("w_x", [2 * INNER, D])
    w_out_h = din("w_out", [D, INNER])
    v_re_h = din("v_re", [INNER, INNER])
    v_im_h = din("v_im", [INNER, INNER])
    h0_re_h = din("h0_re", [INNER])
    h0_im_h = din("h0_im", [INNER])
    gate_w_h = din("gate_w", [NEXP, D])
    gate_b_h = din("gate_b", [NEXP])
    w1_h = din("w1", [NEXP, D, HDIM])
    b1_h = din("b1", [NEXP, HDIM])
    w3_h = din("w3", [NEXP, D, HDIM])
    b3_h = din("b3", [NEXP, HDIM])
    w2_h = din("w2", [NEXP, HDIM, D])
    b2_h = din("b2", [NEXP, D])
    sw1_h = din("sw1", [D, HDIM])
    sb1_h = din("sb1", [HDIM])
    sw3_h = din("sw3", [D, HDIM])
    sb3_h = din("sb3", [HDIM])
    sw2_h = din("sw2", [HDIM, D])
    sb2_h = din("sb2", [D])

    out_h = nc.dram_tensor("out", [BPC, L, D], F32, kind="ExternalOutput").ap()
    out_flat = out_h.rearrange("b l d -> (b l) d")
    x_flat = x_in.rearrange("b l d -> (b l) d")

    if os.environ.get("XPOUT"):
        xp_scr = nc.dram_tensor("xp_scr", [8, 128, TPC], F32,
                                kind="ExternalOutput").ap()
    else:
        xp_scr = nc.dram_tensor("xp_scr", [8, 128, TPC], F32).ap()
    waT_scr = nc.dram_tensor("waT_scr", [8, 128, 2 * INNER], F32).ap()
    wxT_scr = nc.dram_tensor("wxT_scr", [8, 128, 2 * INNER], F32).ap()
    woT_scr = nc.dram_tensor("woT_scr", [4, 128, D], F32).ap()
    sac_scr = nc.dram_tensor("sac_scr", [128, 256], F32).ap()

    dbg = {}
    if debug:
        for nm_, shp in [("dbg_xinv_re", [4, 128, INNER]),
                         ("dbg_xinv_im", [4, 128, INNER]),
                         ("dbg_waT", [9, 128, 2 * INNER]),
                         ("dbg_pl", [20, 128, 2048]),
                         ("dbg_xp", [8, 128, TPC]),
                         ("dbg_it", [NIT_R + NIT_F + 6, 128, 2048])]:
            dbg[nm_] = nc.dram_tensor(nm_, shp, F32, kind="ExternalOutput").ap()

    def sb(name, p, f, dtype=F32):
        return es.enter_context(nc.sbuf_tensor(name, [p, f], dtype))

    ident = sb("ident", 128, 128)
    gsc = sb("gsc", 128, 2)
    xp_keep = sb("xp_keep", 128, 8 * CA)
    ones_row = sb("ones_row", 1, 1024)
    consts = sb("consts", 128, 16)
    crow = sb("crow", 1, 16)
    h0_sb = sb("h0_sb", 128, 8)
    gwT = sb("gwT", 128, 8 * NEXP)
    gbb = sb("gbb", 128, NEXP)
    gb8 = sb("gb8", 1, 8)

    g_stg = sb("g_stg", NEXP, D)
    es_vx = ExitStack()
    def sbx(name, p, f):
        return es_vx.enter_context(nc.sbuf_tensor(name, [p, f], F32))
    x_re = sbx("x_re", 128, 4 * INNER)
    x_im = sbx("x_im", 128, 4 * INNER)
    x_imn = sbx("x_imn", 128, 4 * INNER)
    vT_re = sbx("vT_re", 128, 4 * INNER)
    vT_imn = sbx("vT_imn", 128, 4 * INNER)
    es_va = ExitStack()
    ii_a = es_va.enter_context(nc.sbuf_tensor("ii_a", [128, 128], F32))
    ii_b = es_va.enter_context(nc.sbuf_tensor("ii_b", [128, 128], F32))
    vA_re = es_va.enter_context(nc.sbuf_tensor("vA_re", [128, 4 * INNER], F32))
    vA_im = es_va.enter_context(nc.sbuf_tensor("vA_im", [128, 4 * INNER], F32))
    vA_imn = es_va.enter_context(nc.sbuf_tensor("vA_imn", [128, 4 * INNER], F32))

    psb = [es.enter_context(nc.psum_tensor(f"ps{i}", [128, 512], F32))
           for i in range(8)]
    banks = [Bank(kb, psb[i], i) for i in range(8)]
    dq_ld = DQ(kb, es, "ld", "sync")
    dq_gl = DQ(kb, es, "gl", "gpsimd")
    _steng = "sync" if os.environ.get("STSYNC") else "gpsimd"
    dq_st = DQ(kb, es, "st", _steng)

    def ccol(v):
        return consts[:, CIDX[v]:CIDX[v] + 1]

    def crow_ap(v):
        return crow[0:1, CIDX[v]:CIDX[v] + 1]

    def tile(plane, W, r, c0=None, c1=None):
        if c0 is None:
            return plane[:, r * W:(r + 1) * W]
        return plane[:, r * W + c0: r * W + c1]

    def ms(ap, val):
        kb.run("vector", "v", lambda ve, a=ap, x=float(val): ve.memset(a, x))

    ms(ones_row[:], 1.0)
    for j, val in enumerate(CONST_VALS):
        ms(consts[:, j:j + 1], val)
        ms(crow[0:1, j:j + 1], val)
    kb.run("gpsimd", "g", lambda e: e.iota(ii_a[:], [[1, 128]], base=0,
                                           channel_multiplier=0,
                                           allow_small_or_imprecise_dtypes=True))
    kb.run("gpsimd", "g", lambda e: e.iota(ii_b[:], [[0, 128]], base=0,
                                           channel_multiplier=1,
                                           allow_small_or_imprecise_dtypes=True))
    kb.wait("vector", "g")
    kb.run("vector", "v",
           lambda ve: ve.tensor_tensor(ident[:], ii_a[:], ii_b[:], ALU.is_equal))
    V_CONSTS = kb.cnt("v")
    for eng in ("tensor", "scalar", "gpsimd", "sync"):
        kb.wait(eng, "v", V_CONSTS)
    kb.wait("vector", "v", V_CONSTS)

    dq_gl.dma(gb8[0:1, 0:NEXP], gate_b_h.unsqueeze(0))
    for g in range(4):
        dq_gl.dma(h0_sb[:, g:g + 1], h0_re_h[g * 128:(g + 1) * 128].unsqueeze(1))
        dq_gl.dma(h0_sb[:, 4 + g:5 + g],
                  h0_im_h[g * 128:(g + 1) * 128].unsqueeze(1))
    tok_small = dq_gl.fence()

    for g in range(4):
        dq_ld.dma(tile(vA_re, INNER, g), v_re_h[g * 128:(g + 1) * 128, :])
        dq_ld.dma(tile(vA_im, INNER, g), v_im_h[g * 128:(g + 1) * 128, :])
    tok_v = dq_ld.fence()
    dq_ld.wait("vector", tok_v)
    kb.run("vector", "v", lambda ve: ve.tensor_scalar_mul(vA_imn[:], vA_im[:], -1.0))
    V_VA = kb.cnt("v")

    dq_ld.wait("tensor", tok_v)

    def emit_transpose(src_fn, dst_fn, nrt, nct, negate=False, waitv=None):
        for c in range(nct):
            for r0 in range(0, nrt, 4):
                grp = list(range(r0, min(r0 + 4, nrt)))
                bk = banks[6 + ((c * nrt + r0) // 4) % 2]
                def tf(te, c=c, grp=grp, bk=bk):
                    last = None
                    for k, r in enumerate(grp):
                        last = te.transpose(bk.ap[:, k * 128:(k + 1) * 128],
                                            src_fn(r, c), ident[:])
                    return last
                bk.produce(tf, waitv=waitv)
                if negate:
                    def ev(ve, c=c, grp=grp, bk=bk):
                        return ve.tensor_scalar_mul(
                            dst_fn(c, grp[0], len(grp)),
                            bk.ap[:, 0:len(grp) * 128], -1.0)
                else:
                    def ev(ve, c=c, grp=grp, bk=bk):
                        return ve.tensor_copy(dst_fn(c, grp[0], len(grp)),
                                              bk.ap[:, 0:len(grp) * 128])
                bk.consume("vector", ev, guard=gsc[:, 0:1])

    def src_plane(plane, W):
        return lambda r, c: plane[:, r * W + c * 128: r * W + c * 128 + 128]

    def dst_plane(plane, W):
        return lambda c, r0, n: plane[:, c * W + r0 * 128: c * W + (r0 + n) * 128]

    emit_transpose(src_plane(vA_re, INNER), dst_plane(vT_re, INNER), 4, 4,
                   waitv=V_CONSTS)
    emit_transpose(src_plane(vA_im, INNER), dst_plane(vT_imn, INNER), 4, 4,
                   negate=True)

    dq_ld.dma(g_stg[:, :], gate_w_h[:, :])
    tok_gw = dq_ld.fence()
    dq_ld.wait("tensor", tok_gw)
    for c in range(8):
        bk = banks[6]
        def tg(te, c=c, bk=bk):
            return te.transpose(bk.ap[:, 0:NEXP],
                                g_stg[:, c * 128:(c + 1) * 128],
                                ident[0:NEXP, 0:NEXP])
        bk.produce(tg)
        def eg(ve, c=c, bk=bk):
            return ve.tensor_copy(gwT[:, c * NEXP:(c + 1) * NEXP],
                                  bk.ap[:, 0:NEXP])
        bk.consume("vector", eg)

    dq_gl.wait("tensor", tok_small)
    bk = banks[7]
    def bgb(te, bk=bk):
        return te.matmul(bk.ap[:, 0:NEXP], ones_row[0:1, 0:128],
                         gb8[0:1, 0:NEXP], start=True, stop=True)
    bk.produce(bgb)
    bk.consume("vector",
               lambda ve, bk=bk: ve.tensor_copy(gbb[:], bk.ap[:, 0:NEXP]))

    es_n = ExitStack()
    def sbn(name, p, f, dtype=F32):
        return es_n.enter_context(nc.sbuf_tensor(name, [p, f], dtype))
    vr_re = sbn("vr_re", 128, 4 * INNER, F32R)
    vr_im = sbn("vr_im", 128, 4 * INNER, F32R)
    vr_imn = sbn("vr_imn", 128, 4 * INNER, F32R)
    xr_re = sbn("xr_re", 128, 4 * INNER, F32R)
    xr_im = sbn("xr_im", 128, 4 * INNER, F32R)
    xtr_re = sbn("xtr_re", 128, 4 * INNER, F32R)
    xtr_im = sbn("xtr_im", 128, 4 * INNER, F32R)
    xtr_imn = sbn("xtr_imn", 128, 4 * INNER, F32R)
    pr_re = sbn("pr_re", 128, 4 * INNER, F32R)
    pr_im = sbn("pr_im", 128, 4 * INNER, F32R)
    xt32_re = sbn("xt32_re", 128, 4 * INNER)
    xt32_im = sbn("xt32_im", 128, 4 * INNER)
    xt32_imn = sbn("xt32_imn", 128, 4 * INNER)
    p32_re = sbn("p32_re", 128, 4 * INNER)
    p32_im = sbn("p32_im", 128, 4 * INNER)
    nsc = sbn("nsc", 128, 4)
    ones128 = sbn("ones128", 128, 128)

    kb.wait("vector", "v", V_VA)
    def cast_vr(ve):
        ve.tensor_copy(vr_re[:], vA_re[:])
        ve.tensor_copy(vr_im[:], vA_im[:])
        return ve.tensor_copy(vr_imn[:], vA_imn[:])
    kb.run("vector", "v", cast_vr)
    V_VR = kb.cnt("v")

    kb.run("vector", "v", lambda ve: ve.memset(nsc[:, 0:3], 0.0))
    V_NSC0 = kb.cnt("v")
    kb.wait("vector", "v", V_NSC0)
    V_red = None
    for mi in range(4):
        for comp in (0, 1):
            bk = banks[(2 * mi + comp) % 4]
            if comp == 0:
                pairs = [(vr_re, vr_re), (vr_im, vr_im)]
            else:
                pairs = [(vr_im, vr_re), (vr_re, vr_imn)]
            def fg(te, mi=mi, pairs=pairs, bk=bk):
                last = None
                first = True
                for lhs, rhs in pairs:
                    for k in range(4):
                        last = te.matmul(
                            bk.ap[:, 0:INNER],
                            lhs[:, k * INNER + mi * 128: k * INNER + mi * 128 + 128],
                            rhs[:, k * INNER:(k + 1) * INNER],
                            start=first, stop=(lhs is pairs[1][0] and k == 3))
                        first = False
                return last
            bk.produce(fg, waitv=V_VR if mi == 0 and comp == 0 else None)
            sl = ((2 * mi + comp) % 4) * INNER
            if V_red is not None:
                kb.wait("scalar", "v", V_red)
            def evsq(se, bk=bk, sl=sl):
                return se.square(p32_re[:, sl:sl + INNER], bk.ap[:, 0:INNER])
            bk.consume("scalar", evsq, guard=gsc[:, 1:2])
            S_GSQ = kb.cnt("s")
            kb.wait("vector", "s", S_GSQ)
            kb.run("vector", "v",
                   lambda ve, sl=sl: ve.reduce_sum(nsc[:, 2:3],
                                                   p32_re[:, sl:sl + INNER],
                                                   axis=AXX))
            kb.wait("vector", "v")
            kb.run("vector", "v",
                   lambda ve: ve.tensor_add(nsc[:, 0:1], nsc[:, 0:1],
                                            nsc[:, 2:3]))
            V_red = kb.cnt("v")
    V_GRAM = kb.cnt("v")

    kb.run("vector", "v", lambda ve: ve.memset(ones128[:], 1.0))
    V_ONES128 = kb.cnt("v")
    bk = banks[4]
    bk.produce(lambda te, bk=bk: te.matmul(bk.ap[:, 0:1], ones128[:],
                                           nsc[:, 0:1], start=True, stop=True),
               waitv=max(V_GRAM, V_ONES128))
    def alpha_fn(se, bk=bk):
        return se.activation(nsc[:, 2:3], bk.ap[:, 0:1], AF.Ln)
    bk.consume("scalar", alpha_fn, guard=gsc[:, 1:2])
    kb.wait("scalar", "s")
    kb.run("scalar", "s",
           lambda se: se.activation(nsc[:, 3:4], nsc[:, 2:3], AF.Exp,
                                    scale=ccol(-0.5),
                                    bias=ccol(LN_INV102)))
    kb.wait("scalar", "s", kb.cnt("s"))
    kb.run("scalar", "s", lambda se: se.square(gsc[:, 1:2], nsc[:, 3:4]))
    S_ALPHA = kb.cnt("s")
    kb.wait("scalar", "s", S_ALPHA)
    kb.run("scalar", "s", lambda se: se.square(gsc[:, 1:2], nsc[:, 3:4]))

    import os as _os
    if debug and not _os.environ.get("NODUMP"):
        kb.wait(_steng, "s", S_ALPHA)
        dq_st.dma(dbg["dbg_waT"][8, :, 0:4], nsc[:, 0:4])
        dq_st.fence()

    kb.wait("vector", "s", S_ALPHA)
    def x0(ve):
        al = bass.AP(nsc, 3, [[4, 128], [0, 4 * INNER]])
        ve.tensor_tensor(x_re[:], vA_re[:], al, ALU.mult)
        return ve.tensor_tensor(x_im[:], vA_imn[:], al, ALU.mult)
    kb.run("vector", "v", x0)
    kb.wait("vector", "v", kb.cnt("v"))

    def ndump(idx, plane):
        import os
        if os.environ.get("NODUMP"):
            return
        if not (debug and stage == "newton"):
            return
        kb.wait(_steng, "v")
        kb.wait(_steng, "s")
        w = plane.shape[-1]
        if w > 1024:
            dq_st.dma(dbg["dbg_waT"][idx, :, :], plane[:, 0:1024])
            dq_st.dma(dbg["dbg_waT"][idx + 1, :, :], plane[:, 1024:2048])
        else:
            dq_st.dma(dbg["dbg_waT"][idx, :, 0:w], plane)
        tok = dq_st.fence()
        dq_st.wait("vector", tok)
        dq_st.wait("scalar", tok)

    ndump(0, x_re[:])
    ndump(2, nsc[:, 0:4])

    it_counter = [0]

    def newton_iter(f32r_phase):
        if f32r_phase:
            a_re, a_im, a_imn = vr_re, vr_im, vr_imn
            tt_re, tt_im, tt_imn = xtr_re, xtr_im, xtr_imn
            pp_re, pp_im = pr_re, pr_im
        else:
            a_re, a_im, a_imn = vA_re, vA_im, vA_imn
            tt_re, tt_im, tt_imn = xt32_re, xt32_im, xt32_imn
            pp_re, pp_im = p32_re, p32_im

        if f32r_phase:
            kb.wait("vector", "v", kb.cnt("v"))
            def cx(ve):
                ve.tensor_copy(xr_re[:], x_re[:])
                return ve.tensor_copy(xr_im[:], x_im[:])
            kb.run("vector", "v", cx)
            xx_re, xx_im = xr_re, xr_im
        else:
            xx_re, xx_im = x_re, x_im
        V_X = kb.cnt("v")
        import os as _os2
        first_dbg = (debug and it_counter[0] == 0 and f32r_phase
                     and not _os2.environ.get("NODUMP"))
        if first_dbg:
            kb.wait(_steng, "v", V_X)
            dq_st.dma(dbg["dbg_it"][40, :, :], x_re[:])
            dq_st.dma(dbg["dbg_it"][44, :, :], xr_re[:])
            dq_st.fence()

        for g in range(4):
            bk = banks[6 + g % 2]
            def tf(te, g=g, bk=bk):
                last = None
                for r in range(4):
                    last = te.transpose(
                        bk.ap[:, r * 128:(r + 1) * 128],
                        x_re[:, r * INNER + g * 128: r * INNER + g * 128 + 128],
                        ident[:])
                return last
            bk.produce(tf, waitv=V_X)
            def ev(ve, g=g, bk=bk, tt_re=tt_re):
                return ve.tensor_copy(tile(tt_re, INNER, g), bk.ap[:, 0:INNER])
            bk.consume("vector", ev, guard=gsc[:, 0:1])
        for g in range(4):
            bk = banks[6 + g % 2]
            def tf(te, g=g, bk=bk):
                last = None
                for r in range(4):
                    last = te.transpose(
                        bk.ap[:, r * 128:(r + 1) * 128],
                        x_im[:, r * INNER + g * 128: r * INNER + g * 128 + 128],
                        ident[:])
                return last
            bk.produce(tf)
            def ev(ve, g=g, bk=bk, tt_im=tt_im, tt_imn=tt_imn):
                ve.tensor_copy(tile(tt_im, INNER, g), bk.ap[:, 0:INNER])
                return ve.tensor_scalar_mul(tile(tt_imn, INNER, g),
                                            bk.ap[:, 0:INNER], -1.0)
            bk.consume("vector", ev, guard=gsc[:, 0:1])
        V_TT = kb.cnt("v")
        if first_dbg:
            kb.wait(_steng, "v", V_TT)
            dq_st.dma(dbg["dbg_it"][41, :, :], x_re[:])
            dq_st.dma(dbg["dbg_it"][43, :, :], tt_re[:])
            dq_st.fence()

        for mi in range(4):
            for comp in (0, 1):
                bk = banks[(2 * mi + comp) % 4]
                if comp == 0:
                    pairs = [(a_re, xx_re), (a_imn, xx_im)]
                    dst = pp_re
                else:
                    pairs = [(a_im, xx_re), (a_re, xx_im)]
                    dst = pp_im
                def fp(te, mi=mi, pairs=pairs, bk=bk):
                    last = None
                    first = True
                    for pi, (lhs, rhs) in enumerate(pairs):
                        for k in range(4):
                            last = te.matmul(
                                bk.ap[:, 0:INNER],
                                lhs[:, k * INNER + mi * 128:
                                    k * INNER + mi * 128 + 128],
                                rhs[:, k * INNER:(k + 1) * INNER],
                                start=first, stop=(pi == 1 and k == 3))
                            first = False
                    return last
                bk.produce(fp, waitv=V_X)
                def ev(ve, mi=mi, bk=bk, dst=dst):
                    return ve.tensor_copy(tile(dst, INNER, mi),
                                          bk.ap[:, 0:INNER])
                bk.consume("vector", ev, guard=gsc[:, 0:1])
        V_P = kb.cnt("v")
        if first_dbg:
            kb.wait(_steng, "v", V_P)
            dq_st.dma(dbg["dbg_it"][42, :, :], pp_re[:])
            dq_st.dma(dbg["dbg_it"][45, :, :], x_re[:])
            dq_st.fence()
        it_counter[0] += 1

        for mi in range(4):
            for comp in (0, 1):
                bk = banks[(2 * mi + comp) % 4]
                if comp == 0:
                    pairs = [(tt_re, pp_re), (tt_imn, pp_im)]
                    dst = x_re
                else:
                    pairs = [(tt_im, pp_re), (tt_re, pp_im)]
                    dst = x_im
                def fq(te, mi=mi, pairs=pairs, bk=bk):
                    last = None
                    first = True
                    for pi, (lhs, rhs) in enumerate(pairs):
                        for k in range(4):
                            last = te.matmul(
                                bk.ap[:, 0:INNER],
                                lhs[:, k * INNER + mi * 128:
                                    k * INNER + mi * 128 + 128],
                                rhs[:, k * INNER:(k + 1) * INNER],
                                start=first, stop=(pi == 1 and k == 3))
                            first = False
                    return last
                bk.produce(fq, waitv=max(V_P, V_TT))
                def ev(ve, mi=mi, bk=bk, dst=dst):
                    return ve.scalar_tensor_tensor(
                        tile(dst, INNER, mi), tile(dst, INNER, mi), 2.0,
                        bk.ap[:, 0:INNER], ALU.mult, ALU.subtract)
                bk.consume("vector", ev, guard=gsc[:, 0:1])

    def itdump(i):
        import os
        if (not debug) or os.environ.get("NODUMP"):
            return
        dq_st.dma(dbg["dbg_it"][i, :, :], x_re[:])
        dq_st.fence()

    for _it in range(NIT_R):
        newton_iter(True)
        itdump(_it)
        if _it == 0:
            ndump(4, x_re[:])
            ndump(6, p32_re[:])
    for _it in range(NIT_F):
        newton_iter(False)
        itdump(NIT_R + _it)

    kb.wait("vector", "v", kb.cnt("v"))
    kb.run("vector", "v",
           lambda ve: ve.tensor_scalar_mul(x_imn[:], x_im[:], -1.0))
    V_NEWTON = kb.cnt("v")

    if debug:
        kb.wait(_steng, "v", V_NEWTON)
        kb.wait(_steng, "s", S_ALPHA)
        for g in range(4):
            dq_st.dma(dbg["dbg_xinv_re"][g, :, :], tile(x_re, INNER, g))
            dq_st.dma(dbg["dbg_xinv_im"][g, :, :], tile(x_im, INNER, g))
        dq_st.fence()

    if stage == "newton":
        for tok in dq_st.final_tokens():
            dq_st.wait("gpsimd", tok)
        _finish(nc, kb)
        es_n.close()
        es_va.close()
        es_vx.close()
        es.close()
        return nc, dbg

    if WSCR_PARTS >= -1:
        barrier(kb)
    es_n.close()
    es_va.close()

    es_w = ExitStack()
    stg = es_w.enter_context(nc.sbuf_tensor("stg", [128, 8 * 1024], F32))
    wstg = es_w.enter_context(nc.sbuf_tensor("wstg", [128, 2 * 1024], F32))

    wstg_toks = {}

    def transpose_to_scratch(load_fn, n_src_tiles, n_dst_tiles, scr, dstw):
        if WSCR_PARTS < 0:
            return {}
        kb.wait("sync", "t")
        for r in range(n_src_tiles):
            dq_ld.dma(stg[:, r * 1024: r * 1024 + n_dst_tiles * 128], load_fn(r))
        tok = dq_ld.fence()
        if WSCR_PARTS < 1:
            return {}
        dq_ld.wait("tensor", tok)
        store_toks = wstg_toks
        for c in range(n_dst_tiles):
            par = c % 2
            if par in store_toks:
                dq_st.wait("vector", store_toks[par])
            for r0 in range(0, n_src_tiles, 4):
                grp = list(range(r0, min(r0 + 4, n_src_tiles)))
                bk = banks[6 + (r0 // 4) % 2]
                def tf(te, c=c, grp=grp, bk=bk):
                    last = None
                    for k, r in enumerate(grp):
                        last = te.transpose(
                            bk.ap[:, k * 128:(k + 1) * 128],
                            stg[:, r * 1024 + c * 128: r * 1024 + c * 128 + 128],
                            ident[:])
                    return last
                bk.produce(tf)
                def ev(ve, grp=grp, bk=bk, par=par):
                    return ve.tensor_copy(
                        wstg[:, par * 1024 + grp[0] * 128:
                             par * 1024 + (grp[0] + len(grp)) * 128],
                        bk.ap[:, 0:len(grp) * 128])
                bk.consume("vector", ev, guard=gsc[:, 0:1])
            if WSCR_PARTS < 2:
                continue
            kb.wait(_steng, "v")
            dq_st.dma(scr[c, :, 0:n_src_tiles * 128],
                      wstg[:, par * 1024: par * 1024 + n_src_tiles * 128])
            store_toks[par] = dq_st.fence()
        return store_toks


    wa2 = w_a_h.rearrange("(c two) d -> c two d", two=2)
    wx2 = w_x_h.rearrange("(c two) d -> c two d", two=2)

    def wa_src(r):
        half, i = divmod(r, 4)
        return wa2[i * 128:(i + 1) * 128, half, :]

    def wa_src_fixed(r):
        half = r // 4
        i = r % 4
        return wa2[i * 128:(i + 1) * 128, half, :]

    def wx_src(r):
        half = r // 4
        i = r % 4
        return wx2[i * 128:(i + 1) * 128, half, :]

    toks_wa = transpose_to_scratch(wa_src_fixed, 8, 8, waT_scr, None)
    if WSCR_PARTS >= 3:
        toks_wx = transpose_to_scratch(wx_src, 8, 8, wxT_scr, None)
        toks_wo = transpose_to_scratch(
            lambda r: w_out_h[r * 128:(r + 1) * 128, :], 8, 4, woT_scr, None)
    else:
        toks_wx, toks_wo = {}, {}
    for tk in list(toks_wa.values()) + list(toks_wx.values()) + list(toks_wo.values()):
        dq_st.wait("sync", tk)
    if debug:
        for c in range(8):
            dq_gl.dma(dbg["dbg_waT"][c, :, :], waT_scr[c, :, :])
        dq_gl.fence()

    if debug and stage == "wscr" and WSCR_PARTS >= 2:
        for c in range(8):
            dq_ld.dma(stg[:, 0:1024], waT_scr[c, :, :])
            tk1 = dq_ld.fence()
            dq_ld.wait("gpsimd", tk1)
            dq_gl.dma(dbg["dbg_waT"][c, :, :], stg[:, 0:1024])
            tk2 = dq_gl.fence()
            dq_gl.wait("sync", tk2)
    if stage == "wscr":
        for dq in (dq_st, dq_ld, dq_gl):
            for tok in dq.final_tokens():
                dq.wait("gpsimd", tok)
        _finish(nc, kb)
        es_w.close()
        es_vx.close()
        es.close()
        return nc, dbg

    barrier(kb)
    es_w.close()

    es_pa = ExitStack()
    def sba(name, p, f):
        return es_pa.enter_context(nc.sbuf_tensor(name, [p, f], F32))
    x_tm = sba("x_tm", 128, 2 * 1024)
    x_fm = sba("x_fm", 128, 8 * CA)
    xn = sba("xn", 128, 8 * CA)
    tmp8 = sba("tmp8", 128, 8 * CA)
    srow = sba("srow", 1, 1024)
    rstd_b = sba("rstd_b", 128, CA)
    mrs_b = sba("mrs_b", 128, CA)
    ar_p = sba("ar_p", 128, 4 * CA)
    ai_p = sba("ai_p", 128, 4 * CA)
    m_a = sba("m_a", 128, 4 * CA)
    rho = sba("rho", 128, 4 * CA)
    t4a = sba("t4a", 128, 4 * CA)
    t4b = sba("t4b", 128, 4 * CA)
    X_s = sba("X_s", 128, 4 * (CA + 1))
    kint = es_pa.enter_context(
        nc.sbuf_tensor("kint", [128, 4 * (CA + 1)], mybir.dt.int32))
    psi = sba("psi", 128, 4 * (CA + 1))
    schalf = sba("schalf", 128, 4 * (CA + 1))
    cosS = sba("cosS", 128, 4 * (CA + 1))
    sinS = sba("sinS", 128, 4 * (CA + 1))
    xcr_s = sba("xcr_s", 128, 4 * (CA + 1))
    xci_s = sba("xci_s", 128, 4 * (CA + 1))
    d_re = sba("d_re", 128, 4 * CA)
    d_im = sba("d_im", 128, 4 * CA)
    w_re = sba("w_re", 128, 4 * (CA + 1))
    w_im = sba("w_im", 128, 4 * (CA + 1))
    hre = sba("hre", 128, 4 * CA)
    onesp = sba("onesp", 128, CA)
    wa_buf = sba("wa_buf", 128, 2 * 1024)
    wx_buf = sba("wx_buf", 128, 2 * 1024)
    _wofull = os.environ.get("WOFULL", "1") != "0"
    wo_buf = sba("wo_buf", 128, 4 * 1024 if _wofull else 2 * 1024)
    if _wofull:
        for k in range(4):
            dq_ld.dma(wo_buf[:, k * 1024:(k + 1) * 1024], woT_scr[k, :, :])
        tok_wof = dq_ld.fence()
        dq_ld.wait("tensor", tok_wof)

    W1 = CA + 1

    def ap3(t, col0, tstride, nt, w, pw):
        return bass.AP(t, col0, [[pw, 128], [tstride, nt], [1, w]])

    def pl(t, nt=4, w=CA, col0=0, pw=None):
        return ap3(t, col0, w if pw is None else None, nt, w, nt * w)

    def full4(t, w=CA):
        return bass.AP(t, 0, [[4 * w, 128], [1, 4 * w]])

    def strided4(t, col0, w, tstride):
        return bass.AP(t, col0, [[4 * tstride, 128], [tstride, 4], [1, w]])

    def cbc(v, n):
        return bass.AP(consts, CIDX[v], [[16, 128], [0, n]])

    kb.run("vector", "v", lambda ve: ve.memset(onesp[:], 1.0))
    V_ONESP = kb.cnt("v")
    kb.wait("vector", "v", V_ONESP)

    tok_x = [None] * 3
    wa_use = [0, 0]
    wx_use = [0, 0]
    wo_use = [0, 0]
    wa_tok = [None, None]
    wx_tok = [None, None]
    wo_tok = [None, None]

    def stream_tile(dq, scr, k, buf, use, toks, eng="sync"):
        par = k % 2
        if use[par] > 0:
            kb.wait(eng, "t", use[par])
        dq.dma(buf[:, par * 1024:(par + 1) * 1024], scr[k, :, :])
        toks[par] = dq.fence()

    for ch in range(min(BPC * NCHS, PA_LIMIT)):
        first = (ch % NCHS) == 0
        g0 = ch * CA

        def dbgdump(idx, ap, width, ch=ch):
            _c = int(os.environ.get("CHDUMP", "0") or 0)
            if not (debug and ch == _c):
                return
            kb.wait(_steng, "v")
            kb.wait(_steng, "s")
            dq_st.dma(dbg["dbg_pl"][idx, :, 0:width], ap)
            tok = dq_st.fence()
            dq_st.wait("vector", tok)
            dq_st.wait("scalar", tok)

        for tt in range(2):
            dq_ld.dma(x_tm[:, tt * 1024:(tt + 1) * 1024],
                      x_flat[g0 + tt * 128: g0 + (tt + 1) * 128, :])
        tokx = dq_ld.fence()
        dq_ld.wait("tensor", tokx)
        _chd = os.environ.get("CHDUMP")
        if debug and _chd is not None and ch == int(_chd):
            dq_ld.wait(_steng, tokx)
            dq_st.dma(dbg["dbg_pl"][17, :, 0:2048], x_tm[:, 0:2048])
            tok_chd = dq_st.fence()
            dq_st.wait("vector", tok_chd)
        for g in range(8):
            bk = banks[6 + g % 2]
            def tf(te, g=g, bk=bk):
                last = None
                for tt in range(2):
                    last = te.transpose(
                        bk.ap[:, tt * 128:(tt + 1) * 128],
                        x_tm[:, tt * 1024 + g * 128: tt * 1024 + g * 128 + 128],
                        ident[:])
                return last
            bk.produce(tf)
            def ev(ve, g=g, bk=bk):
                return ve.tensor_copy(x_fm[:, g * CA:(g + 1) * CA],
                                      bk.ap[:, 0:CA])
            bk.consume("vector", ev, guard=gsc[:, 0:1])
        V_XFM = kb.cnt("v")
        if debug and _chd is not None and ch == int(_chd):
            kb.wait(_steng, "v", V_XFM)
            dq_st.dma(dbg["dbg_pl"][18, :, 0:2048], x_fm[:, 0:2048])
            tok_chd2 = dq_st.fence()
            dq_st.wait("vector", tok_chd2)

        kb.wait("scalar", "v", V_XFM)
        kb.run("scalar", "s", lambda se: se.square(tmp8[:], x_fm[:]))
        S_XSQ = kb.cnt("s")
        bk6, bk7 = banks[6], banks[7]
        def sums(te, bk=bk6):
            last = None
            for k in range(8):
                last = te.matmul(bk.ap[0:1, 0:CA], ccol(1.0),
                                 x_fm[:, k * CA:(k + 1) * CA],
                                 start=(k == 0), stop=(k == 7))
            return last
        bk6.produce(sums, waitv=V_XFM)
        kb.wait("tensor", "s", S_XSQ)
        def sumsq(te, bk=bk7):
            last = None
            for k in range(8):
                last = te.matmul(bk.ap[0:1, 0:CA], ccol(1.0),
                                 tmp8[:, k * CA:(k + 1) * CA],
                                 start=(k == 0), stop=(k == 7))
            return last
        bk7.produce(sumsq)

        def stats_a(ve, bk=bk6, bq=bk7):
            ve.tensor_single_scalar(srow[0:1, 0:CA], bk.ap[0:1, 0:CA],
                                    1.0 / D, ALU.mult)
            return ve.tensor_single_scalar(srow[0:1, CA:2 * CA],
                                           bq.ap[0:1, 0:CA], 1.0 / D, ALU.mult)
        kb.wait("vector", "t", bk7.prod)
        bk6.consume("vector", stats_a)
        bk7.cons = ("v", kb.cnt("v"))
        kb.wait("vector", "v")
        kb.run("vector", "v",
               lambda ve: ve.tensor_tensor(srow[0:1, 2 * CA:3 * CA],
                                           srow[0:1, 0:CA], srow[0:1, 0:CA],
                                           ALU.mult))
        kb.wait("vector", "v")
        kb.run("vector", "v",
               lambda ve: ve.tensor_tensor(srow[0:1, CA:2 * CA],
                                           srow[0:1, CA:2 * CA],
                                           srow[0:1, 2 * CA:3 * CA],
                                           ALU.subtract))
        kb.wait("scalar", "v")
        kb.run("scalar", "s",
               lambda se: se.activation(srow[0:1, 2 * CA:3 * CA],
                                        srow[0:1, CA:2 * CA], AF.Ln,
                                        bias=crow_ap(EPS)))
        kb.wait("scalar", "s")
        kb.run("scalar", "s",
               lambda se: se.activation(srow[0:1, 2 * CA:3 * CA],
                                        srow[0:1, 2 * CA:3 * CA], AF.Exp,
                                        scale=crow_ap(-0.5)))
        S_RSTD = kb.cnt("s")
        kb.wait("vector", "s", S_RSTD)
        kb.run("vector", "v",
               lambda ve: ve.tensor_tensor(srow[0:1, 3 * CA:4 * CA],
                                           srow[0:1, 0:CA],
                                           srow[0:1, 2 * CA:3 * CA], ALU.mult))
        V_STATS = kb.cnt("v")
        def bc_rstd(te, bk=bk6):
            return te.matmul(bk.ap[:, 0:CA], ones_row[0:1, 0:128],
                             srow[0:1, 2 * CA:3 * CA], start=True, stop=True)
        bk6.produce(bc_rstd, waitv=V_STATS)
        def bc_mrs(te, bk=bk7):
            return te.matmul(bk.ap[:, 0:CA], ones_row[0:1, 0:128],
                             srow[0:1, 3 * CA:4 * CA], start=True, stop=True)
        bk7.produce(bc_mrs)
        def evb(ve, bk=bk6, bq=bk7):
            ve.tensor_copy(rstd_b[:], bk.ap[:, 0:CA])
            return ve.tensor_copy(mrs_b[:], bq.ap[:, 0:CA])
        kb.wait("vector", "t", bk7.prod)
        bk6.consume("vector", evb)
        bk7.cons = ("v", kb.cnt("v"))
        kb.wait("vector", "v", kb.cnt("v"))
        def apply_ln(ve):
            rb = bass.AP(rstd_b, 0, [[CA, 128], [0, 8], [1, CA]])
            mb = bass.AP(mrs_b, 0, [[CA, 128], [0, 8], [1, CA]])
            x8 = bass.AP(x_fm, 0, [[8 * CA, 128], [CA, 8], [1, CA]])
            xn8 = bass.AP(xn, 0, [[8 * CA, 128], [CA, 8], [1, CA]])
            t8 = bass.AP(tmp8, 0, [[8 * CA, 128], [CA, 8], [1, CA]])
            ve.tensor_tensor(t8, x8, rb, ALU.mult)
            return ve.tensor_tensor(xn8, t8, mb, ALU.subtract)
        kb.run("vector", "v", apply_ln)
        V_XN = kb.cnt("v")
        dbgdump(0, xn[:], 8 * CA)

        for k in range(8):
            stream_tile(dq_ld, waT_scr, k, wa_buf, wa_use, wa_tok)
            dq_ld.wait("tensor", wa_tok[k % 2])
            if k == 0:
                for gb in range(8):
                    nm, c = banks[gb].cons
                    kb.wait("tensor", nm, c)
                kb.wait("tensor", "v", V_XN)
            def mma(te, k=k):
                par = k % 2
                last = None
                for mi in range(8):
                    last = te.matmul(
                        banks[mi].ap[:, 0:CA],
                        wa_buf[:, par * 1024 + mi * 128: par * 1024 + (mi + 1) * 128],
                        xn[:, k * CA:(k + 1) * CA],
                        start=(k == 0), stop=(k == 7))
                return last
            kb.run("tensor", "t", mma)
            wa_use[k % 2] = kb.cnt("t")
        for gb in range(8):
            banks[gb].prod = kb.cnt("t")
        def eva(ve):
            for gb in range(4):
                ve.tensor_copy(ar_p[:, gb * CA:(gb + 1) * CA],
                               banks[gb].ap[:, 0:CA])
            last = None
            for gb in range(4):
                last = ve.tensor_copy(ai_p[:, gb * CA:(gb + 1) * CA],
                                      banks[gb + 4].ap[:, 0:CA])
            return last
        kb.wait("vector", "t")
        kb.run("vector", "v", eva)
        for gb in range(8):
            banks[gb].cons = ("v", kb.cnt("v"))

        def postp_a(ve):
            ve.tensor_tensor(t4a[:], ar_p[:], ar_p[:], ALU.mult)
            ve.tensor_tensor(t4b[:], ai_p[:], ai_p[:], ALU.mult)
            return ve.tensor_tensor(m_a[:], t4a[:], t4b[:], ALU.add)
        kb.run("vector", "v", postp_a)
        V_MA = kb.cnt("v")
        dbgdump(1, ar_p[:], 4 * CA)
        dbgdump(2, ai_p[:], 4 * CA)
        kb.wait("scalar", "v", V_MA)
        def lut_a(se):
            se.activation(t4a[:], m_a[:], AF.Ln, bias=ccol(1.0))
            se.activation(t4a[:], t4a[:], AF.Exp, scale=ccol(-1.0))
            se.activation(t4b[:], m_a[:], AF.Ln)
            return se.activation(t4b[:], t4b[:], AF.Exp, scale=ccol(0.5))
        kb.run("scalar", "s", lut_a)
        S_LUTA = kb.cnt("s")
        kb.wait("vector", "s", S_LUTA)
        def rho_den(ve):
            return ve.tensor_tensor(rho[:], m_a[:], t4a[:], ALU.mult)
        kb.run("vector", "v", rho_den)
        V_DEN = kb.cnt("v")
        dbgdump(3, rho[:], 4 * CA)
        def formsel(ve):
            ve.tensor_single_scalar(kint[:, 0:4 * CA], ar_p[:], 0.0,
                                     ALU.is_ge)
            ve.tensor_tensor(d_im[:], t4b[:], ar_p[:], ALU.subtract)
            ve.tensor_tensor(t4a[:], t4b[:], ar_p[:], ALU.add)
            ve.select(d_re[:], kint[:, 0:4 * CA], ai_p[:], d_im[:])
            return ve.select(t4b[:], kint[:, 0:4 * CA], t4a[:], ai_p[:])
        kb.run("vector", "v", formsel)
        def signs(ve):
            ve.tensor_single_scalar(m_a[:], d_re[:], 0.0, ALU.is_ge)
            ve.tensor_scalar(m_a[:], m_a[:], 2.0, -1.0, ALU.mult, ALU.add)
            ve.tensor_single_scalar(d_im[:], t4b[:], 0.0, ALU.is_ge)
            ve.tensor_scalar(d_im[:], d_im[:], 2.0, -1.0, ALU.mult, ALU.add)
            return ve.tensor_tensor(m_a[:], m_a[:], d_im[:], ALU.mult)
        kb.run("vector", "v", signs)
        V_SGN = kb.cnt("v")
        dbgdump(17, m_a[:], 4 * CA)
        dbgdump(18, d_re[:], 4 * CA)
        dbgdump(19, t4b[:], 4 * CA)
        kb.wait("scalar", "v", V_SGN)
        def lut_absnd(se):
            se.activation(d_re[:], d_re[:], AF.Abs)
            return se.activation(t4b[:], t4b[:], AF.Abs)
        kb.run("scalar", "s", lut_absnd)
        S_ABSND = kb.cnt("s")
        kb.wait("vector", "s", S_ABSND)
        kb.run("vector", "v",
               lambda ve: ve.tensor_single_scalar(t4b[:], t4b[:], 1e-12,
                                                  ALU.max))
        V_CL = kb.cnt("v")
        kb.wait("scalar", "v", V_CL)
        def lut_den(se):
            se.activation(t4b[:], t4b[:], AF.Ln)
            return se.activation(t4b[:], t4b[:], AF.Exp, scale=ccol(-1.0))
        kb.run("scalar", "s", lut_den)
        S_DEN = kb.cnt("s")
        kb.wait("vector", "s", S_DEN)
        kb.run("vector", "v",
               lambda ve: ve.tensor_tensor(t4a[:], d_re[:], t4b[:], ALU.mult))
        V_RATIO = kb.cnt("v")
        kb.wait("scalar", "v", V_RATIO)
        def lut_abs(se):
            se.activation(tmp8[:, 0:4 * CA], t4a[:], AF.Ln)
            return se.activation(tmp8[:, 0:4 * CA], tmp8[:, 0:4 * CA], AF.Exp,
                                 scale=ccol(-1.0))
        kb.run("scalar", "s", lut_abs)
        S_ABS = kb.cnt("s")
        kb.wait("vector", "s", S_ABS)
        def minmask(ve):
            ve.tensor_tensor(tmp8[:, 0:4 * CA], t4a[:], tmp8[:, 0:4 * CA],
                             ALU.min)
            return ve.tensor_single_scalar(t4a[:], t4a[:], 1.0, ALU.is_gt)
        kb.run("vector", "v", minmask)
        V_MIN = kb.cnt("v")
        kb.wait("scalar", "v", V_MIN)
        kb.run("scalar", "s",
               lambda se: se.activation(tmp8[:, 0:4 * CA], tmp8[:, 0:4 * CA],
                                        AF.Arctan))
        S_AT = kb.cnt("s")
        kb.wait("vector", "s", S_AT)
        def atanfold(ve):
            ve.tensor_scalar(t4b[:], tmp8[:, 0:4 * CA], -2.0, math.pi / 2,
                             ALU.mult, ALU.add)
            ve.tensor_tensor(t4b[:], t4a[:], t4b[:], ALU.mult)
            ve.tensor_tensor(t4a[:], tmp8[:, 0:4 * CA], t4b[:], ALU.add)
            return ve.tensor_tensor(t4a[:], t4a[:], m_a[:], ALU.mult)
        kb.run("vector", "v", atanfold)
        S_ATAN = kb.cnt("s")
        dbgdump(4, t4a[:], 4 * CA)

        def xcarry(ve, first=first):
            if first:
                return ve.tensor_copy(strided4(X_s, 0, 1, W1), cbc(0.0, 4).unsqueeze(2))
            return ve.tensor_copy(strided4(X_s, 0, 1, W1),
                                  strided4(X_s, CA, 1, W1))
        kb.wait("vector", "s", S_ATAN)
        kb.run("vector", "v", xcarry)
        kb.wait("vector", "v")
        def scans_x(ve):
            last = None
            for g in range(4):
                last = ve.tensor_tensor_scan(
                    X_s[:, g * W1 + 1: g * W1 + 1 + CA],
                    onesp[:], t4a[:, g * CA:(g + 1) * CA],
                    X_s[:, g * W1: g * W1 + 1], ALU.mult, ALU.add)
            return last
        kb.run("vector", "v", scans_x)
        def fold_psi(ve):
            a_ = bass.AP(X_s, 0, [[4 * W1, 128], [1, 4 * W1]])
            p_ = bass.AP(psi, 0, [[4 * W1, 128], [1, 4 * W1]])
            t_ = bass.AP(tmp8, 0, [[8 * CA, 128], [1, 4 * W1]])
            k_ = bass.AP(kint, 0, [[4 * W1, 128], [1, 4 * W1]])
            ve.tensor_single_scalar(t_, a_, 1.0 / math.pi, ALU.mult)
            ve.tensor_copy(k_, t_)
            ve.tensor_copy(t_, k_)
            ve.scalar_tensor_tensor(p_, t_, -math.pi, a_, ALU.mult, ALU.add)
            ve.tensor_single_scalar(p_, p_, 2.0, ALU.mult)
            ve.tensor_single_scalar(t_, p_, math.pi, ALU.is_gt)
            ve.scalar_tensor_tensor(p_, t_, -2.0 * math.pi, p_, ALU.mult, ALU.add)
            ve.tensor_single_scalar(t_, p_, -math.pi, ALU.is_lt)
            return ve.scalar_tensor_tensor(p_, t_, 2.0 * math.pi, p_,
                                           ALU.mult, ALU.add)
        kb.run("vector", "v", fold_psi)
        V_PSI = kb.cnt("v")
        kb.wait("scalar", "v", V_PSI)
        def lut_trig(se):
            se.activation(sinS[:], psi[:], AF.Sin)
            se.activation(schalf[:], psi[:], AF.Sin, scale=ccol(0.5))
            return se.square(schalf[:], schalf[:])
        kb.run("scalar", "s", lut_trig)
        S_TRIG = kb.cnt("s")
        kb.wait("vector", "s", S_TRIG)
        kb.run("vector", "v",
               lambda ve: ve.tensor_scalar(cosS[:], schalf[:], -2.0, 1.0,
                                           ALU.mult, ALU.add))
        V_TRIG = kb.cnt("v")
        dbgdump(5, X_s[:], 4 * W1)
        dbgdump(6, sinS[:], 4 * W1)
        dbgdump(7, cosS[:], 4 * W1)

        for k in range(8):
            stream_tile(dq_ld, wxT_scr, k, wx_buf, wx_use, wx_tok)
            dq_ld.wait("tensor", wx_tok[k % 2])
            if k == 0:
                for gb in range(8):
                    nm, c = banks[gb].cons
                    kb.wait("tensor", nm, c)
            def mmx(te, k=k):
                par = k % 2
                last = None
                for mi in range(8):
                    last = te.matmul(
                        banks[mi].ap[:, 0:CA],
                        wx_buf[:, par * 1024 + mi * 128: par * 1024 + (mi + 1) * 128],
                        xn[:, k * CA:(k + 1) * CA],
                        start=(k == 0), stop=(k == 7))
                return last
            kb.run("tensor", "t", mmx)
            wx_use[k % 2] = kb.cnt("t")
        for gb in range(8):
            banks[gb].prod = kb.cnt("t")
        def xccarry(ve, first=first):
            if first:
                ve.tensor_copy(strided4(xcr_s, 0, 1, W1),
                               bass.AP(h0_sb, 0, [[8, 128], [1, 4]]).unsqueeze(2))
                return ve.tensor_copy(strided4(xci_s, 0, 1, W1),
                                      bass.AP(h0_sb, 4, [[8, 128], [1, 4]]).unsqueeze(2))
            ve.tensor_copy(strided4(xcr_s, 0, 1, W1), strided4(xcr_s, CA, 1, W1))
            return ve.tensor_copy(strided4(xci_s, 0, 1, W1),
                                  strided4(xci_s, CA, 1, W1))
        kb.run("vector", "v", xccarry)
        def evx(ve):
            for gb in range(4):
                ve.tensor_copy(t4a[:, gb * CA:(gb + 1) * CA],
                               banks[gb].ap[:, 0:CA])
            last = None
            for gb in range(4):
                last = ve.tensor_copy(t4b[:, gb * CA:(gb + 1) * CA],
                                      banks[gb + 4].ap[:, 0:CA])
            return last
        kb.wait("vector", "t")
        kb.run("vector", "v", evx)
        for gb in range(8):
            banks[gb].cons = ("v", kb.cnt("v"))
        def mx_fn(ve):
            ve.tensor_tensor(m_a[:], t4a[:], t4a[:], ALU.mult)
            ve.tensor_tensor(ar_p[:], t4b[:], t4b[:], ALU.mult)
            return ve.tensor_tensor(m_a[:], m_a[:], ar_p[:], ALU.add)
        kb.run("vector", "v", mx_fn)
        V_MX = kb.cnt("v")
        kb.wait("scalar", "v", V_MX)
        def lut_sqx(se):
            se.activation(ar_p[:], m_a[:], AF.Ln)
            return se.activation(ar_p[:], ar_p[:], AF.Exp, scale=ccol(0.5))
        kb.run("scalar", "s", lut_sqx)
        S_SQX = kb.cnt("s")
        kb.wait("vector", "s", S_SQX)
        def xc_fn(ve):
            ve.tensor_tensor(
                strided4(xcr_s, 1, CA, W1), pack4(t4a), pack4(ar_p), ALU.mult)
            return ve.tensor_tensor(
                strided4(xci_s, 1, CA, W1), pack4(t4b), pack4(ar_p), ALU.mult)
        def pack4(t):
            return bass.AP(t, 0, [[4 * CA, 128], [CA, 4], [1, CA]])
        kb.run("vector", "v", xc_fn)
        V_XC = kb.cnt("v")
        dbgdump(8, xcr_s[:], 4 * W1)
        dbgdump(9, xci_s[:], 4 * W1)

        for gb in range(8):
            nm, c = banks[gb].cons
            kb.wait("tensor", nm, c)
        kb.wait("tensor", "v", V_XC)
        def mmu(te):
            last = None
            for comp in range(2):
                for g in range(4):
                    bk_ap = banks[4 * comp + g].ap
                    if comp == 0:
                        pairs = [(x_re, xcr_s), (x_imn, xci_s)]
                    else:
                        pairs = [(x_im, xcr_s), (x_re, xci_s)]
                    firstmm = True
                    for pi, (lhs, rhs) in enumerate(pairs):
                        for k in range(4):
                            last = te.matmul(
                                bk_ap[:, 0:CA],
                                lhs[:, k * INNER + g * 128: k * INNER + g * 128 + 128],
                                rhs[:, k * W1: k * W1 + CA],
                                start=firstmm, stop=(pi == 1 and k == 3))
                            firstmm = False
            return last
        kb.run("tensor", "t", mmu)
        for gb in range(8):
            banks[gb].prod = kb.cnt("t")
        T_U = kb.cnt("t")

        kb.wait("vector", "t", T_U)
        def dbuild(ve):
            last = None
            for comp in range(2):
                dst = d_re if comp == 0 else d_im
                for g in range(4):
                    u_re_ap = banks[g].ap[:, 0:CA]
                    u_im_ap = banks[4 + g].ap[:, 0:CA]
                    cS = cosS[:, g * W1: g * W1 + CA]
                    sS = sinS[:, g * W1: g * W1 + CA]
                    ta = t4a[:, g * CA:(g + 1) * CA]
                    tb = t4b[:, g * CA:(g + 1) * CA]
                    if comp == 0:
                        ve.tensor_tensor(ta, cS, u_re_ap, ALU.mult)
                        ve.tensor_tensor(tb, sS, u_im_ap, ALU.mult)
                        ve.tensor_tensor(ta, ta, tb, ALU.add)
                    else:
                        ve.tensor_tensor(ta, cS, u_im_ap, ALU.mult)
                        ve.tensor_tensor(tb, sS, u_re_ap, ALU.mult)
                        ve.tensor_tensor(ta, ta, tb, ALU.subtract)
                    last = ve.tensor_tensor(dst[:, g * CA:(g + 1) * CA], ta,
                                            rho[:, g * CA:(g + 1) * CA], ALU.mult)
            return last
        kb.run("vector", "v", dbuild)
        for gb in range(8):
            banks[gb].cons = ("v", kb.cnt("v"))
        dbgdump(10, d_re[:], 4 * CA)
        dbgdump(11, d_im[:], 4 * CA)

        def wcarry(ve, first=first):
            if first:
                ve.tensor_copy(strided4(w_re, 0, 1, W1), cbc(0.0, 4).unsqueeze(2))
                return ve.tensor_copy(strided4(w_im, 0, 1, W1),
                                      cbc(0.0, 4).unsqueeze(2))
            ve.tensor_copy(strided4(w_re, 0, 1, W1), strided4(w_re, CA, 1, W1))
            return ve.tensor_copy(strided4(w_im, 0, 1, W1),
                                  strided4(w_im, CA, 1, W1))
        kb.run("vector", "v", wcarry)
        kb.wait("vector", "v")
        def scans_w(ve):
            last = None
            for g in range(4):
                last = ve.tensor_tensor_scan(
                    w_re[:, g * W1 + 1: g * W1 + 1 + CA],
                    rho[:, g * CA:(g + 1) * CA], d_re[:, g * CA:(g + 1) * CA],
                    w_re[:, g * W1: g * W1 + 1], ALU.mult, ALU.add)
            for g in range(4):
                last = ve.tensor_tensor_scan(
                    w_im[:, g * W1 + 1: g * W1 + 1 + CA],
                    rho[:, g * CA:(g + 1) * CA], d_im[:, g * CA:(g + 1) * CA],
                    w_im[:, g * W1: g * W1 + 1], ALU.mult, ALU.add)
            return last
        kb.run("vector", "v", scans_w)
        dbgdump(12, w_re[:], 4 * W1)
        dbgdump(13, w_im[:], 4 * W1)

        def unrot(ve):
            cS = strided4(cosS, 1, CA, W1)
            sS = strided4(sinS, 1, CA, W1)
            wr = strided4(w_re, 1, CA, W1)
            wi = strided4(w_im, 1, CA, W1)
            ve.tensor_tensor(pack4(t4a), cS, wr, ALU.mult)
            ve.tensor_tensor(pack4(t4b), sS, wi, ALU.mult)
            ve.tensor_tensor(pack4(d_re), pack4(t4a), pack4(t4b), ALU.subtract)
            ve.tensor_tensor(pack4(t4a), cS, wi, ALU.mult)
            ve.tensor_tensor(pack4(t4b), sS, wr, ALU.mult)
            return ve.tensor_tensor(pack4(d_im), pack4(t4a), pack4(t4b), ALU.add)
        kb.run("vector", "v", unrot)
        V_S = kb.cnt("v")
        dbgdump(14, d_re[:], 4 * CA)
        dbgdump(15, d_im[:], 4 * CA)

        for gb in range(4):
            nm, c = banks[gb].cons
            kb.wait("tensor", nm, c)
        kb.wait("tensor", "v", V_S)
        def mmvs(te):
            last = None
            for g in range(4):
                bk_ap = banks[g].ap
                firstmm = True
                for pi, (lhs, rhs) in enumerate([(vT_re, d_re), (vT_imn, d_im)]):
                    for k in range(4):
                        last = te.matmul(
                            bk_ap[:, 0:CA],
                            lhs[:, k * INNER + g * 128: k * INNER + g * 128 + 128],
                            rhs[:, k * CA:(k + 1) * CA],
                            start=firstmm, stop=(pi == 1 and k == 3))
                        firstmm = False
            return last
        kb.run("tensor", "t", mmvs)
        for gb in range(4):
            banks[gb].prod = kb.cnt("t")
        kb.wait("vector", "t")
        def evvs(ve):
            last = None
            for g in range(4):
                last = ve.tensor_tensor(
                    hre[:, g * CA:(g + 1) * CA], banks[g].ap[:, 0:CA],
                    xcr_s[:, g * W1 + 1: g * W1 + 1 + CA], ALU.add)
            return last
        kb.run("vector", "v", evvs)
        for gb in range(4):
            banks[gb].cons = ("v", kb.cnt("v"))
        V_HRE = kb.cnt("v")
        dbgdump(16, hre[:], 4 * CA)

        for k in range(4):
            if not _wofull:
                stream_tile(dq_ld, woT_scr, k, wo_buf, wo_use, wo_tok)
                dq_ld.wait("tensor", wo_tok[k % 2])
            if k == 0:
                for gb in range(8):
                    nm, c = banks[gb].cons
                    kb.wait("tensor", nm, c)
                kb.wait("tensor", "v", V_HRE)
            def mmo(te, k=k):
                par = k if _wofull else k % 2
                last = None
                for mi in range(8):
                    last = te.matmul(
                        banks[mi].ap[:, 0:CA],
                        wo_buf[:, par * 1024 + mi * 128: par * 1024 + (mi + 1) * 128],
                        hre[:, k * CA:(k + 1) * CA],
                        start=(k == 0), stop=(k == 3))
                return last
            kb.run("tensor", "t", mmo)
            wo_use[k % 2] = kb.cnt("t")
        for gb in range(8):
            banks[gb].prod = kb.cnt("t")
        if debug and _chd is not None and ch == int(_chd):
            kb.wait(_steng, "s")
            kb.wait(_steng, "v")
            kb.wait(_steng, "t")
            dq_st.dma(dbg["dbg_pl"][19, :, 0:1024], hre[:, 0:1024])
            dq_st.dma(dbg["dbg_pl"][16, :, 0:2048], x_fm[:, 0:2048])
            dq_st.dma(dbg["dbg_xp"][0, :, 2048:4096], wo_buf[:, 0:2048])
            kb.wait("vector", "t")
            def _psevac(ve):
                last = None
                for _gb in range(8):
                    last = ve.tensor_copy(tmp8[:, _gb * CA:(_gb + 1) * CA],
                                          banks[_gb].ap[:, 0:CA])
                return last
            kb.run("vector", "v", _psevac)
            kb.wait(_steng, "v", kb.cnt("v"))
            dq_st.dma(dbg["dbg_xp"][1, :, 2048:4096], tmp8[:, 0:2048])
            tok_chd3 = dq_st.fence()
            dq_st.wait("vector", tok_chd3)
        kb.wait("vector", "t")
        def evo(ve):
            last = None
            for mi in range(8):
                last = ve.tensor_tensor(
                    x_fm[:, mi * CA:(mi + 1) * CA], x_fm[:, mi * CA:(mi + 1) * CA],
                    banks[mi].ap[:, 0:CA], ALU.add)
            return last
        kb.run("vector", "v", evo)
        for gb in range(8):
            banks[gb].cons = ("v", kb.cnt("v"))
        V_XP = kb.cnt("v")
        if ch == min(BPC * NCHS, PA_LIMIT) - 1:
            kb.wait("vector", "v", V_XP)
            kb.run("vector", "v",
                   lambda ve: ve.tensor_copy(xp_keep[:], x_fm[:]))
            V_XPKEEP = kb.cnt("v")
        if debug and _chd is not None and ch == int(_chd):
            kb.wait(_steng, "v", V_XP)
            dq_st.dma(dbg["dbg_xp"][2, :, 2048:4096], x_fm[:, 0:2048])
            tok_chd4 = dq_st.fence()
            dq_st.wait("vector", tok_chd4)

        kb.wait(_steng, "v", V_XP)
        for g in range(8):
            dq_st.dma(xp_scr[g, :, g0:g0 + CA], x_fm[:, g * CA:(g + 1) * CA])
            if debug:
                dq_st.dma(dbg["dbg_xp"][g, :, g0:g0 + CA],
                          x_fm[:, g * CA:(g + 1) * CA])
        tok_xp = dq_st.fence()
        dq_st.wait("vector", tok_xp)

    if stage == "passA":
        for dq in (dq_st, dq_ld, dq_gl):
            for tok in dq.final_tokens():
                dq.wait("gpsimd", tok)
        _finish(nc, kb)
        es_pa.close()
        es_vx.close()
        es.close()
        return nc, dbg

    if os.environ.get("FAKEP"):
        for gb in range(8):
            bk = banks[gb]
            def fake(te, bk=bk):
                last = None
                for k in range(4):
                    last = te.matmul(bk.ap[:, 0:CA],
                                     wo_buf[:, k * 128:(k + 1) * 128],
                                     hre[:, 0:CA],
                                     start=(k == 0), stop=(k == 3))
                return last
            bk.produce(fake)
            def fcons(ve, bk=bk):
                return ve.tensor_copy(tmp8[:, 0:CA], bk.ap[:, 0:CA])
            bk.consume("vector", fcons)

    _tpad = int(os.environ.get("TPAD", "0"))
    for _i in range(_tpad):
        kb.run("tensor", "t",
               lambda te, _i=_i: te.matmul(banks[6 + _i % 2].ap[:, 0:128],
                                           ident[:], ident[:],
                                           start=True, stop=True))
        if _i % 4 == 3:
            kb.wait("tensor", "t", kb.cnt("t"))

    if os.environ.get("SAC"):
        for g in range(8):
            dq_st.dma(xp_scr[g, :, 0:CA], x_fm[:, g * CA:(g + 1) * CA])
        dq_st.fence()
    _nflush = int(os.environ.get("NFLUSH", "0"))
    for _ in range(_nflush):
        dq_st.dma(sac_scr[:, 0:2], gsc[:, 0:2])
    if _nflush:
        dq_st.fence()
    ndelay = int(os.environ.get("PADELAY", "0"))
    for _ in range(ndelay):
        kb.run("vector", "v",
               lambda ve: ve.tensor_copy(tmp8[:, 0:2048], xn[:, 0:2048]))
    if ndelay:
        kb.wait("vector", "v", kb.cnt("v"))
    for tok in dq_st.final_tokens():
        for _e in BARRIER_ENGS:
            dq_st.wait(_e, tok)
    es_pa.close()
    es_vx.close()

    es_pb = ExitStack()
    def sbb(name, p, f):
        return es_pb.enter_context(nc.sbuf_tensor(name, [p, f], F32))
    w1g = sbb("w1g", 128, 2 * 512)
    w3g = sbb("w3g", 128, 2 * 512)
    w2k = sbb("w2k", 128, 2 * D)
    xn2 = sbb("xn2", 128, 8 * TB)
    xpb = sbb("xpb", 128, 8 * TB)
    y_sb = sbb("y_sb", 128, 8 * TB)
    h_buf = sbb("h_buf", 128, HT * TB)
    t1_sb = sbb("t1_sb", 128, 2 * TB)
    srow2 = sbb("srow2", 1, 4 * TB)
    rstd2 = sbb("rstd2", 128, TB)
    mrs2 = sbb("mrs2", 128, TB)
    rw_tm = sbb("rw_tm", 128, 4 * NEXP)
    rws = sbb("rws", NEXP, TB)
    rw_b = sbb("rw_b", 128, TB)
    selP = sbb("selP", NEXP, NEXP * 128)
    b2all = sbb("b2all", NEXP, D)
    sb2row = sbb("sb2row", 1, D)
    b1row = sbb("b1row", 1, 2 * HDIM)
    b3row = sbb("b3row", 1, 2 * HDIM)
    rt1 = sbb("rt1", 128, 4 * NEXP)
    rt2 = sbb("rt2", 128, 4 * NEXP)
    rrow = sbb("rrow", 128, 8)
    rt3 = sbb("rt3", 128, 4 * NEXP)
    rt4 = sbb("rt4", 128, 4 * NEXP)
    yt_sb = sbb("yt_sb", 128, TB)

    _pbsetup = int(os.environ.get("PBSETUP", "3"))
    if _pbsetup & 1:
        dq_gl.dma(b2all[:, :], b2_h[:, :])
        dq_gl.dma(sb2row[0:1, :], sb2_h.unsqueeze(0))
        tok_bias = dq_gl.fence()
        dq_gl.wait("tensor", tok_bias)

    if _pbsetup & 2:
        kb.run("gpsimd", "g", lambda e: e.iota(h_buf[0:NEXP, 0:NEXP * 128],
                                               [[1, NEXP], [0, 128]], base=0,
                                               channel_multiplier=0,
                                               allow_small_or_imprecise_dtypes=True))
        kb.run("gpsimd", "g", lambda e: e.iota(h_buf[0:NEXP, 1024:1024 + NEXP * 128],
                                               [[0, NEXP * 128]], base=0,
                                               channel_multiplier=1,
                                               allow_small_or_imprecise_dtypes=True))
        kb.wait("vector", "g")
        kb.run("vector", "v",
               lambda ve: ve.tensor_tensor(selP[:, :], h_buf[0:NEXP, 0:NEXP * 128],
                                           h_buf[0:NEXP, 1024:1024 + NEXP * 128],
                                           ALU.is_equal))
        V_SEL = kb.cnt("v")
        kb.wait("tensor", "v", V_SEL)
        kb.wait("vector", "v", V_SEL)
        kb.wait("scalar", "v", V_SEL)

    w13_use = [0, 0]
    w13_tok = [None, None]
    w13_i = [0]
    w2_use = [0, 0]
    w2_tok = [None, None]
    w2_i = [0]
    bias_use = [0, 0]
    bias_tok = [None, None]

    def load_bias(e):
        par = e % 2
        kb.wait("gpsimd", "t", bias_use[par])
        if e < NEXP:
            dq_gl.dma(b1row[0:1, par * HDIM:(par + 1) * HDIM],
                      b1_h[e].unsqueeze(0))
            dq_gl.dma(b3row[0:1, par * HDIM:(par + 1) * HDIM],
                      b3_h[e].unsqueeze(0))
        else:
            dq_gl.dma(b1row[0:1, par * HDIM:(par + 1) * HDIM],
                      sb1_h.unsqueeze(0))
            dq_gl.dma(b3row[0:1, par * HDIM:(par + 1) * HDIM],
                      sb3_h.unsqueeze(0))
        bias_tok[par] = dq_gl.fence()

    MGRP = [(0, 4), (4, 4), (8, 3)]

    def stream_w13(e, m0, nm, k):
        par = w13_i[0] % 2
        w13_i[0] += 1
        if w13_use[par] > 0:
            kb.wait("sync", "t", w13_use[par])
        c0, c1 = m0 * 128, (m0 + nm) * 128
        if e < NEXP:
            dq_ld.dma(w1g[:, par * 512: par * 512 + (c1 - c0)],
                      w1_h[e, k * 128:(k + 1) * 128, c0:c1])
            dq_ld.dma(w3g[:, par * 512: par * 512 + (c1 - c0)],
                      w3_h[e, k * 128:(k + 1) * 128, c0:c1])
        else:
            dq_ld.dma(w1g[:, par * 512: par * 512 + (c1 - c0)],
                      sw1_h[k * 128:(k + 1) * 128, c0:c1])
            dq_ld.dma(w3g[:, par * 512: par * 512 + (c1 - c0)],
                      sw3_h[k * 128:(k + 1) * 128, c0:c1])
        w13_tok[par] = dq_ld.fence()
        return par

    def stream_w2(e, k):
        par = w2_i[0] % 2
        w2_i[0] += 1
        if w2_use[par] > 0:
            kb.wait("gpsimd", "t", w2_use[par])
        if e < NEXP:
            dq_gl.dma(w2k[:, par * D:(par + 1) * D],
                      w2_h[e, k * 128:(k + 1) * 128, :])
        else:
            dq_gl.dma(w2k[:, par * D:(par + 1) * D],
                      sw2_h[k * 128:(k + 1) * 128, :])
        w2_tok[par] = dq_gl.fence()
        return par

    tok_out_prev = [None]
    _pblim = int(os.environ.get("PBLIM", str(10**9)))
    for tb in range(min(NTB, _pblim)):
        g0 = tb * TB
        if debug and os.environ.get("TRKDUMP"):
            kb.wait("gpsimd", "v")
            for g in range(8):
                dq_gl.dma(dbg["dbg_xp"][g, :, tb * 256:(tb + 1) * 256],
                          xp_scr[g, :, 3840:4096])
            dq_gl.fence()
        if tok_out_prev[0] is not None:
            dq_st.wait("gpsimd", tok_out_prev[0])
        for g in range(8):
            dq_gl.dma(xpb[:, g * TB:(g + 1) * TB], xp_scr[g, :, g0:g0 + TB])
        tok_xp = dq_gl.fence()
        dq_gl.wait("vector", tok_xp)
        dq_gl.wait("scalar", tok_xp)
        dq_gl.wait("tensor", tok_xp)
        if tb == NTB - 1:
            def patch_xpb(ve):
                dst = bass.AP(xpb, CA, [[8 * TB, 128], [TB, 8], [1, CA]])
                srcp = bass.AP(xp_keep, 0, [[8 * CA, 128], [CA, 8], [1, CA]])
                return ve.tensor_copy(dst, srcp)
            kb.run("vector", "v", patch_xpb)
            V_PB_PATCH = kb.cnt("v")
            kb.wait("scalar", "v", V_PB_PATCH)
            kb.wait("tensor", "v", V_PB_PATCH)

        kb.wait("scalar", "t")
        def sq2(se):
            return se.square(h_buf[:, 0:8 * TB], xpb[:])
        kb.run("scalar", "s", sq2)
        S_SQ2 = kb.cnt("s")
        bk6, bk7 = banks[6], banks[7]
        def sums2(te, bk=bk6):
            last = None
            for k in range(8):
                last = te.matmul(bk.ap[0:1, 0:TB], ccol(1.0),
                                 xpb[:, k * TB:(k + 1) * TB],
                                 start=(k == 0), stop=(k == 7))
            return last
        bk6.produce(sums2)
        kb.wait("tensor", "s", S_SQ2)
        def sumsq2(te, bk=bk7):
            last = None
            for k in range(8):
                last = te.matmul(bk.ap[0:1, 0:TB], ccol(1.0),
                                 h_buf[:, k * TB:(k + 1) * TB],
                                 start=(k == 0), stop=(k == 7))
            return last
        bk7.produce(sumsq2)
        def stats2_a(ve, bk=bk6, bq=bk7):
            ve.tensor_single_scalar(srow2[0:1, 0:TB], bk.ap[0:1, 0:TB],
                                    1.0 / D, ALU.mult)
            return ve.tensor_single_scalar(srow2[0:1, TB:2 * TB],
                                           bq.ap[0:1, 0:TB], 1.0 / D, ALU.mult)
        kb.wait("vector", "t", bk7.prod)
        bk6.consume("vector", stats2_a)
        bk7.cons = ("v", kb.cnt("v"))
        kb.wait("vector", "v")
        kb.run("vector", "v",
               lambda ve: ve.tensor_tensor(srow2[0:1, 2 * TB:3 * TB],
                                           srow2[0:1, 0:TB], srow2[0:1, 0:TB],
                                           ALU.mult))
        kb.wait("vector", "v")
        kb.run("vector", "v",
               lambda ve: ve.tensor_tensor(srow2[0:1, TB:2 * TB],
                                           srow2[0:1, TB:2 * TB],
                                           srow2[0:1, 2 * TB:3 * TB],
                                           ALU.subtract))
        kb.wait("scalar", "v")
        kb.run("scalar", "s",
               lambda se: se.activation(srow2[0:1, 2 * TB:3 * TB],
                                        srow2[0:1, TB:2 * TB], AF.Ln,
                                        bias=crow_ap(EPS)))
        kb.wait("scalar", "s")
        kb.run("scalar", "s",
               lambda se: se.activation(srow2[0:1, 2 * TB:3 * TB],
                                        srow2[0:1, 2 * TB:3 * TB], AF.Exp,
                                        scale=crow_ap(-0.5)))
        S_R2 = kb.cnt("s")
        kb.wait("vector", "s", S_R2)
        kb.run("vector", "v",
               lambda ve: ve.tensor_tensor(srow2[0:1, 3 * TB:4 * TB],
                                           srow2[0:1, 0:TB],
                                           srow2[0:1, 2 * TB:3 * TB], ALU.mult))
        V_ST2 = kb.cnt("v")
        def bcr(te, bk=bk6):
            return te.matmul(bk.ap[:, 0:TB], ones_row[0:1, 0:128],
                             srow2[0:1, 2 * TB:3 * TB], start=True, stop=True)
        bk6.produce(bcr, waitv=V_ST2)
        def bcm(te, bk=bk7):
            return te.matmul(bk.ap[:, 0:TB], ones_row[0:1, 0:128],
                             srow2[0:1, 3 * TB:4 * TB], start=True, stop=True)
        bk7.produce(bcm)
        def evb2(ve, bk=bk6, bq=bk7):
            ve.tensor_copy(rstd2[:], bk.ap[:, 0:TB])
            return ve.tensor_copy(mrs2[:], bq.ap[:, 0:TB])
        kb.wait("vector", "t", bk7.prod)
        bk6.consume("vector", evb2)
        bk7.cons = ("v", kb.cnt("v"))
        kb.wait("vector", "v", kb.cnt("v"))
        def apply_ln2(ve):
            rb = bass.AP(rstd2, 0, [[TB, 128], [0, 8], [1, TB]])
            mb = bass.AP(mrs2, 0, [[TB, 128], [0, 8], [1, TB]])
            x8 = bass.AP(xpb, 0, [[8 * TB, 128], [TB, 8], [1, TB]])
            xn8 = bass.AP(xn2, 0, [[8 * TB, 128], [TB, 8], [1, TB]])
            ve.tensor_tensor(xn8, x8, rb, ALU.mult)
            return ve.tensor_tensor(xn8, xn8, mb, ALU.subtract)
        kb.run("vector", "v", apply_ln2)
        V_XN2 = kb.cnt("v")

        def bdump(idx, ap, width, tb=tb):
            import os as _os3
            if not (debug and tb == int(_os3.environ.get("BDTB", "0"))):
                return
            kb.wait(_steng, "v")
            kb.wait(_steng, "s")
            dq_st.dma(dbg["dbg_pl"][idx, :, 0:width], ap)
            tok = dq_st.fence()
            dq_st.wait("vector", tok)
            dq_st.wait("scalar", tok)
        bdump(0, xpb[:, 0:2048], 2048)
        bdump(1, xn2[:, 0:2048], 2048)

        for tt in range(4):
            bk = banks[tt]
            def logits(te, tt=tt, bk=bk):
                last = None
                for k in range(8):
                    last = te.matmul(
                        bk.ap[:, 0:NEXP],
                        xn2[:, k * TB + tt * 128: k * TB + (tt + 1) * 128],
                        gwT[:, k * NEXP:(k + 1) * NEXP],
                        start=(k == 0), stop=(k == 7))
                return last
            bk.produce(logits, waitv=V_XN2)
        def router_m1(ve):
            last = None
            for tt in range(4):
                last = ve.reduce_max(rrow[:, tt:tt + 1],
                                     banks[tt].ap[:, 0:NEXP], axis=AXX)
            return last
        kb.wait("vector", "t")
        kb.run("vector", "v", router_m1)
        kb.wait("vector", "v")
        def router_v1(ve):
            last = None
            for tt in range(4):
                last = ve.tensor_scalar(rt1[:, tt * 8:(tt + 1) * 8],
                                        banks[tt].ap[:, 0:NEXP],
                                        rrow[:, tt:tt + 1], None, ALU.subtract)
            return last
        kb.run("vector", "v", router_v1)
        for tt in range(4):
            banks[tt].cons = ("v", kb.cnt("v"))
        bdump(7, rt1[:, 0:32], 32)
        bdump(8, rrow[:, 0:8], 8)
        kb.wait("scalar", "v")
        kb.run("scalar", "s",
               lambda se: se.activation(rt1[:], rt1[:], AF.Exp))
        S_REXP = kb.cnt("s")
        kb.wait("vector", "s", S_REXP)
        def router_v2a(ve):
            last = None
            for tt in range(4):
                last = ve.reduce_sum(rrow[:, 4 + tt:5 + tt],
                                     rt1[:, tt * 8:(tt + 1) * 8], axis=AXX)
            return last
        kb.run("vector", "v", router_v2a)
        kb.wait("vector", "v")
        kb.run("vector", "v",
               lambda ve: ve.tensor_copy(rt2[:, 0:4], rrow[:, 4:8]))
        V_RS = kb.cnt("v")
        kb.wait("scalar", "v", V_RS)
        kb.run("scalar", "s",
               lambda se: se.activation(rt2[:, 0:4], rt2[:, 0:4], AF.Ln))
        kb.wait("scalar", "s")
        kb.run("scalar", "s",
               lambda se: se.activation(rt2[:, 0:4], rt2[:, 0:4], AF.Exp,
                                        scale=ccol(-1.0)))
        S_RINV = kb.cnt("s")
        kb.wait("vector", "s", S_RINV)
        bdump(9, rt2[:, 0:8], 8)
        bdump(10, rt1[:, 0:32], 32)
        def phase(fn):
            kb.run("vector", "v", fn)
            kb.wait("vector", "v")

        def r_score(ve):
            last = None
            for tt in range(4):
                sl = slice(tt * 8, (tt + 1) * 8)
                last = ve.tensor_scalar(rt1[:, sl], rt1[:, sl],
                                        rt2[:, tt:tt + 1], None, ALU.mult)
            for tt in range(4):
                sl = slice(tt * 8, (tt + 1) * 8)
                last = ve.tensor_tensor(rw_tm[:, sl], rt1[:, sl], gbb[:],
                                        ALU.add)
            for tt in range(4):
                sl = slice(tt * 8, (tt + 1) * 8)
                last = ve.reduce_max(rrow[:, tt:tt + 1], rw_tm[:, sl], axis=AXX)
            return last
        phase(r_score)
        bdump(11, rt1[:, 0:32], 32)
        def r_eq(ve):
            last = None
            for tt in range(4):
                sl = slice(tt * 8, (tt + 1) * 8)
                last = ve.tensor_scalar(rt3[:, sl], rw_tm[:, sl],
                                        rrow[:, tt:tt + 1], None, ALU.is_equal)
            for tt in range(4):
                sl = slice(tt * 8, (tt + 1) * 8)
                last = ve.scalar_tensor_tensor(rt4[:, sl], rt3[:, sl], -1e30,
                                               rw_tm[:, sl], ALU.mult, ALU.add)
            for tt in range(4):
                sl = slice(tt * 8, (tt + 1) * 8)
                last = ve.reduce_max(rrow[:, 4 + tt:5 + tt], rt4[:, sl],
                                     axis=AXX)
            return last
        phase(r_eq)
        def r_mask(ve):
            last = None
            for tt in range(4):
                sl = slice(tt * 8, (tt + 1) * 8)
                last = ve.tensor_scalar(rt3[:, sl], rw_tm[:, sl],
                                        rrow[:, 4 + tt:5 + tt], None, ALU.is_ge)
            for tt in range(4):
                sl = slice(tt * 8, (tt + 1) * 8)
                last = ve.tensor_tensor(rw_tm[:, sl], rt1[:, sl],
                                        rt3[:, sl], ALU.mult)
            return last
        kb.run("vector", "v", r_mask)
        V_RW = kb.cnt("v")
        bdump(2, rw_tm[:, 0:32], 32)
        bdump(3, rt1[:, 0:32], 32)

        bk4 = banks[4]
        def rwt(te, bk=bk4):
            last = None
            for tt in range(4):
                last = te.transpose(bk.ap[0:NEXP, tt * 128:(tt + 1) * 128],
                                    rw_tm[:, tt * 8:(tt + 1) * 8],
                                    ident[:])
            return last
        bk4.produce(rwt, waitv=V_RW)
        bk4.consume("vector",
                    lambda ve, bk=bk4: ve.tensor_copy(rws[:, :],
                                                      bk.ap[0:NEXP, 0:TB]))
        V_RWS = kb.cnt("v")

        for j in range(8):
            bk = banks[4 + j % 2]
            def yinit(te, j=j, bk=bk):
                te.matmul(bk.ap[:, 0:TB], b2all[:, j * 128:(j + 1) * 128],
                          rws[:, :], start=True, stop=False)
                return te.matmul(bk.ap[:, 0:TB],
                                 sb2row[0:1, j * 128:(j + 1) * 128],
                                 ones_row[0:1, 0:TB], start=False, stop=True)
            bk.produce(yinit, waitv=V_RWS)
            def evy0(ve, j=j, bk=bk):
                return ve.tensor_copy(y_sb[:, j * TB:(j + 1) * TB],
                                      bk.ap[:, 0:TB])
            bk.consume("vector", evy0)

        for e in range(NEXP + 1):
            shared = (e == NEXP)
            if not shared:
                bk5 = banks[5]
                def rwb(te, e=e, bk=bk5):
                    return te.matmul(bk.ap[:, 0:TB],
                                     selP[:, e * 128:(e + 1) * 128],
                                     rws[:, :], start=True, stop=True)
                bk5.produce(rwb)
                bk5.consume("vector",
                            lambda ve, bk=bk5: ve.tensor_copy(rw_b[:],
                                                              bk.ap[:, 0:TB]))
            V_RWB = kb.cnt("v")

            if e == 0:
                load_bias(0)
            _bpar = e % 2
            dq_gl.wait("tensor", bias_tok[_bpar])

            for m0, nm in MGRP:
                for mi in range(nm):
                    for gb in (banks[mi], banks[4 + mi]):
                        nmn, c = gb.cons
                        kb.wait("tensor", nmn, c)
                if m0 == 0 and e == 0:
                    kb.wait("tensor", "v", V_XN2)
                for k in range(8):
                    par = stream_w13(e, m0, nm, k)
                    dq_ld.wait("tensor", w13_tok[par])
                    def mmh(te, k=k, nm=nm, par=par):
                        last = None
                        for mi in range(nm):
                            last = te.matmul(
                                banks[mi].ap[:, 0:TB],
                                w1g[:, par * 512 + mi * 128:
                                    par * 512 + (mi + 1) * 128],
                                xn2[:, k * TB:(k + 1) * TB],
                                start=(k == 0), stop=False)
                        for mi in range(nm):
                            last = te.matmul(
                                banks[4 + mi].ap[:, 0:TB],
                                w3g[:, par * 512 + mi * 128:
                                    par * 512 + (mi + 1) * 128],
                                xn2[:, k * TB:(k + 1) * TB],
                                start=(k == 0), stop=False)
                        return last
                    kb.run("tensor", "t", mmh)
                    w13_use[par] = kb.cnt("t")
                def mmbias(te, m0=m0, nm=nm, _bpar=_bpar):
                    last = None
                    for mi in range(nm):
                        m = m0 + mi
                        last = te.matmul(banks[mi].ap[:, 0:TB],
                                         b1row[0:1, _bpar * HDIM + m * 128:
                                               _bpar * HDIM + (m + 1) * 128],
                                         ones_row[0:1, 0:TB],
                                         start=False, stop=True)
                        last = te.matmul(banks[4 + mi].ap[:, 0:TB],
                                         b3row[0:1, _bpar * HDIM + m * 128:
                                               _bpar * HDIM + (m + 1) * 128],
                                         ones_row[0:1, 0:TB],
                                         start=False, stop=True)
                    return last
                kb.run("tensor", "t", mmbias)
                bias_use[_bpar] = kb.cnt("t")
                for mi in range(nm):
                    banks[mi].prod = kb.cnt("t")
                    banks[4 + mi].prod = kb.cnt("t")
                for mi in range(nm):
                    m = m0 + mi
                    par2 = mi % 2
                    def evh1(se, mi=mi, par2=par2):
                        return se.copy(t1_sb[:, par2 * TB:(par2 + 1) * TB],
                                       banks[mi].ap[:, 0:TB])
                    banks[mi].consume("scalar", evh1)
                    S_T1 = kb.cnt("s")
                    kb.wait("vector", "s", S_T1)
                    def evh2(ve, mi=mi, par2=par2):
                        return ve.tensor_tensor(
                            t1_sb[:, par2 * TB:(par2 + 1) * TB],
                            t1_sb[:, par2 * TB:(par2 + 1) * TB],
                            banks[4 + mi].ap[:, 0:TB], ALU.mult)
                    banks[4 + mi].consume("vector", evh2)
                    V_T1 = kb.cnt("v")
                    kb.wait("scalar", "v", V_T1)
                    def evh3(se, m=m, par2=par2):
                        return se.activation(h_buf[:, m * TB:(m + 1) * TB],
                                             t1_sb[:, par2 * TB:(par2 + 1) * TB],
                                             AF.Silu)
                    kb.run("scalar", "s", evh3)
            S_H = kb.cnt("s")

            if e < NEXP:
                load_bias(e + 1)

            for gb in range(8):
                nmn, c = banks[gb].cons
                kb.wait("tensor", nmn, c)
            kb.wait("tensor", "s", S_H)
            for k in range(HT):
                par = stream_w2(e, k)
                dq_ld.wait("tensor", w2_tok[par])
                def mmw2(te, k=k, par=par):
                    last = None
                    for j in range(8):
                        last = te.matmul(
                            banks[j].ap[:, 0:TB],
                            w2k[:, par * D + j * 128: par * D + (j + 1) * 128],
                            h_buf[:, k * TB:(k + 1) * TB],
                            start=(k == 0), stop=(k == HT - 1))
                    return last
                kb.run("tensor", "t", mmw2)
                w2_use[par] = kb.cnt("t")
            for j in range(8):
                banks[j].prod = kb.cnt("t")
            kb.wait("vector", "t")
            if shared:
                def evy(ve):
                    last = None
                    for j in range(8):
                        last = ve.tensor_tensor(
                            y_sb[:, j * TB:(j + 1) * TB],
                            y_sb[:, j * TB:(j + 1) * TB],
                            banks[j].ap[:, 0:TB], ALU.add)
                    return last
                kb.run("vector", "v", evy)
            else:
                def evy(ve):
                    last = None
                    for j in range(8):
                        ve.tensor_tensor(yt_sb[:], banks[j].ap[:, 0:TB],
                                         rw_b[:], ALU.mult)
                        last = ve.tensor_tensor(
                            y_sb[:, j * TB:(j + 1) * TB],
                            y_sb[:, j * TB:(j + 1) * TB],
                            yt_sb[:], ALU.add)
                    return last
                kb.run("vector", "v", evy)
            for j in range(8):
                banks[j].cons = ("v", kb.cnt("v"))
            if e == 0:
                bdump(4, h_buf[:, 0:2048], 2048)
                bdump(5, y_sb[:, 0:2048], 2048)
            if e == NEXP:
                bdump(6, y_sb[:, 0:2048], 2048)
        V_Y = kb.cnt("v")

        kb.wait("gpsimd", "t")
        kb.wait("gpsimd", "s")
        for g in range(8):
            dq_gl.dma(h_buf[:, g * TB:(g + 1) * TB], xp_scr[g, :, g0:g0 + TB])
        tok_xp2 = dq_gl.fence()
        dq_gl.wait("vector", tok_xp2)
        dq_gl.wait("gpsimd", tok_xp2)
        bdump(17, h_buf[:, 0:2048], 2048)
        if tb == NTB - 1:
            def patch_hbuf(ve):
                dst = bass.AP(h_buf, CA, [[HT * TB, 128], [TB, 8], [1, CA]])
                srcp = bass.AP(xp_keep, 0, [[8 * CA, 128], [CA, 8], [1, CA]])
                return ve.tensor_copy(dst, srcp)
            kb.run("vector", "v", patch_hbuf)
        def addres(ve):
            return ve.tensor_tensor(h_buf[:, 0:8 * TB], h_buf[:, 0:8 * TB],
                                    y_sb[:, 0:8 * TB], ALU.add)
        kb.run("vector", "v", addres)
        V_OUT = kb.cnt("v")
        bdump(18, h_buf[:, 0:2048], 2048)
        for tt in range(4):
            for half in range(2):
                bk = banks[6 + half]
                def tfo(te, tt=tt, half=half, bk=bk):
                    last = None
                    for q in range(4):
                        g = half * 4 + q
                        last = te.transpose(
                            bk.ap[:, q * 128:(q + 1) * 128],
                            h_buf[:, g * TB + tt * 128: g * TB + (tt + 1) * 128],
                            ident[:])
                    return last
                bk.produce(tfo, waitv=V_OUT)
                def evt(ve, tt=tt, half=half, bk=bk):
                    return ve.tensor_copy(
                        xpb[:, tt * 1024 + half * 512: tt * 1024 + (half + 1) * 512],
                        bk.ap[:, 0:512])
                bk.consume("vector", evt)
        V_TM = kb.cnt("v")
        kb.wait(_steng, "v", V_TM)
        bdump(19, xpb[:, 2048:4096], 2048)
        for tt in range(4):
            dq_st.dma(out_flat[g0 + tt * 128: g0 + (tt + 1) * 128, :],
                      xpb[:, tt * 1024:(tt + 1) * 1024])
        tok_out = dq_st.fence()
        dq_st.wait("vector", tok_out)
        tok_out_prev[0] = tok_out

    import os as _os4
    if debug and _os4.environ.get("XPDUMP"):
        for tok in dq_st.final_tokens():
            dq_st.wait("gpsimd", tok)
        for g in range(8):
            dq_st.dma(dbg["dbg_xp"][g, :, :], xp_scr[g, :, :])
        dq_st.fence()

    if debug and os.environ.get("OCOPY"):
        for tok in dq_st.final_tokens():
            dq_st.wait("gpsimd", tok)
        for tt in range(4):
            dq_st.dma(dbg["dbg_xp"][4 + tt, :, 0:1024],
                      out_flat[3584 + tt * 128: 3584 + (tt + 1) * 128, :])
        dq_st.fence()

    for dq in (dq_st, dq_ld, dq_gl):
        for tok in dq.final_tokens():
            dq.wait("gpsimd", tok)
    _edelay = int(os.environ.get("ENDDELAY", "0"))
    if _edelay:
        kb.wait("vector", "g")
        for _ in range(_edelay):
            kb.run("vector", "v", lambda ve: ve.tensor_copy(gsc[:, 0:1],
                                                            consts[:, 0:1]))
        kb.wait("gpsimd", "v", kb.cnt("v"))
    _finish(nc, kb)
    es_pb.close()
    es.close()
    return nc, dbg


def _finish(nc, kb):
    with nc.Block() as block:
        @block.sync
        def _(eng):
            for e, fn in kb.plan:
                if e == "sync":
                    fn(eng)

        @block.vector
        def _(eng):
            for e, fn in kb.plan:
                if e == "vector":
                    fn(eng)

        @block.scalar
        def _(eng):
            for e, fn in kb.plan:
                if e == "scalar":
                    fn(eng)

        @block.tensor
        def _(eng):
            for e, fn in kb.plan:
                if e == "tensor":
                    fn(eng)

        @block.gpsimd
        def _(eng):
            for e, fn in kb.plan:
                if e == "gpsimd":
                    fn(eng)


_NAME_MAP = [
    ("x", "L0_attn_x"), ("w_a", "L0_attn_w_a"), ("w_x", "L0_attn_w_x"),
    ("w_out", "L0_attn_w_out"), ("v_re", "L0_attn_mat_v_real"),
    ("v_im", "L0_attn_mat_v_imag"), ("h0_re", "L0_attn_last_hidden_real"),
    ("h0_im", "L0_attn_last_hidden_imag"), ("gate_w", "L0_ffn_gate_w"),
    ("gate_b", "L0_ffn_gate_b"), ("w1", "L0_ffn_w1"), ("b1", "L0_ffn_b1"),
    ("w3", "L0_ffn_w3"), ("b3", "L0_ffn_b3"), ("w2", "L0_ffn_w2"),
    ("b2", "L0_ffn_b2"), ("sw1", "L0_ffn_sw1"), ("sb1", "L0_ffn_sb1"),
    ("sw3", "L0_ffn_sw3"), ("sb3", "L0_ffn_sb3"), ("sw2", "L0_ffn_sw2"),
    ("sb2", "L0_ffn_sb2"),
]


def kernel(**inputs):
    nc, _ = build_program(stage="full", debug=False)
    reps = {k: np.ascontiguousarray(np.asarray(inputs[v], dtype=np.float32))
            for k, v in _NAME_MAP if k != "x"}
    xfull = np.asarray(inputs["L0_attn_x"], dtype=np.float32)
    shards = np.split(np.ascontiguousarray(xfull), NCORES, axis=0)
    in_maps = [dict(reps, x=np.ascontiguousarray(s)) for s in shards]
    res = run_bass_kernel_spmd(nc, in_maps, core_ids=list(range(NCORES)))
    return np.concatenate([r["out"] for r in res.results], axis=0)


if __name__ == "__main__":
    nc, dbg = build_program(stage="newton", debug=True)
    n = sum(1 for _ in nc.all_instructions())
    print("built newton stage, instructions:", n)
